# version 05
# baseline (speedup 1.0000x reference)
import numpy as np
import ml_dtypes
from contextlib import ExitStack

import concourse.bass as bass
import concourse.mybir as mybir
from concourse.bass_utils import run_bass_kernel_spmd

F32 = mybir.dt.float32
BF16 = mybir.dt.bfloat16
AF = mybir.ActivationFunctionType
ALU = mybir.AluOpType
AX = mybir.AxisListType.X

P = 128
C = 1024
KCH = 8
H = 4096
DK = 128
NCORES = 8
EPS = 1e-5
INV_SQRT_DK = float(1.0 / np.sqrt(128.0))

ENGS = ("sync", "gpsimd", "scalar", "vector", "tensor")


class Sem:
    __slots__ = ("name", "owner", "c", "h")

    def __init__(self, name, owner=None):
        self.name = name
        self.owner = owner
        self.c = 0
        self.h = None


class Res:
    __slots__ = ("name", "w", "rd")

    def __init__(self, name):
        self.name = name
        self.w = None
        self.rd = {}


class Em:
    def __init__(self):
        self.ops = {e: [] for e in ENGS}
        self.hw = {e: {} for e in ENGS}
        self.sems = []

    def sem(self, name, owner=None):
        s = Sem(name, owner)
        self.sems.append(s)
        return s

    def wait(self, eng, sem, val=None):
        val = sem.c if val is None else val
        if val <= 0:
            return
        if sem.owner == eng:
            return
        if self.hw[eng].get(sem.name, -1) >= val:
            return
        self.hw[eng][sem.name] = val
        self.ops[eng].append(("wait", sem, val))

    def do(self, eng, fn, sem=None, inc=1):
        if sem is not None:
            sem.c += inc
            self.ops[eng].append(("do", fn, sem, inc))
            return sem.c
        self.ops[eng].append(("do", fn, None, 0))
        return None

    def rguard(self, eng, *ress):
        for r in ress:
            if r.w:
                self.wait(eng, r.w[0], r.w[1])

    def wguard(self, eng, *ress):
        for r in ress:
            if r.w:
                self.wait(eng, r.w[0], r.w[1])
            for (s, v) in list(r.rd.values()):
                self.wait(eng, s, v)

    def simulate(self):
        pos = {e: 0 for e in ENGS}
        cnt = {s.name: 0 for s in self.sems}
        progressed = True
        while progressed:
            progressed = False
            for e in ENGS:
                while pos[e] < len(self.ops[e]):
                    op = self.ops[e][pos[e]]
                    if op[0] == "wait":
                        if cnt[op[1].name] < op[2]:
                            break
                    else:
                        _, fn, sem, inc = op
                        if sem is not None:
                            cnt[sem.name] += inc
                    pos[e] += 1
                    progressed = True
        stuck = {e: (pos[e], len(self.ops[e])) for e in ENGS if pos[e] < len(self.ops[e])}
        if stuck:
            lines = []
            for e, (p_, n) in stuck.items():
                op = self.ops[e][p_]
                lines.append(f"  {e}: op {p_}/{n} waiting {op[1].name} >= {op[2]} (now {cnt[op[1].name]})")
            raise RuntimeError("DEADLOCK in simulated schedule:\n" + "\n".join(lines))

    def counts(self):
        return {e: len(self.ops[e]) for e in ENGS}


def rmark(res, sem):
    res.rd[id(sem)] = (sem, sem.c)


def wmark(res, sem):
    res.w = (sem, sem.c)
    res.rd = {}


class _Stop(Exception):
    pass


def build_program(NT=16, stop_after=None, debug_dump=False):
    TCORE = NT * P
    NTG = NT // 4
    GK = 8 * TCORE
    NKC = GK // 512
    HHALF = H // 2
    HQ = H // 4
    NHMQ = HQ // P

    nc = bass.Bass("TRN2", target_bir_lowering=False, debug=False)
    em = Em()

    def ck(name):
        if stop_after is not None and name == stop_after:
            raise _Stop()

    def din(name, shape, dt=F32):
        return nc.dram_tensor(name, shape, dt, kind="ExternalInput").ap()

    x_in = din("x", [TCORE, C])
    wq_in = din("wq", [C, DK])
    wk_in = din("wk", [C, DK])
    wva_in = din("wva", [C, DK])
    woa_in = din("woa", [DK, C])
    wr_in = din("wr", [C, C])
    wvf_in = din("wvf", [C, C])
    wof_in = din("wof", [C, C])
    ws_in = din("ws", [C, C])
    w1_in = din("w1", [C, C])
    w2_in = din("w2", [C, C])
    w3_in = din("w3", [C, C])
    krw_in = din("krw", [2, C, H])
    vrw_in = din("vrw", [2, H, C])
    wsmT_in = din("wsmT", [7, C])
    lnc_in = din("lnc", [P, 32])
    sh3_in = din("sh3", [P, 3])
    idn_in = din("idn", [P, P], BF16)
    gmask_in = din("gmask", [P, 4 * 512], BF16)
    onesp_in = din("onesp", [P, 1], BF16)
    onesr_in = din("onesr", [1, P], BF16)
    kmask_in = din("kmask", [P, GK], BF16)
    vmask_in = din("vmask", [P, GK // P])
    nzero_in = din("nzero", [P, 1])

    y_out = nc.dram_tensor("y", [TCORE, C], F32, kind="ExternalOutput").ap()

    if debug_dump or stop_after is not None:
        dxs_out = nc.dram_tensor("dxs", [TCORE, C], F32, kind="ExternalOutput").ap()
        db_out = [nc.dram_tensor("db%d" % i, [P, 16 * C], F32, kind="ExternalOutput").ap()
                  for i in range(4)]
        dsm_out = nc.dram_tensor("dsm", [P, 176], F32, kind="ExternalOutput").ap()
    send_d = nc.dram_tensor("kvsend", [P, 2 * TCORE], BF16, kind="Internal")
    recv_d = nc.dram_tensor("kvrecv", [NCORES * P, 2 * TCORE], BF16, kind="Internal")

    with ExitStack() as st:
        def sb(name, shape, dt):
            return st.enter_context(nc.sbuf_tensor(name, shape, dt))

        XS = sb("XS", [P, NT * C], F32)
        BIG1 = sb("BIG1", [P, 16 * C], BF16)
        BIG2 = sb("BIG2", [P, 16 * C], BF16)
        BIG3 = sb("BIG3", [P, 16 * C], BF16)
        BIG4 = sb("BIG4", [P, 16 * C], BF16)
        TMP2 = sb("TMP2", [P, 2 * C], BF16)
        SB512 = sb("SB512", [P, 2 * 512], BF16)
        SM = sb("SM", [P, 176], F32)
        SMB = sb("SMB", [P, 64], BF16)
        LNC = sb("LNC", [P, 32], F32)
        SH3 = sb("SH3", [P, 3], F32)
        IDN = sb("IDN", [P, P], BF16)
        GMK = sb("GMK", [P, 4 * 512], BF16)
        ONESP = sb("ONESP", [P, 1], BF16)
        ONESR = sb("ONESR", [1, P], BF16)
        VMK = sb("VMK", [P, GK // P], F32)
        NZERO = sb("NZERO", [P, 1], F32)
        LSF = sb("LSF", [1, 512], F32)

        pb = [st.enter_context(nc.psum_tensor("pb%d" % i, [P, 512], F32)) for i in range(8)]

        TS = em.sem("TS", "tensor")
        SS = em.sem("SS", "scalar")
        VS = em.sem("VS", "vector")
        GS = em.sem("GS", "gpsimd")
        DG = em.sem("DG", None)
        DSY = em.sem("DSY", None)
        CCS = em.sem("CCS", None)

        R_xs = [Res("xs%d" % t) for t in range(NT)]
        R_pb = [Res("pb%d" % i) for i in range(8)]
        R_tmp2 = [Res("tmp2_%d" % i) for i in range(2)]
        R_sb512 = [Res("sb512_%d" % i) for i in range(2)]
        R_smb = Res("smb")
        R_wt7 = Res("wt7")
        R_wtok = Res("wtok")
        R_sstat = Res("sstat")
        R_norm = Res("norm")
        R_send = Res("send")
        R_recv = Res("recv")
        R_lnst = [Res("lnst%d" % i) for i in range(2)]
        R_bid = [Res("bid%d" % i) for i in range(2)]
        R_lt = [Res("lt%d" % i) for i in range(2)]

        region_owner = {"B1": [], "B2": [], "B3": [], "B4a": [], "B4b": []}

        def claim(eng, region, new_res):
            for r in region_owner[region]:
                em.wguard(eng, r)
            region_owner[region] = [new_res]

        def add_owner(region, res):
            if res not in region_owner[region]:
                region_owner[region].append(res)

        SST0 = 0
        WTK0 = 32
        TMPC = 80
        NRM0 = 96
        ONEC = 112
        BID0 = 113
        LT0 = 162

        def T(fn, inc=False):
            return em.do("tensor", fn, TS if inc else None)

        def S(fn):
            return em.do("scalar", fn, SS, 1)

        def V(fn):
            return em.do("vector", fn, VS, 1)

        def Vd(fn):
            r = em.do("vector", fn, VS, 1)
            em.do("vector", lambda e: e.drain())
            return r

        def Sd(fn):
            r = em.do("scalar", fn, SS, 1)
            em.do("scalar", lambda e: e.drain())
            return r

        def G(fn):
            return em.do("gpsimd", fn, GS, 1)

        def GD(fn):
            return em.do("gpsimd", fn, DG, 16)

        def SD(fn):
            return em.do("sync", fn, DSY, 16)

        def xs_ap(t, half=None):
            if half is None:
                return XS[:, t * C:(t + 1) * C]
            return XS[:, t * C + half * 512: t * C + (half + 1) * 512]

        def fm(big, k, lo, n):
            return big[:, k * TCORE + lo: k * TCORE + lo + n]

        SD(lambda e: e.dma_start(out=XS[:, :].rearrange("p (t c) -> p t c", c=C),
                                 in_=x_in.rearrange("(t p) c -> p t c", p=P)))
        for t in range(NT):
            wmark(R_xs[t], DSY)
        SD(lambda e: e.dma_start(out=LNC[:, :], in_=lnc_in))
        SD(lambda e: e.dma_start(out=SH3[:, :], in_=sh3_in))
        SD(lambda e: e.dma_start(out=IDN[:, :], in_=idn_in))
        SD(lambda e: e.dma_start(out=GMK[:, :], in_=gmask_in))
        SD(lambda e: e.dma_start(out=ONESP[:, :], in_=onesp_in))
        SD(lambda e: e.dma_start(out=ONESR[:, :], in_=onesr_in))
        SD(lambda e: e.dma_start(out=VMK[:, :], in_=vmask_in))
        SD(lambda e: e.dma_start(out=NZERO[:, :], in_=nzero_in))
        smalls_dma = DSY.c

        GD(lambda e: e.dma_start(out=TMP2[0:7, 0:C], in_=wsmT_in))
        wmark(R_wt7, DG)
        wmark(R_tmp2[0], DG)

        em.wait("vector", DSY, smalls_dma)
        V(lambda e: e.memset(SM[:, ONEC:ONEC + 1], 1.0))

        em.rguard("tensor", R_wt7)
        em.wait("tensor", DSY, smalls_dma)
        for c_ in range(KCH):
            T(lambda e, c_=c_: e.matmul(pb[6][:, c_ * 8:c_ * 8 + 7],
                                        TMP2[0:7, c_ * P:(c_ + 1) * P],
                                        IDN[0:7, 0:7], start=True, stop=True),
              inc=(c_ == KCH - 1))
        rmark(R_tmp2[0], TS)
        wmark(R_pb[6], TS)
        em.rguard("scalar", R_pb[6])
        for c_ in range(KCH):
            S(lambda e, c_=c_: e.copy(out=SMB[:, c_ * 8:c_ * 8 + 7],
                                      in_=pb[6][:, c_ * 8:c_ * 8 + 7]))
        rmark(R_pb[6], SS)
        wmark(R_smb, SS)

        R_ws = [Res("wsA"), Res("wsB")]

        def stage_w(src_ap, slot):
            off = slot * 8 * C
            r = R_ws[slot]
            claim("gpsimd", "B4a" if slot == 0 else "B4b", r)
            GD(lambda e, off=off, src_ap=src_ap: e.dma_start(
                out=BIG4[:, off:off + 8 * C].rearrange("p (k n) -> p k n", n=C),
                in_=src_ap.rearrange("(k p) n -> p k n", p=P)))
            wmark(r, DG)
            return (lambda k, off=off: BIG4[:, off + k * C: off + (k + 1) * C]), r

        def ln_transpose_tile(t, src_ap, dst_big, dst_res, lnw_off, double_ln,
                              save_stats, plain=False):
            sl = t % 2
            ca = TMPC + sl * 8
            em.wguard("vector", R_lnst[sl])
            V(lambda e, src_ap=src_ap, ca=ca: e.reduce_sum(SM[:, ca:ca + 1], src_ap, axis=AX))
            for hb in range(2):
                em.wguard("scalar", R_pb[6 + hb])
                S(lambda e, src_ap=src_ap, hb=hb: e.square(
                    out=pb[6 + hb][:, :], in_=src_ap[:, hb * 512:(hb + 1) * 512]))
                wmark(R_pb[6 + hb], SS)
            em.rguard("vector", R_pb[6])
            em.rguard("vector", R_pb[7])
            Vd(lambda e, ca=ca: e.reduce_sum(SM[:, ca + 1:ca + 2], pb[6][:, :], axis=AX))
            Vd(lambda e, ca=ca: e.reduce_sum(SM[:, ca + 2:ca + 3], pb[7][:, :], axis=AX))
            rmark(R_pb[6], VS)
            rmark(R_pb[7], VS)
            Vd(lambda e, ca=ca: e.tensor_tensor(SM[:, ca + 1:ca + 2], SM[:, ca + 1:ca + 2],
                                               SM[:, ca + 2:ca + 3], ALU.add))
            Vd(lambda e, ca=ca: e.tensor_scalar_mul(SM[:, ca:ca + 1], SM[:, ca:ca + 1], 1.0 / C))
            Vd(lambda e, ca=ca: e.tensor_scalar_mul(SM[:, ca + 1:ca + 2], SM[:, ca + 1:ca + 2], 1.0 / C))
            Vd(lambda e, ca=ca: e.tensor_tensor(SM[:, ca + 2:ca + 3], SM[:, ca:ca + 1],
                                               SM[:, ca:ca + 1], ALU.mult))
            Vd(lambda e, ca=ca: e.tensor_tensor(SM[:, ca + 1:ca + 2], SM[:, ca + 1:ca + 2],
                                               SM[:, ca + 2:ca + 3], ALU.subtract))
            Vd(lambda e, ca=ca: e.tensor_scalar_add(SM[:, ca + 2:ca + 3], SM[:, ca + 1:ca + 2], EPS))
            Vd(lambda e, ca=ca: e.reciprocal(SM[:, ca + 3:ca + 4], SM[:, ca + 2:ca + 3]))
            em.wait("scalar", VS)
            Sd(lambda e, ca=ca: e.sqrt(out=SM[:, ca + 4:ca + 5], in_=SM[:, ca + 3:ca + 4]))
            em.wait("vector", SS)
            if double_ln:
                Vd(lambda e, ca=ca: e.tensor_tensor(SM[:, ca + 5:ca + 6], SM[:, ca + 1:ca + 2],
                                                   SM[:, ca + 3:ca + 4], ALU.mult))
                Vd(lambda e, ca=ca: e.tensor_scalar_add(SM[:, ca + 5:ca + 6], SM[:, ca + 5:ca + 6], EPS))
                Vd(lambda e, ca=ca: e.reciprocal(SM[:, ca + 6:ca + 7], SM[:, ca + 5:ca + 6]))
                em.wait("scalar", VS)
                Sd(lambda e, ca=ca: e.sqrt(out=SM[:, ca + 5:ca + 6], in_=SM[:, ca + 6:ca + 7]))
                em.wait("vector", SS)
                Vd(lambda e, ca=ca: e.tensor_tensor(SM[:, ca + 6:ca + 7], SM[:, ca + 4:ca + 5],
                                                   SM[:, ca + 5:ca + 6], ALU.mult))
                comb = ca + 6
            else:
                comb = ca + 4
            Vd(lambda e, ca=ca, comb=comb: e.scalar_tensor_tensor(
                out=SM[:, ca + 7:ca + 8], in0=SM[:, ca:ca + 1], scalar=-1.0,
                in1=SM[:, comb:comb + 1], op0=ALU.mult, op1=ALU.mult))
            if save_stats:
                em.wguard("vector", R_sstat) if t == 0 else None
                V(lambda e, ca=ca, t=t: e.tensor_copy(
                    out=SM[:, SST0 + 2 * t:SST0 + 2 * t + 1], in_=SM[:, ca + 4:ca + 5]))
                V(lambda e, ca=ca, t=t: e.scalar_tensor_tensor(
                    out=SM[:, SST0 + 2 * t + 1:SST0 + 2 * t + 2],
                    in0=SM[:, ca:ca + 1], scalar=-1.0, in1=SM[:, ca + 4:ca + 5],
                    op0=ALU.mult, op1=ALU.mult))
                wmark(R_sstat, VS)
            wmark(R_lnst[sl], VS)
            vstat = VS.c
            em.wguard("scalar", R_tmp2[sl])
            em.wait("scalar", VS, vstat)
            S(lambda e, src_ap=src_ap, sl=sl, ca=ca, comb=comb: e.activation(
                out=TMP2[:, sl * C:(sl + 1) * C], in_=src_ap, func=AF.Identity,
                bias=SM[:, ca + 7:ca + 8], scale=SM[:, comb:comb + 1]))
            wmark(R_tmp2[sl], SS)
            rmark(R_lnst[sl], SS)
            for hb in range(2):
                bank = 6 + hb
                em.wguard("tensor", R_pb[bank])
                em.rguard("tensor", R_tmp2[sl])
                em.wait("tensor", DSY, smalls_dma)
                for j in range(4):
                    cc = hb * 4 + j
                    T(lambda e, cc=cc, j=j, bank=bank, sl=sl: e.matmul(
                        pb[bank][:, j * P:(j + 1) * P],
                        TMP2[:, sl * C + cc * P: sl * C + (cc + 1) * P],
                        IDN[:, :], start=True, stop=True), inc=(j == 3))
                wmark(R_pb[bank], TS)
                rmark(R_tmp2[sl], TS)
                em.rguard("scalar", R_pb[bank])
                for j in range(4):
                    cc = hb * 4 + j
                    if plain:
                        S(lambda e, cc=cc, j=j, bank=bank: e.copy(
                            out=fm(dst_big, cc, t * P, P), in_=pb[bank][:, j * P:(j + 1) * P]))
                    else:
                        S(lambda e, cc=cc, j=j, bank=bank: e.activation(
                            out=fm(dst_big, cc, t * P, P), in_=pb[bank][:, j * P:(j + 1) * P],
                            func=AF.Identity, bias=LNC[:, lnw_off + 8 + cc:lnw_off + 9 + cc],
                            scale=LNC[:, lnw_off + cc:lnw_off + 1 + cc]))
                rmark(R_pb[bank], SS)
            wmark(dst_res, SS)

        def ccmm_fm(wslot, in_big, in_res_list, out_big, out_res, kind, other_big=None,
                    banks=None):
            wfn, wres = wslot
            for m in range(KCH):
                if NTG == 4:
                    bset = [(m % 2) * 4 + tg for tg in range(4)] if banks is None else \
                           [banks[tg % len(banks)] for tg in range(4)]
                else:
                    bset = [(2, 3)[(m * NTG + tg) % 2] for tg in range(NTG)]
                for tg in range(NTG):
                    em.wguard("tensor", R_pb[bset[tg]])
                em.rguard("tensor", wres, *in_res_list)
                for k in range(KCH):
                    for tg in range(NTG):
                        bank = bset[tg]
                        T(lambda e, k=k, m=m, tg=tg, bank=bank: e.matmul(
                            pb[bank][:, :], wfn(k)[:, m * P:(m + 1) * P],
                            fm(in_big, k, tg * 512, 512),
                            start=(k == 0), stop=(k == KCH - 1)),
                          inc=(k == KCH - 1 and tg == NTG - 1))
                rmark(wres, TS)
                for rr in in_res_list:
                    rmark(rr, TS)
                for tg in range(NTG):
                    bank = bset[tg]
                    wmark(R_pb[bank], TS)
                    if kind == "mult":
                        em.rguard("vector", R_pb[bank])
                        V(lambda e, m=m, tg=tg, bank=bank: e.tensor_tensor(
                            out=fm(out_big, m, tg * 512, 512),
                            in0=fm(other_big, m, tg * 512, 512),
                            in1=pb[bank][:, :], op=ALU.mult))
                        rmark(R_pb[bank], VS)
                    else:
                        em.rguard("scalar", R_pb[bank])
                        if kind == "copy":
                            S(lambda e, m=m, tg=tg, bank=bank: e.copy(
                                out=fm(out_big, m, tg * 512, 512), in_=pb[bank][:, :]))
                        else:
                            S(lambda e, m=m, tg=tg, bank=bank: e.activation(
                                out=fm(out_big, m, tg * 512, 512), in_=pb[bank][:, :],
                                func=AF.Sigmoid))
                        rmark(R_pb[bank], SS)
            wmark(out_res, VS if kind == "mult" else SS)

        def ccmm_tok(wslot, in_big, in_res_list, evac, banks=((4, 5), (6, 7))):
            wfn, wres = wslot
            for t in range(NT):
                bpair = banks[t % 2]
                em.wguard("tensor", R_pb[bpair[0]])
                em.wguard("tensor", R_pb[bpair[1]])
                em.rguard("tensor", wres, *in_res_list)
                for k in range(KCH):
                    for half in range(2):
                        T(lambda e, k=k, t=t, half=half, bank=bpair[half]: e.matmul(
                            pb[bank][:, :], fm(in_big, k, t * P, P),
                            wfn(k)[:, half * 512:(half + 1) * 512],
                            start=(k == 0), stop=(k == KCH - 1)),
                          inc=(k == KCH - 1 and half == 1))
                rmark(wres, TS)
                for rr in in_res_list:
                    rmark(rr, TS)
                for half in range(2):
                    wmark(R_pb[bpair[half]], TS)
                    evac(t, half, bpair[half])

        def emit_ffn(second):
            pref = "2" if second else "1"
            R_xlnf = [Res("xlnf%d_%d" % (second, g)) for g in range(NTG)]
            R_stf = Res("stf%d" % second)
            R_sgf = Res("sgf%d" % second)
            R_xf = [Res("xf%d_%d" % (second, t)) for t in range(NT)]
            R_hf = [Res("hf%d_%d" % (second, g)) for g in range(NTG)]
            R_sg2f = Res("sg2f%d" % second)
            R_hrh = Res("hrh%d" % second)
            R_ks = [Res("ks%d_%d" % (second, i)) for i in range(2)]
            R_vst = [Res("vst%d_%d" % (second, i)) for i in range(2)]

            ws_slot = stage_w(ws_in, 0)
            for tg in range(NTG):
                for t in range(tg * 4, tg * 4 + 4):
                    em.rguard("vector", R_xs[t])
                    em.rguard("scalar", R_xs[t])
                    if t == 0:
                        claim("scalar", "B1", R_xlnf[0])
                        for g2 in range(1, NTG):
                            add_owner("B1", R_xlnf[g2])
                    ln_transpose_tile(t, xs_ap(t), BIG1, R_xlnf[tg],
                                      lnw_off=0, double_ln=True, save_stats=True)
                    rmark(R_xs[t], SS)
                    rmark(R_xs[t], VS)
                for m in range(KCH):
                    bset = [2, 3, 4, 5] if NTG == 4 else [(2, 3)[m % 2]] * NTG
                    for tg2 in range(tg + 1):
                        pass
                    bank = bset[(m + tg * KCH) % len(bset)]
                    em.wguard("tensor", R_pb[bank])
                    em.rguard("tensor", ws_slot[1], R_xlnf[tg])
                    if m == 0 and tg == 0:
                        claim("scalar", "B2", R_stf)
                    for k in range(KCH):
                        T(lambda e, k=k, m=m, tg=tg, bank=bank: e.matmul(
                            pb[bank][:, :], ws_slot[0](k)[:, m * P:(m + 1) * P],
                            fm(BIG1, k, tg * 512, 512),
                            start=(k == 0), stop=(k == KCH - 1)), inc=(k == KCH - 1))
                    wmark(R_pb[bank], TS)
                    rmark(ws_slot[1], TS)
                    rmark(R_xlnf[tg], TS)
                    em.rguard("scalar", R_pb[bank])
                    S(lambda e, m=m, tg=tg, bank=bank: e.copy(
                        out=fm(BIG2, m, tg * 512, 512), in_=pb[bank][:, :]))
                    rmark(R_pb[bank], SS)
            wmark(R_stf, SS)
            ck(pref + "f2")

            wr_slot = stage_w(wr_in, 1)
            claim("scalar", "B3", R_sgf)
            ccmm_fm(wr_slot, BIG1, R_xlnf, BIG3, R_sgf, "sigmoid")
            ck(pref + "f3")
            wv_slot = stage_w(wvf_in, 0)
            ccmm_fm(wv_slot, BIG1, R_xlnf, BIG3, R_sgf, "mult", other_big=BIG3)
            ck(pref + "f4")
            wo_slot = stage_w(wof_in, 1)

            def evac_f5(t, half, bank):
                if t == 0 and half == 0:
                    claim("scalar", "B1", R_xf[0])
                    for t2 in range(1, NT):
                        add_owner("B1", R_xf[t2])
                em.rguard("scalar", R_sstat)
                em.rguard("scalar", R_xs[t])
                S(lambda e, t=t, half=half: e.activation(
                    out=BIG1[:, t * C + half * 512: t * C + (half + 1) * 512],
                    in_=xs_ap(t, half), func=AF.Identity,
                    bias=SM[:, SST0 + 2 * t + 1:SST0 + 2 * t + 2],
                    scale=SM[:, SST0 + 2 * t:SST0 + 2 * t + 1]))
                rmark(R_sstat, SS)
                rmark(R_xs[t], SS)
                em.wait("vector", SS)
                em.rguard("vector", R_pb[bank])
                V(lambda e, t=t, half=half, bank=bank: e.tensor_tensor(
                    out=BIG1[:, t * C + half * 512: t * C + (half + 1) * 512],
                    in0=BIG1[:, t * C + half * 512: t * C + (half + 1) * 512],
                    in1=pb[bank][:, :], op=ALU.add))
                rmark(R_pb[bank], VS)
                wmark(R_xf[t], VS)
                em.wguard("vector", R_xs[t])
                V(lambda e, t=t, half=half: e.scalar_tensor_tensor(
                    out=xs_ap(t, half),
                    in0=BIG1[:, t * C + half * 512: t * C + (half + 1) * 512],
                    scalar=0.5, in1=xs_ap(t, half), op0=ALU.mult, op1=ALU.add))
                rmark(R_xf[t], VS)
                wmark(R_xs[t], VS)

            ccmm_tok(wo_slot, BIG3, [R_sgf], evac_f5)
            ck(pref + "f5")
            w2_slot = stage_w(w2_in, 0)

            for tg in range(NTG):
                for t in range(tg * 4, tg * 4 + 4):
                    em.rguard("vector", R_xf[t])
                    em.rguard("scalar", R_xf[t])
                    if t == 0:
                        claim("scalar", "B3", R_hf[0])
                        for g2 in range(1, NTG):
                            add_owner("B3", R_hf[g2])
                    ln_transpose_tile(t, BIG1[:, t * C:(t + 1) * C], BIG3, R_hf[tg],
                                      lnw_off=16, double_ln=False, save_stats=False)
                    rmark(R_xf[t], SS)
                    rmark(R_xf[t], VS)
            ck(pref + "f6")
            claim("scalar", "B1", R_sg2f)
            ccmm_fm(w2_slot, BIG2, [R_stf], BIG1, R_sg2f, "sigmoid")
            ck(pref + "f7")
            w1_slot = stage_w(w1_in, 1)
            ccmm_fm(w1_slot, BIG3, R_hf, BIG1, R_sg2f, "mult", other_big=BIG1)
            ck(pref + "f8")

            for t in range(NT):
                bank = 6 + (t % 2)
                par = t % 2
                em.wguard("tensor", R_pb[bank])
                em.rguard("tensor", R_smb, R_hf[t // 4])
                for k in range(KCH):
                    T(lambda e, k=k, t=t, bank=bank: e.matmul(
                        pb[bank][:, 0:7], fm(BIG3, k, t * P, P),
                        SMB[:, k * 8:k * 8 + 7],
                        start=(k == 0), stop=(k == KCH - 1)), inc=(k == KCH - 1))
                wmark(R_pb[bank], TS)
                rmark(R_smb, TS)
                rmark(R_hf[t // 4], TS)
                b0 = BID0 + par * 24
                em.rguard("scalar", R_pb[bank])
                em.wguard("scalar", R_bid[par])
                S(lambda e, bank=bank, b0=b0: e.activation(out=SM[:, b0:b0 + 4],
                                                           in_=pb[bank][:, 0:4], func=AF.Sigmoid))
                wmark(R_bid[par], SS)
                em.wait("vector", SS)
                em.wait("vector", DSY, smalls_dma)
                Vd(lambda e, b0=b0: e.tensor_single_scalar(SM[:, b0 + 4:b0 + 7], SH3[:, :],
                                                          SM[:, b0 + 3:b0 + 4], ALU.mult))
                Vd(lambda e, b0=b0: e.tensor_tensor(SM[:, b0 + 7:b0 + 10], SM[:, b0:b0 + 3],
                                                   SM[:, b0 + 4:b0 + 7], ALU.mult))
                em.rguard("vector", R_pb[bank])
                Vd(lambda e, bank=bank, b0=b0: e.tensor_tensor(
                    SM[:, b0 + 7:b0 + 10], SM[:, b0 + 7:b0 + 10], pb[bank][:, 4:7], ALU.add))
                rmark(R_pb[bank], VS)
                Vd(lambda e, b0=b0: e.reduce_max(SM[:, b0 + 10:b0 + 11], SM[:, b0 + 7:b0 + 10], axis=AX))
                Vd(lambda e, b0=b0: e.tensor_single_scalar(SM[:, b0 + 11:b0 + 14],
                                                          SM[:, b0 + 7:b0 + 10],
                                                          SM[:, b0 + 10:b0 + 11], ALU.is_ge))
                Vd(lambda e, b0=b0: e.tensor_copy(out=SM[:, b0 + 14:b0 + 15], in_=SM[:, b0 + 11:b0 + 12]))
                Vd(lambda e, b0=b0: e.tensor_tensor(SM[:, b0 + 17:b0 + 18], SM[:, b0 + 12:b0 + 13],
                                                   SM[:, b0 + 11:b0 + 12], ALU.mult))
                Vd(lambda e, b0=b0: e.tensor_tensor(SM[:, b0 + 15:b0 + 16], SM[:, b0 + 12:b0 + 13],
                                                   SM[:, b0 + 17:b0 + 18], ALU.subtract))
                Vd(lambda e, b0=b0: e.tensor_tensor(SM[:, b0 + 17:b0 + 18], SM[:, b0 + 14:b0 + 15],
                                                   SM[:, b0 + 15:b0 + 16], ALU.add))
                Vd(lambda e, b0=b0: e.scalar_tensor_tensor(
                    out=SM[:, b0 + 16:b0 + 17], in0=SM[:, b0 + 17:b0 + 18], scalar=-1.0,
                    in1=SM[:, ONEC:ONEC + 1], op0=ALU.mult, op1=ALU.add))
                Vd(lambda e, b0=b0: e.tensor_tensor(SM[:, b0 + 17:b0 + 20], SM[:, b0 + 14:b0 + 17],
                                                   SM[:, b0:b0 + 3], ALU.mult))
                Vd(lambda e, b0=b0: e.reduce_sum(SM[:, b0 + 20:b0 + 21], SM[:, b0 + 17:b0 + 20], axis=AX))
                Vd(lambda e, b0=b0: e.tensor_scalar_add(SM[:, b0 + 21:b0 + 22], SM[:, b0 + 20:b0 + 21], 1e-6))
                Vd(lambda e, b0=b0: e.reciprocal(SM[:, b0 + 22:b0 + 23], SM[:, b0 + 21:b0 + 22]))
                Vd(lambda e, b0=b0: e.tensor_tensor(SM[:, b0 + 23:b0 + 24], SM[:, b0 + 20:b0 + 21],
                                                   SM[:, b0 + 22:b0 + 23], ALU.mult))
                Vd(lambda e, b0=b0: e.tensor_scalar_mul(SM[:, b0 + 23:b0 + 24], SM[:, b0 + 23:b0 + 24], 0.5))
                Vd(lambda e, t=t, b0=b0: e.tensor_single_scalar(
                    SM[:, WTK0 + 3 * t:WTK0 + 3 * t + 3], SM[:, b0 + 14:b0 + 17],
                    SM[:, b0 + 23:b0 + 24], ALU.mult))
                rmark(R_bid[par], VS)
            wmark(R_wtok, VS)
            ck(pref + "f9")

            w3_slot = stage_w(w3_in, 0)

            def evac_f10(t, half, bank):
                em.rguard("vector", R_pb[bank], R_wtok)
                em.wguard("vector", R_xs[t])
                V(lambda e, t=t, half=half, bank=bank: e.scalar_tensor_tensor(
                    out=xs_ap(t, half), in0=pb[bank][:, :],
                    scalar=SM[:, WTK0 + 3 * t + 2:WTK0 + 3 * t + 3],
                    in1=xs_ap(t, half), op0=ALU.mult, op1=ALU.add))
                rmark(R_pb[bank], VS)
                rmark(R_wtok, VS)
                wmark(R_xs[t], VS)

            ccmm_tok(w3_slot, BIG1, [R_sg2f], evac_f10)
            ck(pref + "f10")

            NQ = 4
            for e_ in range(2):
                for q4 in range(NQ):
                    qi = e_ * NQ + q4
                    ks = R_ks[qi % 2]
                    koff = (qi % 2) * 8 * C
                    if qi == 0 and not second:
                        claim("gpsimd", "B2", R_ks[0])
                        add_owner("B2", R_ks[1])
                    elif qi == 0:
                        claim("gpsimd", "B2", R_ks[0])
                        add_owner("B2", R_ks[1])
                    else:
                        em.wguard("gpsimd", ks)
                    GD(lambda e, e_=e_, q4=q4, koff=koff: e.dma_start(
                        out=BIG2[:, koff:koff + 8 * HQ].rearrange("p (k n) -> p k n", n=HQ),
                        in_=krw_in[e_][:, q4 * HQ:(q4 + 1) * HQ]
                        .rearrange("(k p) n -> p k n", p=P)))
                    wmark(ks, DG)
                    vs_ = R_vst[qi % 2]
                    voff = (qi % 2) * 8 * C
                    if qi == 0:
                        claim("gpsimd", "B1", R_vst[0])
                        add_owner("B1", R_vst[1])
                    else:
                        em.wguard("gpsimd", vs_)
                    GD(lambda e, e_=e_, q4=q4, voff=voff: e.dma_start(
                        out=BIG1[:, voff:voff + 8 * C].rearrange("p (m c) -> p m c", c=C),
                        in_=vrw_in[e_][q4 * HQ:(q4 + 1) * HQ, :].rearrange(
                            "(m p) c -> p m c", p=P)))
                    wmark(vs_, DG)
                    if qi == 0:
                        claim("vector", "B4a", R_hrh)
                        claim("vector", "B4b", R_hrh)
                    else:
                        em.wguard("vector", R_hrh)
                    for hml in range(NHMQ):
                        if NTG == 4:
                            bset = [(hml % 2) * 4 + tg for tg in range(4)]
                        else:
                            bset = [(0, 1, 2, 3)[(hml * NTG + tg) % 4] for tg in range(NTG)]
                        for tg in range(NTG):
                            em.wguard("tensor", R_pb[bset[tg]])
                        em.rguard("tensor", ks)
                        for tg in range(NTG):
                            em.rguard("tensor", R_hf[tg])
                        for k in range(KCH):
                            for tg in range(NTG):
                                T(lambda e, k=k, hml=hml, tg=tg, bank=bset[tg], koff=koff: e.matmul(
                                    pb[bank][:, :],
                                    BIG2[:, koff + k * HQ + hml * P: koff + k * HQ + (hml + 1) * P],
                                    fm(BIG3, k, tg * 512, 512),
                                    start=(k == 0), stop=(k == KCH - 1)),
                                  inc=(k == KCH - 1 and tg == NTG - 1))
                        rmark(ks, TS)
                        for tg in range(NTG):
                            rmark(R_hf[tg], TS)
                        for tg in range(NTG):
                            bank = bset[tg]
                            wmark(R_pb[bank], TS)
                            sl = (hml * NTG + tg) % 2
                            em.wguard("scalar", R_sb512[sl])
                            em.rguard("scalar", R_pb[bank])
                            S(lambda e, bank=bank, sl=sl: e.activation(
                                out=SB512[:, sl * 512:(sl + 1) * 512], in_=pb[bank][:, :],
                                func=AF.Relu))
                            rmark(R_pb[bank], SS)
                            wmark(R_sb512[sl], SS)
                            em.rguard("vector", R_sb512[sl])
                            V(lambda e, hml=hml, tg=tg, sl=sl: e.tensor_tensor(
                                out=BIG4[:, hml * TCORE + tg * 512: hml * TCORE + (tg + 1) * 512],
                                in0=SB512[:, sl * 512:(sl + 1) * 512],
                                in1=SB512[:, sl * 512:(sl + 1) * 512], op=ALU.mult))
                            rmark(R_sb512[sl], VS)
                    wmark(R_hrh, VS)
                    for t in range(NT):
                        bpair = ((4, 5), (6, 7))[t % 2]
                        em.wguard("tensor", R_pb[bpair[0]])
                        em.wguard("tensor", R_pb[bpair[1]])
                        em.rguard("tensor", R_hrh, vs_)
                        for hm in range(NHMQ):
                            for ch in range(2):
                                T(lambda e, hm=hm, t=t, ch=ch, bank=bpair[ch], voff=voff: e.matmul(
                                    pb[bank][:, :],
                                    BIG4[:, hm * TCORE + t * P: hm * TCORE + (t + 1) * P],
                                    BIG1[:, voff + hm * C + ch * 512: voff + hm * C + (ch + 1) * 512],
                                    start=(hm == 0), stop=(hm == NHMQ - 1)),
                                  inc=(hm == NHMQ - 1 and ch == 1))
                        rmark(R_hrh, TS)
                        rmark(vs_, TS)
                        for ch in range(2):
                            bank = bpair[ch]
                            wmark(R_pb[bank], TS)
                            em.rguard("vector", R_pb[bank], R_wtok)
                            em.wguard("vector", R_xs[t])
                            V(lambda e, t=t, ch=ch, bank=bank, e_=e_: e.scalar_tensor_tensor(
                                out=xs_ap(t, ch), in0=pb[bank][:, :],
                                scalar=SM[:, WTK0 + 3 * t + e_:WTK0 + 3 * t + e_ + 1],
                                in1=xs_ap(t, ch), op0=ALU.mult, op1=ALU.add))
                            rmark(R_pb[bank], VS)
                            rmark(R_wtok, VS)
                            wmark(R_xs[t], VS)
                            if second and e_ == 1 and q4 == NQ - 1:
                                em.wait("sync", VS)
                                SD(lambda e, t=t, ch=ch: e.dma_start(
                                    out=y_out[t * P:(t + 1) * P, ch * 512:(ch + 1) * 512],
                                    in_=xs_ap(t, ch)))
                                rmark(R_xs[t], DSY)
            ck(pref + "f11")

        def emit_attention():
            R_y1f = [Res("y1f%d" % g) for g in range(NTG)]
            R_qf = Res("qf")
            R_ko = Res("ko")
            R_vo = Res("vo")
            R_ka = Res("ka")
            R_va = Res("va")
            R_attw = Res("attw")
            R_kmk = [Res("kmk%d" % i) for i in range(2)]
            R_osb = [Res("osb%d" % i) for i in range(2)]
            R_ptb = [Res("ptb%d" % i) for i in range(2)]
            R_lsb = [Res("lsb"), ]
            R_pbf = R_sb512

            for tg in range(NTG):
                for t in range(tg * 4, tg * 4 + 4):
                    em.rguard("vector", R_xs[t])
                    em.rguard("scalar", R_xs[t])
                    if t == 0:
                        claim("scalar", "B1", R_y1f[0])
                        for g2 in range(1, NTG):
                            add_owner("B1", R_y1f[g2])
                    ln_transpose_tile(t, xs_ap(t), BIG1, R_y1f[tg],
                                      lnw_off=0, double_ln=False, save_stats=False, plain=True)
                    rmark(R_xs[t], SS)
                    rmark(R_xs[t], VS)
            ck("a1")

            AW0 = 3 * TCORE
            KMS0 = AW0 + 4096
            claim("gpsimd", "B3", R_attw)
            GD(lambda e: e.dma_start(
                out=BIG3[:, AW0:AW0 + 1024].rearrange("p (k n) -> p k n", n=DK),
                in_=wq_in.rearrange("(k p) n -> p k n", p=P)))
            GD(lambda e: e.dma_start(
                out=BIG3[:, AW0 + 1024:AW0 + 2048].rearrange("p (k n) -> p k n", n=DK),
                in_=wk_in.rearrange("(k p) n -> p k n", p=P)))
            GD(lambda e: e.dma_start(
                out=BIG3[:, AW0 + 2048:AW0 + 3072].rearrange("p (k n) -> p k n", n=DK),
                in_=wva_in.rearrange("(k p) n -> p k n", p=P)))
            GD(lambda e: e.dma_start(out=BIG3[:, AW0 + 3072:AW0 + 4096], in_=woa_in))
            wmark(R_attw, DG)

            def wq_k(k):
                return BIG3[:, AW0 + k * DK:AW0 + (k + 1) * DK]

            def wk_k(k):
                return BIG3[:, AW0 + 1024 + k * DK:AW0 + 1024 + (k + 1) * DK]

            def wv_k(k):
                return BIG3[:, AW0 + 2048 + k * DK:AW0 + 2048 + (k + 1) * DK]

            WOap = BIG3[:, AW0 + 3072:AW0 + 4096]
            QF0, KO0, VO0 = 0, TCORE, 2 * TCORE

            for (dst0, wfn, scale, res) in ((QF0, wq_k, INV_SQRT_DK, R_qf),
                                            (KO0, wk_k, 1.0, R_ko)):
                for tg in range(NTG):
                    bank = (2, 3)[tg % 2]
                    em.wguard("tensor", R_pb[bank])
                    em.rguard("tensor", R_attw, R_y1f[tg])
                    for k in range(KCH):
                        T(lambda e, k=k, tg=tg, bank=bank, wfn=wfn: e.matmul(
                            pb[bank][:, :],
                            wfn(k), fm(BIG1, k, tg * 512, 512),
                            start=(k == 0), stop=(k == KCH - 1)), inc=(k == KCH - 1))
                    wmark(R_pb[bank], TS)
                    rmark(R_attw, TS)
                    rmark(R_y1f[tg], TS)
                    if tg == 0:
                        add_owner("B3", res)
                    em.rguard("scalar", R_pb[bank])
                    S(lambda e, dst0=dst0, tg=tg, bank=bank, scale=scale: e.activation(
                        out=BIG3[:, dst0 + tg * 512: dst0 + (tg + 1) * 512],
                        in_=pb[bank][:, :], func=AF.Copy, scale=scale))
                    rmark(R_pb[bank], SS)
                wmark(res, SS)
                add_owner("B3", res)
            for t in range(NT):
                bank = (2, 3)[t % 2]
                em.wguard("tensor", R_pb[bank])
                em.rguard("tensor", R_attw, R_y1f[t // 4])
                for k in range(KCH):
                    T(lambda e, k=k, t=t, bank=bank: e.matmul(
                        pb[bank][:, 0:P], fm(BIG1, k, t * P, P), wv_k(k),
                        start=(k == 0), stop=(k == KCH - 1)), inc=(k == KCH - 1))
                wmark(R_pb[bank], TS)
                rmark(R_attw, TS)
                rmark(R_y1f[t // 4], TS)
                em.rguard("scalar", R_pb[bank])
                S(lambda e, t=t, bank=bank: e.copy(
                    out=BIG3[:, VO0 + t * P: VO0 + (t + 1) * P], in_=pb[bank][:, 0:P]))
                rmark(R_pb[bank], SS)
            wmark(R_vo, SS)
            add_owner("B3", R_vo)
            ck("a2")

            em.rguard("gpsimd", R_ko, R_vo)
            GD(lambda e: e.dma_start(out=send_d.ap()[:, 0:TCORE], in_=BIG3[:, KO0:KO0 + TCORE]))
            GD(lambda e: e.dma_start(out=send_d.ap()[:, TCORE:2 * TCORE],
                                     in_=BIG3[:, VO0:VO0 + TCORE]))
            rmark(R_ko, DG)
            rmark(R_vo, DG)
            wmark(R_send, DG)
            em.wait("gpsimd", DG)
            em.do("gpsimd", lambda e: e.collective_compute(
                "AllGather", ALU.bypass, replica_groups=[list(range(NCORES))],
                ins=[send_d.ap()], outs=[recv_d.ap()]), CCS, 1)
            wmark(R_recv, CCS)
            em.wait("gpsimd", CCS)
            claim("gpsimd", "B2", R_ka)
            rv = recv_d.ap().rearrange("(r p) m -> p r m", p=P)
            GD(lambda e: e.dma_start(
                out=BIG2[:, 0:GK].rearrange("p (r n) -> p r n", n=TCORE),
                in_=rv[:, :, 0:TCORE]))
            wmark(R_ka, DG)
            claim("gpsimd", "B4a", R_va)
            claim("gpsimd", "B4b", R_va)
            GD(lambda e: e.dma_start(
                out=BIG4[:, 0:GK].rearrange("p (r n) -> p r n", n=TCORE),
                in_=rv[:, :, TCORE:2 * TCORE]))
            wmark(R_va, DG)
            em.rguard("vector", R_ka)
            for i in range(GK // 2048):
                ksl = KMS0 + (i % 2) * 2048
                em.wguard("gpsimd", R_kmk[i % 2])
                GD(lambda e, i=i, ksl=ksl: e.dma_start(
                    out=BIG3[:, ksl:ksl + 2048], in_=kmask_in[:, i * 2048:(i + 1) * 2048]))
                wmark(R_kmk[i % 2], DG)
                em.rguard("vector", R_kmk[i % 2])
                V(lambda e, i=i, ksl=ksl: e.tensor_tensor(
                    out=BIG2[:, i * 2048:(i + 1) * 2048],
                    in0=BIG2[:, i * 2048:(i + 1) * 2048],
                    in1=BIG3[:, ksl:ksl + 2048], op=ALU.mult))
                rmark(R_kmk[i % 2], VS)
            wmark(R_ka, VS)
            em.rguard("vector", R_va)
            em.wait("vector", DSY, smalls_dma)
            for j in range(GK // P):
                V(lambda e, j=j: e.tensor_single_scalar(
                    BIG4[:, j * P:(j + 1) * P],
                    BIG4[:, j * P:(j + 1) * P],
                    VMK[:, j:j + 1], ALU.mult))
            wmark(R_va, VS)
            ck("a5")

            nq, nk = NRM0, NRM0 + 1
            V(lambda e: e.memset(SM[0:1, nq:nq + 2], 0.0))
            em.wait("tensor", DSY, smalls_dma)

            def norm_scan(big, src_off, nmax_col, nchunks, res_r):
                for ch_ in range(nchunks):
                    sl = ch_ % 2
                    em.rguard("vector", res_r)
                    em.wguard("vector", R_pbf[sl])
                    V(lambda e, big=big, ch_=ch_, src_off=src_off, sl=sl: e.tensor_tensor(
                        out=SB512[:, sl * 512:(sl + 1) * 512],
                        in0=big[:, src_off + ch_ * 512: src_off + (ch_ + 1) * 512],
                        in1=big[:, src_off + ch_ * 512: src_off + (ch_ + 1) * 512], op=ALU.mult))
                    wmark(R_pbf[sl], VS)
                    em.wguard("tensor", R_pb[7])
                    em.rguard("tensor", R_pbf[sl])
                    T(lambda e, sl=sl: e.matmul(pb[7][0:1, :], ONESP[:, :],
                                                SB512[:, sl * 512:(sl + 1) * 512],
                                                start=True, stop=True), inc=True)
                    wmark(R_pb[7], TS)
                    rmark(R_pbf[sl], TS)
                    em.rguard("vector", R_pb[7])
                    Vd(lambda e: e.reduce_max(SM[0:1, NRM0 + 2:NRM0 + 3], pb[7][0:1, :], axis=AX))
                    Vd(lambda e, nmax_col=nmax_col: e.tensor_tensor(
                        SM[0:1, nmax_col:nmax_col + 1], SM[0:1, nmax_col:nmax_col + 1],
                        SM[0:1, NRM0 + 2:NRM0 + 3], ALU.max))
                    rmark(R_pb[7], VS)

            norm_scan(BIG3, QF0, nq, TCORE // 512, R_qf)
            norm_scan(BIG3, KO0, nk, TCORE // 512, R_ko)
            norm_scan(BIG2, 0, nk, GK // 512, R_ka)
            Vd(lambda e: e.tensor_tensor(SM[0:1, NRM0 + 2:NRM0 + 3], SM[0:1, nq:nq + 1],
                                        SM[0:1, nk:nk + 1], ALU.mult))
            em.wait("scalar", VS)
            Sd(lambda e: e.sqrt(out=SM[0:1, NRM0 + 3:NRM0 + 4], in_=SM[0:1, NRM0 + 2:NRM0 + 3]))
            em.wguard("scalar", R_smb)
            Sd(lambda e: e.copy(out=SMB[0:1, 63:64], in_=SM[0:1, NRM0 + 3:NRM0 + 4]))
            wmark(R_smb, SS)
            em.wait("tensor", SS)
            em.wguard("tensor", R_pb[7])
            T(lambda e: e.matmul(pb[7][:, 0:1], ONESR[0:1, :], SMB[0:1, 63:64],
                                 start=True, stop=True, skip_group_check=True), inc=True)
            wmark(R_pb[7], TS)
            em.rguard("vector", R_pb[7])
            Vd(lambda e: e.tensor_scalar_mul(SM[:, NRM0 + 5:NRM0 + 6], pb[7][:, 0:1], -1.0))
            rmark(R_pb[7], VS)
            negm_c = NRM0 + 5
            em.wait("scalar", VS)
            Sd(lambda e: e.activation(out=SM[:, NRM0 + 6:NRM0 + 7], in_=SM[:, negm_c:negm_c + 1],
                                     func=AF.Exp))
            em.wguard("scalar", R_smb)
            Sd(lambda e: e.copy(out=SMB[0:1, 62:63], in_=SM[0:1, NRM0 + 6:NRM0 + 7]))
            Sd(lambda e: e.copy(out=SM[0:1, NRM0 + 8:NRM0 + 9], in_=SMB[0:1, 62:63]))
            wmark(R_smb, SS)
            em.wait("tensor", SS)
            em.wguard("tensor", R_pb[1])
            T(lambda e: e.matmul(pb[1][:, 0:1], ONESR[0:1, :], SMB[0:1, 62:63],
                                 start=True, stop=True, skip_group_check=True), inc=True)
            wmark(R_pb[1], TS)
            em.rguard("vector", R_pb[1])
            Vd(lambda e: e.tensor_copy(out=SM[:, NRM0 + 6:NRM0 + 7], in_=pb[1][:, 0:1]))
            rmark(R_pb[1], VS)
            em.wait("vector", SS)
            em.wait("vector", DSY, smalls_dma)
            Vd(lambda e: e.tensor_tensor(SM[:, NRM0 + 7:NRM0 + 8], SM[:, NRM0 + 6:NRM0 + 7],
                                        NZERO[:, :], ALU.mult))
            wmark(R_norm, VS)
            ck("a3")

            NSC = GK // P
            for g in range(NTG):
                par = g % 2
                obank = 6 + par
                lbank = 0 + par
                qlo = g * 512
                em.wguard("tensor", R_pb[obank])
                em.wguard("tensor", R_pb[lbank])
                state = {"first": True, "idx": 0, "pend": []}

                def st_sub(src_big, koff, vtile, masked_ki, last):
                    idx = state["idx"]
                    state["idx"] += 1
                    sbank = (2, 3)[idx % 2]
                    sl = idx % 2
                    em.wguard("tensor", R_pb[sbank])
                    em.rguard("tensor", R_qf, R_ka if src_big is BIG2 else R_ko)
                    T(lambda e, src_big=src_big, koff=koff, sbank=sbank, qlo=qlo: e.matmul(
                        pb[sbank][:, :], src_big[:, koff:koff + P],
                        BIG3[:, QF0 + qlo: QF0 + qlo + 512], start=True, stop=True,
                        skip_group_check=True), inc=True)
                    wmark(R_pb[sbank], TS)
                    rmark(R_qf, TS)
                    rmark(R_ka if src_big is BIG2 else R_ko, TS)
                    em.rguard("scalar", R_pb[sbank], R_norm)
                    em.wguard("scalar", R_ptb[sl])
                    S(lambda e, sbank=sbank, sl=sl: e.activation(
                        out=TMP2[:, sl * 512:(sl + 1) * 512], in_=pb[sbank][:, :],
                        func=AF.Exp, bias=SM[:, negm_c:negm_c + 1]))
                    rmark(R_pb[sbank], SS)
                    rmark(R_norm, SS)
                    wmark(R_ptb[sl], SS)
                    if masked_ki is not None:
                        em.rguard("vector", R_ptb[sl])
                        em.wait("vector", DSY, smalls_dma)
                        V(lambda e, sl=sl, ki=masked_ki: e.tensor_tensor(
                            out=TMP2[:, sl * 512:(sl + 1) * 512],
                            in0=TMP2[:, sl * 512:(sl + 1) * 512],
                            in1=GMK[:, ki * 512:(ki + 1) * 512], op=ALU.mult))
                        wmark(R_ptb[sl], VS)
                    state["pend"].append((sl, vtile, last))
                    if len(state["pend"]) >= 2 or last:
                        flush()

                def flush():
                    while state["pend"]:
                        if len(state["pend"]) < 2 and not state["pend"][0][2]:
                            return
                        sl, vtile, last = state["pend"].pop(0)
                        first = state["first"]
                        state["first"] = False
                        em.rguard("tensor", R_ptb[sl])
                        em.wait("tensor", DSY, smalls_dma)
                        T(lambda e, sl=sl, lbank=lbank, first=first, last=last: e.matmul(
                            pb[lbank][0:1, :], ONESP[:, :],
                            TMP2[:, sl * 512:(sl + 1) * 512],
                            start=first, stop=last, skip_group_check=True), inc=False)
                        em.rguard("tensor", R_va if vtile[0] is BIG4 else R_vo)
                        T(lambda e, sl=sl, vtile=vtile, obank=obank, first=first, last=last: e.matmul(
                            pb[obank][:, :], vtile[0][:, vtile[1]:vtile[1] + P],
                            TMP2[:, sl * 512:(sl + 1) * 512],
                            start=first, stop=last, skip_group_check=True), inc=last)
                        rmark(R_ptb[sl], TS)
                        rmark(R_va if vtile[0] is BIG4 else R_vo, TS)

                for sc in range(NSC):
                    st_sub(BIG2, sc * P, (BIG4, sc * P), None, False)
                for sc in range(4 * g):
                    st_sub(BIG3, KO0 + sc * P, (BIG3, VO0 + sc * P), None, False)
                for ki in range(4):
                    st_sub(BIG3, KO0 + qlo + ki * P, (BIG3, VO0 + qlo + ki * P), ki,
                           ki == 3)
                wmark(R_pb[obank], TS)
                wmark(R_pb[lbank], TS)
                em.rguard("scalar", R_pb[lbank])
                em.wguard("scalar", R_lsb[0])
                S(lambda e, lbank=lbank: e.copy(
                    out=LSF[0:1, :], in_=pb[lbank][0:1, :]))
                wmark(R_lsb[0], SS)
                rmark(R_pb[lbank], SS)
                for tq in range(4):
                    t = g * 4 + tq
                    tpar = t % 2
                    lcol = LT0 + tpar
                    em.wait("tensor", SS)
                    em.wguard("tensor", R_pb[4 + tq % 2])
                    em.rguard("tensor", R_lsb[0])
                    T(lambda e, tq=tq, bank=4 + tq % 2: e.matmul(
                        pb[bank][:, 0:1],
                        LSF[0:1, tq * P:(tq + 1) * P],
                        SM[0:1, ONEC:ONEC + 1],
                        start=True, stop=True, skip_group_check=True), inc=True)
                    wmark(R_pb[4 + tq % 2], TS)
                    rmark(R_lsb[0], TS)
                    em.rguard("vector", R_pb[4 + tq % 2], R_norm)
                    em.wguard("vector", R_lt[tpar])
                    Vd(lambda e, bank=4 + tq % 2, lcol=lcol: e.tensor_tensor(
                        SM[:, lcol:lcol + 1], pb[bank][:, 0:1],
                        SM[:, NRM0 + 7:NRM0 + 8], ALU.subtract))
                    rmark(R_pb[4 + tq % 2], VS)
                    rmark(R_norm, VS)
                    Vd(lambda e, lcol=lcol, tpar=tpar: e.reciprocal(
                        SM[:, LT0 + 4 + tpar:LT0 + 5 + tpar], SM[:, lcol:lcol + 1]))
                    wmark(R_lt[tpar], VS)
                em.rguard("scalar", R_pb[obank])
                em.wguard("scalar", R_osb[par])
                S(lambda e, obank=obank, par=par: e.copy(
                    out=TMP2[:, 1024 + par * 512:1024 + (par + 1) * 512],
                    in_=pb[obank][:, :]))
                wmark(R_osb[par], SS)
                rmark(R_pb[obank], SS)
                for tq in range(4):
                    t = g * 4 + tq
                    tpar = t % 2
                    linv_col = LT0 + 4 + tpar
                    for half in range(2):
                        wbank = (2, 3)[half]
                        em.wguard("tensor", R_pb[wbank])
                        em.rguard("tensor", R_osb[par], R_attw)
                        T(lambda e, half=half, par=par, tq=tq, wbank=wbank: e.matmul(
                            pb[wbank][:, :],
                            TMP2[:, 1024 + par * 512 + tq * P:1024 + par * 512 + (tq + 1) * P],
                            WOap[:, half * 512:(half + 1) * 512], start=True, stop=True,
                            skip_group_check=True), inc=True)
                        wmark(R_pb[wbank], TS)
                        rmark(R_osb[par], TS)
                        rmark(R_attw, TS)
                        em.rguard("vector", R_pb[wbank], R_lt[tpar])
                        em.wguard("vector", R_xs[t])
                        V(lambda e, t=t, half=half, wbank=wbank, linv_col=linv_col: e.scalar_tensor_tensor(
                            out=xs_ap(t, half), in0=pb[wbank][:, :],
                            scalar=SM[:, linv_col:linv_col + 1],
                            in1=xs_ap(t, half), op0=ALU.mult, op1=ALU.add))
                        rmark(R_pb[wbank], VS)
                        rmark(R_lt[tpar], VS)
                        wmark(R_xs[t], VS)
            for i in range(2):
                for (s_, v_) in list(R_ptb[i].rd.values()):
                    rmark(R_tmp2[i], s_)
                if R_ptb[i].w:
                    R_tmp2[i].w = R_ptb[i].w
                for (s_, v_) in list(R_osb[i].rd.values()):
                    rmark(R_tmp2[i], s_)

        try:
            emit_ffn(second=False)
            emit_attention()
            emit_ffn(second=True)
            ck("end")
        except _Stop:
            pass

        if debug_dump or stop_after is not None:
            em.wait("sync", VS)
            em.wait("sync", SS)
            em.wait("sync", TS)
            em.wait("sync", GS)
            em.wait("sync", DG)
            SD(lambda e: e.dma_start(
                out=dxs_out.rearrange("(t p) c -> p t c", p=P),
                in_=XS[:, :].rearrange("p (t c) -> p t c", c=C)))
            SD(lambda e: e.dma_start(out=dsm_out, in_=SM[:, :]))
            em.wait("gpsimd", DSY)
            for i, big in enumerate((BIG1, BIG2, BIG3, BIG4)):
                GD(lambda e, i=i, big=big: e.dma_start(out=db_out[i], in_=big[:, :]))

        em.wait("sync", DSY)
        em.wait("gpsimd", DG)

        for s in em.sems:
            s.h = st.enter_context(nc.semaphore(s.name))
        block = st.enter_context(nc.Block())

        def replay(engobj, eng):
            for op in em.ops[eng]:
                if op[0] == "wait":
                    engobj.wait_ge(op[1].h, op[2])
                else:
                    _, fn, sem, inc = op
                    ins = fn(engobj)
                    if sem is not None:
                        ins.then_inc(sem.h, inc)

        @block.sync
        def _(e):
            replay(e, "sync")

        @block.gpsimd
        def _(e):
            replay(e, "gpsimd")

        @block.scalar
        def _(e):
            replay(e, "scalar")

        @block.vector
        def _(e):
            replay(e, "vector")

        @block.tensor
        def _(e):
            replay(e, "tensor")

    em.simulate()
    return nc, em



def host_inputs(full, core, NT=16):
    TCORE = NT * P
    GK = 8 * TCORE
    b, g = core // 4, core % 4
    bf = ml_dtypes.bfloat16
    f32 = lambda k: np.ascontiguousarray(np.asarray(full[k], dtype=np.float32))
    d = {}
    x = f32("L0_attn_x")
    d["x"] = np.ascontiguousarray(x[b, g * TCORE:(g + 1) * TCORE, :])
    d["wq"] = f32("L0_attn_Wq")
    d["wk"] = f32("L0_attn_Wk")
    d["wva"] = f32("L0_attn_Wv")
    d["woa"] = f32("L0_attn_Wo")
    d["wr"] = f32("L0_ffn_Wr")
    d["wvf"] = f32("L0_ffn_Wv")
    d["wof"] = f32("L0_ffn_Wo")
    d["ws"] = f32("L0_ffn_Ws")
    d["w1"] = f32("L0_ffn_W1")
    d["w2"] = f32("L0_ffn_W2")
    d["w3"] = f32("L0_ffn_W3")
    d["krw"] = f32("L0_ffn_K_rwkv")
    d["vrw"] = f32("L0_ffn_V_rwkv")
    wsm = np.concatenate([
        f32("L0_ffn_conf_rwkv").T,
        f32("L0_ffn_conf_trans")[:, None],
        f32("L0_ffn_w_diff")[:, None],
        f32("L0_ffn_W_aff")], axis=1)
    d["wsmT"] = np.ascontiguousarray(wsm.T)
    lnc = np.zeros((P, 32), np.float32)
    for i, nm in enumerate(["ln1_w", "ln1_b", "ln2_w", "ln2_b"]):
        lnc[:, i * 8:(i + 1) * 8] = f32("L0_ffn_" + nm).reshape(8, P).T
    d["lnc"] = lnc
    d["sh3"] = np.ascontiguousarray(
        np.broadcast_to(f32("L0_ffn_capital_shares"), (P, 3)))
    d["idn"] = np.eye(P, dtype=np.float32).astype(bf)
    d["tri"] = np.tril(np.ones((P, P), np.float32)).astype(bf)
    gm = np.zeros((P, 4 * 512), np.float32)
    for ki in range(4):
        for k in range(P):
            gm[k, ki * 512 + ki * P + k:(ki + 1) * 512] = 1.0
    d["gmask"] = np.ascontiguousarray(gm).astype(bf)
    d["onesp"] = np.ones((P, 1), np.float32).astype(bf)
    d["onesr"] = np.ones((1, P), np.float32).astype(bf)
    km = np.zeros((GK,), np.float32)
    km[4 * b * TCORE: (4 * b + g) * TCORE] = 1.0
    d["kmask"] = np.ascontiguousarray(np.broadcast_to(km, (P, GK))).astype(bf)
    d["vmask"] = np.ascontiguousarray(km.reshape(GK // P, P).T)
    d["nzero"] = np.full((P, 1), float(GK - g * TCORE), np.float32)
    return d


def kernel(**inputs):
    NT = 16
    nc, _em = build_program(NT=NT)
    in_maps = [host_inputs(inputs, r, NT) for r in range(NCORES)]
    res = run_bass_kernel_spmd(nc, in_maps, core_ids=list(range(NCORES)))
    TCORE = NT * P
    out = np.zeros((2, 4 * TCORE, C), np.float32)
    for r in range(NCORES):
        b, g = r // 4, r % 4
        out[b, g * TCORE:(g + 1) * TCORE, :] = res.results[r]["y"]
    return out


# version 06
# speedup vs baseline: 1.0100x; 1.0100x over previous
import numpy as np
import ml_dtypes
from contextlib import ExitStack

import concourse.bass as bass
import concourse.mybir as mybir
from concourse.bass_utils import run_bass_kernel_spmd

F32 = mybir.dt.float32
BF16 = mybir.dt.bfloat16
AF = mybir.ActivationFunctionType
ALU = mybir.AluOpType
AX = mybir.AxisListType.X

P = 128
C = 1024
KCH = 8
H = 4096
DK = 128
NCORES = 8
EPS = 1e-5
INV_SQRT_DK = float(1.0 / np.sqrt(128.0))

ENGS = ("sync", "gpsimd", "scalar", "vector", "tensor")


class Sem:
    __slots__ = ("name", "owner", "c", "h")

    def __init__(self, name, owner=None):
        self.name = name
        self.owner = owner
        self.c = 0
        self.h = None


class Res:
    __slots__ = ("name", "w", "rd")

    def __init__(self, name):
        self.name = name
        self.w = None
        self.rd = {}


class Em:
    def __init__(self):
        self.ops = {e: [] for e in ENGS}
        self.hw = {e: {} for e in ENGS}
        self.sems = []

    def sem(self, name, owner=None):
        s = Sem(name, owner)
        self.sems.append(s)
        return s

    def wait(self, eng, sem, val=None):
        val = sem.c if val is None else val
        if val <= 0:
            return
        if sem.owner == eng:
            return
        if self.hw[eng].get(sem.name, -1) >= val:
            return
        self.hw[eng][sem.name] = val
        self.ops[eng].append(("wait", sem, val))

    def do(self, eng, fn, sem=None, inc=1):
        if sem is not None:
            sem.c += inc
            self.ops[eng].append(("do", fn, sem, inc))
            return sem.c
        self.ops[eng].append(("do", fn, None, 0))
        return None

    def rguard(self, eng, *ress):
        for r in ress:
            if r.w:
                self.wait(eng, r.w[0], r.w[1])

    def wguard(self, eng, *ress):
        for r in ress:
            if r.w:
                self.wait(eng, r.w[0], r.w[1])
            for (s, v) in list(r.rd.values()):
                self.wait(eng, s, v)

    def simulate(self):
        pos = {e: 0 for e in ENGS}
        cnt = {s.name: 0 for s in self.sems}
        progressed = True
        while progressed:
            progressed = False
            for e in ENGS:
                while pos[e] < len(self.ops[e]):
                    op = self.ops[e][pos[e]]
                    if op[0] == "wait":
                        if cnt[op[1].name] < op[2]:
                            break
                    else:
                        _, fn, sem, inc = op
                        if sem is not None:
                            cnt[sem.name] += inc
                    pos[e] += 1
                    progressed = True
        stuck = {e: (pos[e], len(self.ops[e])) for e in ENGS if pos[e] < len(self.ops[e])}
        if stuck:
            lines = []
            for e, (p_, n) in stuck.items():
                op = self.ops[e][p_]
                lines.append(f"  {e}: op {p_}/{n} waiting {op[1].name} >= {op[2]} (now {cnt[op[1].name]})")
            raise RuntimeError("DEADLOCK in simulated schedule:\n" + "\n".join(lines))

    def counts(self):
        return {e: len(self.ops[e]) for e in ENGS}


def rmark(res, sem):
    res.rd[id(sem)] = (sem, sem.c)


def wmark(res, sem):
    res.w = (sem, sem.c)
    res.rd = {}


class _Stop(Exception):
    pass


def build_program(NT=16, stop_after=None, debug_dump=False):
    TCORE = NT * P
    NTG = NT // 4
    GK = 8 * TCORE
    NKC = GK // 512
    HHALF = H // 2
    HQ = H // 4
    NHMQ = HQ // P

    nc = bass.Bass("TRN2", target_bir_lowering=False, debug=False)
    em = Em()

    def ck(name):
        if stop_after is not None and name == stop_after:
            raise _Stop()

    def din(name, shape, dt=F32):
        return nc.dram_tensor(name, shape, dt, kind="ExternalInput").ap()

    x_in = din("x", [TCORE, C])
    wq_in = din("wq", [C, DK])
    wk_in = din("wk", [C, DK])
    wva_in = din("wva", [C, DK])
    woa_in = din("woa", [DK, C])
    wr_in = din("wr", [C, C])
    wvf_in = din("wvf", [C, C])
    wof_in = din("wof", [C, C])
    ws_in = din("ws", [C, C])
    w1_in = din("w1", [C, C])
    w2_in = din("w2", [C, C])
    w3_in = din("w3", [C, C])
    krw_in = din("krw", [2, C, H])
    vrw_in = din("vrw", [2, H, C])
    wsmT_in = din("wsmT", [7, C])
    lnc_in = din("lnc", [P, 32])
    sh3_in = din("sh3", [P, 3])
    idn_in = din("idn", [P, P], BF16)
    gmask_in = din("gmask", [P, 4 * 512], BF16)
    onesp_in = din("onesp", [P, 1], BF16)
    onesr_in = din("onesr", [1, P], BF16)
    kmask_in = din("kmask", [P, GK], BF16)
    vmask_in = din("vmask", [P, GK // P])
    nzero_in = din("nzero", [P, 1])

    y_out = nc.dram_tensor("y", [TCORE, C], F32, kind="ExternalOutput").ap()

    if debug_dump or stop_after is not None:
        dxs_out = nc.dram_tensor("dxs", [TCORE, C], F32, kind="ExternalOutput").ap()
        db_out = [nc.dram_tensor("db%d" % i, [P, 16 * C], F32, kind="ExternalOutput").ap()
                  for i in range(4)]
        dsm_out = nc.dram_tensor("dsm", [P, 176], F32, kind="ExternalOutput").ap()
    send_d = nc.dram_tensor("kvsend", [P, 2 * TCORE], BF16, kind="Internal")
    recv_d = nc.dram_tensor("kvrecv", [NCORES * P, 2 * TCORE], BF16, kind="Internal")

    with ExitStack() as st:
        def sb(name, shape, dt):
            return st.enter_context(nc.sbuf_tensor(name, shape, dt))

        XS = sb("XS", [P, NT * C], F32)
        BIG1 = sb("BIG1", [P, 16 * C], BF16)
        BIG2 = sb("BIG2", [P, 16 * C], BF16)
        BIG3 = sb("BIG3", [P, 16 * C], BF16)
        BIG4 = sb("BIG4", [P, 16 * C], BF16)
        TMP2 = sb("TMP2", [P, 2 * C], BF16)
        SB512 = sb("SB512", [P, 2 * 512], BF16)
        SM = sb("SM", [P, 176], F32)
        SMB = sb("SMB", [P, 64], BF16)
        LNC = sb("LNC", [P, 32], F32)
        SH3 = sb("SH3", [P, 3], F32)
        IDN = sb("IDN", [P, P], BF16)
        GMK = sb("GMK", [P, 4 * 512], BF16)
        ONESP = sb("ONESP", [P, 1], BF16)
        ONESR = sb("ONESR", [1, P], BF16)
        VMK = sb("VMK", [P, GK // P], F32)
        NZERO = sb("NZERO", [P, 1], F32)
        LSF = sb("LSF", [1, 512], F32)

        pb = [st.enter_context(nc.psum_tensor("pb%d" % i, [P, 512], F32)) for i in range(8)]

        TS = em.sem("TS", "tensor")
        SS = em.sem("SS", "scalar")
        VS = em.sem("VS", "vector")
        GS = em.sem("GS", "gpsimd")
        DG = em.sem("DG", None)
        DSY = em.sem("DSY", None)
        CCS = em.sem("CCS", None)

        R_xs = [Res("xs%d" % t) for t in range(NT)]
        R_pb = [Res("pb%d" % i) for i in range(8)]
        R_tmp2 = [Res("tmp2_%d" % i) for i in range(2)]
        R_sb512 = [Res("sb512_%d" % i) for i in range(2)]
        R_smb = Res("smb")
        R_wt7 = Res("wt7")
        R_wtok = Res("wtok")
        R_sstat = Res("sstat")
        R_norm = Res("norm")
        R_send = Res("send")
        R_recv = Res("recv")
        R_lnst = [Res("lnst%d" % i) for i in range(2)]
        R_bid = [Res("bid%d" % i) for i in range(2)]
        R_lt = [Res("lt%d" % i) for i in range(2)]

        region_owner = {"B1": [], "B2": [], "B3": [], "B4a": [], "B4b": []}

        def claim(eng, region, new_res):
            for r in region_owner[region]:
                em.wguard(eng, r)
            region_owner[region] = [new_res]

        def add_owner(region, res):
            if res not in region_owner[region]:
                region_owner[region].append(res)

        SST0 = 0
        WTK0 = 32
        TMPC = 80
        NRM0 = 96
        ONEC = 112
        BID0 = 113
        LT0 = 162

        def T(fn, inc=False):
            return em.do("tensor", fn, TS if inc else None)

        def S(fn):
            return em.do("scalar", fn, SS, 1)

        def V(fn):
            return em.do("vector", fn, VS, 1)

        def Vd(fn):
            r = em.do("vector", fn, VS, 1)
            em.do("vector", lambda e: e.drain())
            return r

        def Sd(fn):
            r = em.do("scalar", fn, SS, 1)
            em.do("scalar", lambda e: e.drain())
            return r

        def G(fn):
            return em.do("gpsimd", fn, GS, 1)

        def GD(fn):
            return em.do("gpsimd", fn, DG, 16)

        def SD(fn):
            return em.do("sync", fn, DSY, 16)

        def xs_ap(t, half=None):
            if half is None:
                return XS[:, t * C:(t + 1) * C]
            return XS[:, t * C + half * 512: t * C + (half + 1) * 512]

        def fm(big, k, lo, n):
            return big[:, k * TCORE + lo: k * TCORE + lo + n]

        SD(lambda e: e.dma_start(out=XS[:, :].rearrange("p (t c) -> p t c", c=C),
                                 in_=x_in.rearrange("(t p) c -> p t c", p=P)))
        for t in range(NT):
            wmark(R_xs[t], DSY)
        SD(lambda e: e.dma_start(out=LNC[:, :], in_=lnc_in))
        SD(lambda e: e.dma_start(out=SH3[:, :], in_=sh3_in))
        SD(lambda e: e.dma_start(out=IDN[:, :], in_=idn_in))
        SD(lambda e: e.dma_start(out=GMK[:, :], in_=gmask_in))
        SD(lambda e: e.dma_start(out=ONESP[:, :], in_=onesp_in))
        SD(lambda e: e.dma_start(out=ONESR[:, :], in_=onesr_in))
        SD(lambda e: e.dma_start(out=VMK[:, :], in_=vmask_in))
        SD(lambda e: e.dma_start(out=NZERO[:, :], in_=nzero_in))
        smalls_dma = DSY.c

        GD(lambda e: e.dma_start(out=TMP2[0:7, 0:C], in_=wsmT_in))
        wmark(R_wt7, DG)
        wmark(R_tmp2[0], DG)

        em.wait("vector", DSY, smalls_dma)
        V(lambda e: e.memset(SM[:, ONEC:ONEC + 1], 1.0))

        em.rguard("tensor", R_wt7)
        em.wait("tensor", DSY, smalls_dma)
        for c_ in range(KCH):
            T(lambda e, c_=c_: e.matmul(pb[6][:, c_ * 8:c_ * 8 + 7],
                                        TMP2[0:7, c_ * P:(c_ + 1) * P],
                                        IDN[0:7, 0:7], start=True, stop=True),
              inc=(c_ == KCH - 1))
        rmark(R_tmp2[0], TS)
        wmark(R_pb[6], TS)
        em.rguard("scalar", R_pb[6])
        for c_ in range(KCH):
            S(lambda e, c_=c_: e.copy(out=SMB[:, c_ * 8:c_ * 8 + 7],
                                      in_=pb[6][:, c_ * 8:c_ * 8 + 7]))
        rmark(R_pb[6], SS)
        wmark(R_smb, SS)

        R_ws = [Res("wsA"), Res("wsB")]

        def stage_w(src_ap, slot):
            off = slot * 8 * C
            r = R_ws[slot]
            claim("gpsimd", "B4a" if slot == 0 else "B4b", r)
            GD(lambda e, off=off, src_ap=src_ap: e.dma_start(
                out=BIG4[:, off:off + 8 * C].rearrange("p (k n) -> p k n", n=C),
                in_=src_ap.rearrange("(k p) n -> p k n", p=P)))
            wmark(r, DG)
            return (lambda k, off=off: BIG4[:, off + k * C: off + (k + 1) * C]), r

        def ln_transpose_tile(t, src_ap, dst_big, dst_res, lnw_off, double_ln,
                              save_stats, plain=False):
            sl = t % 2
            ca = TMPC + sl * 8
            em.wguard("vector", R_lnst[sl])
            V(lambda e, src_ap=src_ap, ca=ca: e.reduce_sum(SM[:, ca:ca + 1], src_ap, axis=AX))
            for hb in range(2):
                em.wguard("scalar", R_pb[6 + hb])
                S(lambda e, src_ap=src_ap, hb=hb: e.square(
                    out=pb[6 + hb][:, :], in_=src_ap[:, hb * 512:(hb + 1) * 512]))
                wmark(R_pb[6 + hb], SS)
            em.rguard("vector", R_pb[6])
            em.rguard("vector", R_pb[7])
            Vd(lambda e, ca=ca: e.reduce_sum(SM[:, ca + 1:ca + 2], pb[6][:, :], axis=AX))
            Vd(lambda e, ca=ca: e.reduce_sum(SM[:, ca + 2:ca + 3], pb[7][:, :], axis=AX))
            rmark(R_pb[6], VS)
            rmark(R_pb[7], VS)
            Vd(lambda e, ca=ca: e.tensor_tensor(SM[:, ca + 1:ca + 2], SM[:, ca + 1:ca + 2],
                                               SM[:, ca + 2:ca + 3], ALU.add))
            Vd(lambda e, ca=ca: e.tensor_scalar_mul(SM[:, ca:ca + 1], SM[:, ca:ca + 1], 1.0 / C))
            Vd(lambda e, ca=ca: e.tensor_scalar_mul(SM[:, ca + 1:ca + 2], SM[:, ca + 1:ca + 2], 1.0 / C))
            Vd(lambda e, ca=ca: e.tensor_tensor(SM[:, ca + 2:ca + 3], SM[:, ca:ca + 1],
                                               SM[:, ca:ca + 1], ALU.mult))
            Vd(lambda e, ca=ca: e.tensor_tensor(SM[:, ca + 1:ca + 2], SM[:, ca + 1:ca + 2],
                                               SM[:, ca + 2:ca + 3], ALU.subtract))
            Vd(lambda e, ca=ca: e.tensor_scalar_add(SM[:, ca + 2:ca + 3], SM[:, ca + 1:ca + 2], EPS))
            Vd(lambda e, ca=ca: e.reciprocal(SM[:, ca + 3:ca + 4], SM[:, ca + 2:ca + 3]))
            em.wait("scalar", VS)
            Sd(lambda e, ca=ca: e.sqrt(out=SM[:, ca + 4:ca + 5], in_=SM[:, ca + 3:ca + 4]))
            em.wait("vector", SS)
            if double_ln:
                Vd(lambda e, ca=ca: e.tensor_tensor(SM[:, ca + 5:ca + 6], SM[:, ca + 1:ca + 2],
                                                   SM[:, ca + 3:ca + 4], ALU.mult))
                Vd(lambda e, ca=ca: e.tensor_scalar_add(SM[:, ca + 5:ca + 6], SM[:, ca + 5:ca + 6], EPS))
                Vd(lambda e, ca=ca: e.reciprocal(SM[:, ca + 6:ca + 7], SM[:, ca + 5:ca + 6]))
                em.wait("scalar", VS)
                Sd(lambda e, ca=ca: e.sqrt(out=SM[:, ca + 5:ca + 6], in_=SM[:, ca + 6:ca + 7]))
                em.wait("vector", SS)
                Vd(lambda e, ca=ca: e.tensor_tensor(SM[:, ca + 6:ca + 7], SM[:, ca + 4:ca + 5],
                                                   SM[:, ca + 5:ca + 6], ALU.mult))
                comb = ca + 6
            else:
                comb = ca + 4
            Vd(lambda e, ca=ca, comb=comb: e.scalar_tensor_tensor(
                out=SM[:, ca + 7:ca + 8], in0=SM[:, ca:ca + 1], scalar=-1.0,
                in1=SM[:, comb:comb + 1], op0=ALU.mult, op1=ALU.mult))
            if save_stats:
                em.wguard("vector", R_sstat) if t == 0 else None
                V(lambda e, ca=ca, t=t: e.tensor_copy(
                    out=SM[:, SST0 + 2 * t:SST0 + 2 * t + 1], in_=SM[:, ca + 4:ca + 5]))
                V(lambda e, ca=ca, t=t: e.scalar_tensor_tensor(
                    out=SM[:, SST0 + 2 * t + 1:SST0 + 2 * t + 2],
                    in0=SM[:, ca:ca + 1], scalar=-1.0, in1=SM[:, ca + 4:ca + 5],
                    op0=ALU.mult, op1=ALU.mult))
                wmark(R_sstat, VS)
            wmark(R_lnst[sl], VS)
            vstat = VS.c
            em.wguard("scalar", R_tmp2[sl])
            em.wait("scalar", VS, vstat)
            S(lambda e, src_ap=src_ap, sl=sl, ca=ca, comb=comb: e.activation(
                out=TMP2[:, sl * C:(sl + 1) * C], in_=src_ap, func=AF.Identity,
                bias=SM[:, ca + 7:ca + 8], scale=SM[:, comb:comb + 1]))
            wmark(R_tmp2[sl], SS)
            rmark(R_lnst[sl], SS)
            for hb in range(2):
                bank = 6 + hb
                em.wguard("tensor", R_pb[bank])
                em.rguard("tensor", R_tmp2[sl])
                em.wait("tensor", DSY, smalls_dma)
                for j in range(4):
                    cc = hb * 4 + j
                    T(lambda e, cc=cc, j=j, bank=bank, sl=sl: e.matmul(
                        pb[bank][:, j * P:(j + 1) * P],
                        TMP2[:, sl * C + cc * P: sl * C + (cc + 1) * P],
                        IDN[:, :], start=True, stop=True), inc=(j == 3))
                wmark(R_pb[bank], TS)
                rmark(R_tmp2[sl], TS)
                em.rguard("scalar", R_pb[bank])
                for j in range(4):
                    cc = hb * 4 + j
                    if plain:
                        S(lambda e, cc=cc, j=j, bank=bank: e.copy(
                            out=fm(dst_big, cc, t * P, P), in_=pb[bank][:, j * P:(j + 1) * P]))
                    else:
                        S(lambda e, cc=cc, j=j, bank=bank: e.activation(
                            out=fm(dst_big, cc, t * P, P), in_=pb[bank][:, j * P:(j + 1) * P],
                            func=AF.Identity, bias=LNC[:, lnw_off + 8 + cc:lnw_off + 9 + cc],
                            scale=LNC[:, lnw_off + cc:lnw_off + 1 + cc]))
                rmark(R_pb[bank], SS)
            wmark(dst_res, SS)

        def ccmm_fm(wslot, in_big, in_res_list, out_big, out_res, kind, other_big=None,
                    banks=None, mcb=None):
            wfn, wres = wslot
            for m in range(KCH):
                if mcb is not None:
                    mcb(m)
                if NTG == 4:
                    bset = [(m % 2) * 4 + tg for tg in range(4)] if banks is None else \
                           [banks[tg % len(banks)] for tg in range(4)]
                else:
                    bset = [(2, 3)[(m * NTG + tg) % 2] for tg in range(NTG)]
                for tg in range(NTG):
                    em.wguard("tensor", R_pb[bset[tg]])
                em.rguard("tensor", wres, *in_res_list)
                for k in range(KCH):
                    for tg in range(NTG):
                        bank = bset[tg]
                        T(lambda e, k=k, m=m, tg=tg, bank=bank: e.matmul(
                            pb[bank][:, :], wfn(k)[:, m * P:(m + 1) * P],
                            fm(in_big, k, tg * 512, 512),
                            start=(k == 0), stop=(k == KCH - 1)),
                          inc=(k == KCH - 1 and tg == NTG - 1))
                rmark(wres, TS)
                for rr in in_res_list:
                    rmark(rr, TS)
                for tg in range(NTG):
                    bank = bset[tg]
                    wmark(R_pb[bank], TS)
                    if kind == "mult":
                        em.rguard("vector", R_pb[bank])
                        V(lambda e, m=m, tg=tg, bank=bank: e.tensor_tensor(
                            out=fm(out_big, m, tg * 512, 512),
                            in0=fm(other_big, m, tg * 512, 512),
                            in1=pb[bank][:, :], op=ALU.mult))
                        rmark(R_pb[bank], VS)
                    else:
                        em.rguard("scalar", R_pb[bank])
                        if kind == "copy":
                            S(lambda e, m=m, tg=tg, bank=bank: e.copy(
                                out=fm(out_big, m, tg * 512, 512), in_=pb[bank][:, :]))
                        else:
                            S(lambda e, m=m, tg=tg, bank=bank: e.activation(
                                out=fm(out_big, m, tg * 512, 512), in_=pb[bank][:, :],
                                func=AF.Sigmoid))
                        rmark(R_pb[bank], SS)
            wmark(out_res, VS if kind == "mult" else SS)

        def ccmm_tok(wslot, in_big, in_res_list, evac, banks=((4, 5), (6, 7))):
            wfn, wres = wslot
            for t in range(NT):
                bpair = banks[t % 2]
                em.wguard("tensor", R_pb[bpair[0]])
                em.wguard("tensor", R_pb[bpair[1]])
                em.rguard("tensor", wres, *in_res_list)
                for k in range(KCH):
                    for half in range(2):
                        T(lambda e, k=k, t=t, half=half, bank=bpair[half]: e.matmul(
                            pb[bank][:, :], fm(in_big, k, t * P, P),
                            wfn(k)[:, half * 512:(half + 1) * 512],
                            start=(k == 0), stop=(k == KCH - 1)),
                          inc=(k == KCH - 1 and half == 1))
                rmark(wres, TS)
                for rr in in_res_list:
                    rmark(rr, TS)
                for half in range(2):
                    wmark(R_pb[bpair[half]], TS)
                    evac(t, half, bpair[half])

        def emit_ffn(second):
            pref = "2" if second else "1"
            R_xlnf = [Res("xlnf%d_%d" % (second, g)) for g in range(NTG)]
            R_stf = Res("stf%d" % second)
            R_sgf = Res("sgf%d" % second)
            R_xf = [Res("xf%d_%d" % (second, t)) for t in range(NT)]
            R_hf = [Res("hf%d_%d" % (second, g)) for g in range(NTG)]
            R_sg2f = Res("sg2f%d" % second)
            R_hrh = Res("hrh%d" % second)
            R_ks = [Res("ks%d_%d" % (second, i)) for i in range(2)]
            R_vst = [Res("vst%d_%d" % (second, i)) for i in range(2)]

            ws_slot = stage_w(ws_in, 0)

            def emit_f2_col(m, tg2):
                bank = (2, 3, 4, 5)[(m + tg2 * KCH) % 4]
                em.wguard("tensor", R_pb[bank])
                em.rguard("tensor", ws_slot[1], R_xlnf[tg2])
                for k in range(KCH):
                    T(lambda e, k=k, m=m, tg2=tg2, bank=bank: e.matmul(
                        pb[bank][:, :], ws_slot[0](k)[:, m * P:(m + 1) * P],
                        fm(BIG1, k, tg2 * 512, 512),
                        start=(k == 0), stop=(k == KCH - 1)), inc=(k == KCH - 1))
                wmark(R_pb[bank], TS)
                rmark(ws_slot[1], TS)
                rmark(R_xlnf[tg2], TS)
                em.rguard("scalar", R_pb[bank])
                S(lambda e, m=m, tg2=tg2, bank=bank: e.copy(
                    out=fm(BIG2, m, tg2 * 512, 512), in_=pb[bank][:, :]))
                rmark(R_pb[bank], SS)

            colq = []
            for tg in range(NTG):
                for t in range(tg * 4, tg * 4 + 4):
                    em.rguard("vector", R_xs[t])
                    em.rguard("scalar", R_xs[t])
                    if t == 0:
                        claim("scalar", "B1", R_xlnf[0])
                        for g2 in range(1, NTG):
                            add_owner("B1", R_xlnf[g2])
                        claim("scalar", "B2", R_stf)
                    ln_transpose_tile(t, xs_ap(t), BIG1, R_xlnf[tg],
                                      lnw_off=0, double_ln=True, save_stats=True)
                    rmark(R_xs[t], SS)
                    rmark(R_xs[t], VS)
                    for _ in range(2):
                        if colq:
                            emit_f2_col(*colq.pop(0))
                colq += [(m, tg) for m in range(KCH)]
            while colq:
                emit_f2_col(*colq.pop(0))
            wmark(R_stf, SS)
            ck(pref + "f2")

            wr_slot = stage_w(wr_in, 1)
            claim("scalar", "B3", R_sgf)
            ccmm_fm(wr_slot, BIG1, R_xlnf, BIG3, R_sgf, "sigmoid")
            ck(pref + "f3")
            wv_slot = stage_w(wvf_in, 0)
            ccmm_fm(wv_slot, BIG1, R_xlnf, BIG3, R_sgf, "mult", other_big=BIG3)
            ck(pref + "f4")
            wo_slot = stage_w(wof_in, 1)

            def evac_f5(t, half, bank):
                if t == 0 and half == 0:
                    claim("scalar", "B1", R_xf[0])
                    for t2 in range(1, NT):
                        add_owner("B1", R_xf[t2])
                em.rguard("scalar", R_sstat)
                em.rguard("scalar", R_xs[t])
                S(lambda e, t=t, half=half: e.activation(
                    out=BIG1[:, t * C + half * 512: t * C + (half + 1) * 512],
                    in_=xs_ap(t, half), func=AF.Identity,
                    bias=SM[:, SST0 + 2 * t + 1:SST0 + 2 * t + 2],
                    scale=SM[:, SST0 + 2 * t:SST0 + 2 * t + 1]))
                rmark(R_sstat, SS)
                rmark(R_xs[t], SS)
                em.wait("vector", SS)
                em.rguard("vector", R_pb[bank])
                V(lambda e, t=t, half=half, bank=bank: e.tensor_tensor(
                    out=BIG1[:, t * C + half * 512: t * C + (half + 1) * 512],
                    in0=BIG1[:, t * C + half * 512: t * C + (half + 1) * 512],
                    in1=pb[bank][:, :], op=ALU.add))
                rmark(R_pb[bank], VS)
                wmark(R_xf[t], VS)
                em.wguard("vector", R_xs[t])
                V(lambda e, t=t, half=half: e.scalar_tensor_tensor(
                    out=xs_ap(t, half),
                    in0=BIG1[:, t * C + half * 512: t * C + (half + 1) * 512],
                    scalar=0.5, in1=xs_ap(t, half), op0=ALU.mult, op1=ALU.add))
                rmark(R_xf[t], VS)
                wmark(R_xs[t], VS)

            ccmm_tok(wo_slot, BIG3, [R_sgf], evac_f5)
            ck(pref + "f5")
            w2_slot = stage_w(w2_in, 0)

            claim("scalar", "B1", R_sg2f)
            f7_cols = []

            def emit_f7_col(m):
                bset = [(2, 3, 4, 5)[(m * 2 + i) % 4] for i in range(NTG)] if NTG == 4 \
                    else [(2, 3)[(m * NTG + tg_) % 2] for tg_ in range(NTG)]
                for tg_ in range(NTG):
                    em.wguard("tensor", R_pb[bset[tg_]])
                em.rguard("tensor", w2_slot[1], R_stf)
                for k in range(KCH):
                    for tg_ in range(NTG):
                        T(lambda e, k=k, m=m, tg_=tg_, bank=bset[tg_]: e.matmul(
                            pb[bank][:, :], w2_slot[0](k)[:, m * P:(m + 1) * P],
                            fm(BIG2, k, tg_ * 512, 512),
                            start=(k == 0), stop=(k == KCH - 1)),
                          inc=(k == KCH - 1 and tg_ == NTG - 1))
                rmark(w2_slot[1], TS)
                rmark(R_stf, TS)
                for tx in range((m * TCORE) // C, ((m + 1) * TCORE + C - 1) // C):
                    em.wguard("scalar", R_xf[tx])
                for tg_ in range(NTG):
                    bank = bset[tg_]
                    wmark(R_pb[bank], TS)
                    em.rguard("scalar", R_pb[bank])
                    S(lambda e, m=m, tg_=tg_, bank=bank: e.activation(
                        out=fm(BIG1, m, tg_ * 512, 512), in_=pb[bank][:, :],
                        func=AF.Sigmoid))
                    rmark(R_pb[bank], SS)

            mque = list(range(KCH))
            for tg in range(NTG):
                for t in range(tg * 4, tg * 4 + 4):
                    em.rguard("vector", R_xf[t])
                    em.rguard("scalar", R_xf[t])
                    if t == 0:
                        claim("scalar", "B3", R_hf[0])
                        for g2 in range(1, NTG):
                            add_owner("B3", R_hf[g2])
                    ln_transpose_tile(t, BIG1[:, t * C:(t + 1) * C], BIG3, R_hf[tg],
                                      lnw_off=16, double_ln=False, save_stats=False)
                    rmark(R_xf[t], SS)
                    rmark(R_xf[t], VS)
                    if t % 2 == 1 and mque:
                        emit_f7_col(mque.pop(0))
            while mque:
                emit_f7_col(mque.pop(0))
            wmark(R_sg2f, SS)
            ck(pref + "f6")
            ck(pref + "f7")
            w1_slot = stage_w(w1_in, 1)
            ccmm_fm(w1_slot, BIG3, R_hf, BIG1, R_sg2f, "mult", other_big=BIG1)
            ck(pref + "f8")

            for t in range(NT):
                bank = 6 + (t % 2)
                par = t % 2
                em.wguard("tensor", R_pb[bank])
                em.rguard("tensor", R_smb, R_hf[t // 4])
                for k in range(KCH):
                    T(lambda e, k=k, t=t, bank=bank: e.matmul(
                        pb[bank][:, 0:7], fm(BIG3, k, t * P, P),
                        SMB[:, k * 8:k * 8 + 7],
                        start=(k == 0), stop=(k == KCH - 1)), inc=(k == KCH - 1))
                wmark(R_pb[bank], TS)
                rmark(R_smb, TS)
                rmark(R_hf[t // 4], TS)
                b0 = BID0 + par * 24
                em.rguard("scalar", R_pb[bank])
                em.wguard("scalar", R_bid[par])
                S(lambda e, bank=bank, b0=b0: e.activation(out=SM[:, b0:b0 + 4],
                                                           in_=pb[bank][:, 0:4], func=AF.Sigmoid))
                wmark(R_bid[par], SS)
                em.wait("vector", SS)
                em.wait("vector", DSY, smalls_dma)
                Vd(lambda e, b0=b0: e.tensor_single_scalar(SM[:, b0 + 4:b0 + 7], SH3[:, :],
                                                          SM[:, b0 + 3:b0 + 4], ALU.mult))
                Vd(lambda e, b0=b0: e.tensor_tensor(SM[:, b0 + 7:b0 + 10], SM[:, b0:b0 + 3],
                                                   SM[:, b0 + 4:b0 + 7], ALU.mult))
                em.rguard("vector", R_pb[bank])
                Vd(lambda e, bank=bank, b0=b0: e.tensor_tensor(
                    SM[:, b0 + 7:b0 + 10], SM[:, b0 + 7:b0 + 10], pb[bank][:, 4:7], ALU.add))
                rmark(R_pb[bank], VS)
                Vd(lambda e, b0=b0: e.reduce_max(SM[:, b0 + 10:b0 + 11], SM[:, b0 + 7:b0 + 10], axis=AX))
                Vd(lambda e, b0=b0: e.tensor_single_scalar(SM[:, b0 + 11:b0 + 14],
                                                          SM[:, b0 + 7:b0 + 10],
                                                          SM[:, b0 + 10:b0 + 11], ALU.is_ge))
                Vd(lambda e, b0=b0: e.tensor_copy(out=SM[:, b0 + 14:b0 + 15], in_=SM[:, b0 + 11:b0 + 12]))
                Vd(lambda e, b0=b0: e.tensor_tensor(SM[:, b0 + 17:b0 + 18], SM[:, b0 + 12:b0 + 13],
                                                   SM[:, b0 + 11:b0 + 12], ALU.mult))
                Vd(lambda e, b0=b0: e.tensor_tensor(SM[:, b0 + 15:b0 + 16], SM[:, b0 + 12:b0 + 13],
                                                   SM[:, b0 + 17:b0 + 18], ALU.subtract))
                Vd(lambda e, b0=b0: e.tensor_tensor(SM[:, b0 + 17:b0 + 18], SM[:, b0 + 14:b0 + 15],
                                                   SM[:, b0 + 15:b0 + 16], ALU.add))
                Vd(lambda e, b0=b0: e.scalar_tensor_tensor(
                    out=SM[:, b0 + 16:b0 + 17], in0=SM[:, b0 + 17:b0 + 18], scalar=-1.0,
                    in1=SM[:, ONEC:ONEC + 1], op0=ALU.mult, op1=ALU.add))
                Vd(lambda e, b0=b0: e.tensor_tensor(SM[:, b0 + 17:b0 + 20], SM[:, b0 + 14:b0 + 17],
                                                   SM[:, b0:b0 + 3], ALU.mult))
                Vd(lambda e, b0=b0: e.reduce_sum(SM[:, b0 + 20:b0 + 21], SM[:, b0 + 17:b0 + 20], axis=AX))
                Vd(lambda e, b0=b0: e.tensor_scalar_add(SM[:, b0 + 21:b0 + 22], SM[:, b0 + 20:b0 + 21], 1e-6))
                Vd(lambda e, b0=b0: e.reciprocal(SM[:, b0 + 22:b0 + 23], SM[:, b0 + 21:b0 + 22]))
                Vd(lambda e, b0=b0: e.tensor_tensor(SM[:, b0 + 23:b0 + 24], SM[:, b0 + 20:b0 + 21],
                                                   SM[:, b0 + 22:b0 + 23], ALU.mult))
                Vd(lambda e, b0=b0: e.tensor_scalar_mul(SM[:, b0 + 23:b0 + 24], SM[:, b0 + 23:b0 + 24], 0.5))
                Vd(lambda e, t=t, b0=b0: e.tensor_single_scalar(
                    SM[:, WTK0 + 3 * t:WTK0 + 3 * t + 3], SM[:, b0 + 14:b0 + 17],
                    SM[:, b0 + 23:b0 + 24], ALU.mult))
                rmark(R_bid[par], VS)
            wmark(R_wtok, VS)
            ck(pref + "f9")

            w3_slot = stage_w(w3_in, 0)

            def evac_f10(t, half, bank):
                em.rguard("vector", R_pb[bank], R_wtok)
                em.wguard("vector", R_xs[t])
                V(lambda e, t=t, half=half, bank=bank: e.scalar_tensor_tensor(
                    out=xs_ap(t, half), in0=pb[bank][:, :],
                    scalar=SM[:, WTK0 + 3 * t + 2:WTK0 + 3 * t + 3],
                    in1=xs_ap(t, half), op0=ALU.mult, op1=ALU.add))
                rmark(R_pb[bank], VS)
                rmark(R_wtok, VS)
                wmark(R_xs[t], VS)

            ccmm_tok(w3_slot, BIG1, [R_sg2f], evac_f10)
            ck(pref + "f10")

            NQ = 4
            for e_ in range(2):
                for q4 in range(NQ):
                    qi = e_ * NQ + q4
                    ks = R_ks[qi % 2]
                    koff = (qi % 2) * 8 * C
                    if qi == 0 and not second:
                        claim("gpsimd", "B2", R_ks[0])
                        add_owner("B2", R_ks[1])
                    elif qi == 0:
                        claim("gpsimd", "B2", R_ks[0])
                        add_owner("B2", R_ks[1])
                    else:
                        em.wguard("gpsimd", ks)
                    GD(lambda e, e_=e_, q4=q4, koff=koff: e.dma_start(
                        out=BIG2[:, koff:koff + 8 * HQ].rearrange("p (k n) -> p k n", n=HQ),
                        in_=krw_in[e_][:, q4 * HQ:(q4 + 1) * HQ]
                        .rearrange("(k p) n -> p k n", p=P)))
                    wmark(ks, DG)
                    vs_ = R_vst[qi % 2]
                    voff = (qi % 2) * 8 * C
                    if qi == 0:
                        claim("gpsimd", "B1", R_vst[0])
                        add_owner("B1", R_vst[1])
                    else:
                        em.wguard("gpsimd", vs_)
                    GD(lambda e, e_=e_, q4=q4, voff=voff: e.dma_start(
                        out=BIG1[:, voff:voff + 8 * C].rearrange("p (m c) -> p m c", c=C),
                        in_=vrw_in[e_][q4 * HQ:(q4 + 1) * HQ, :].rearrange(
                            "(m p) c -> p m c", p=P)))
                    wmark(vs_, DG)
                    if qi == 0:
                        claim("vector", "B4a", R_hrh)
                        claim("vector", "B4b", R_hrh)
                    else:
                        em.wguard("vector", R_hrh)
                    for hml in range(NHMQ):
                        if NTG == 4:
                            bset = [(hml % 2) * 4 + tg for tg in range(4)]
                        else:
                            bset = [(0, 1, 2, 3)[(hml * NTG + tg) % 4] for tg in range(NTG)]
                        for tg in range(NTG):
                            em.wguard("tensor", R_pb[bset[tg]])
                        em.rguard("tensor", ks)
                        for tg in range(NTG):
                            em.rguard("tensor", R_hf[tg])
                        for k in range(KCH):
                            for tg in range(NTG):
                                T(lambda e, k=k, hml=hml, tg=tg, bank=bset[tg], koff=koff: e.matmul(
                                    pb[bank][:, :],
                                    BIG2[:, koff + k * HQ + hml * P: koff + k * HQ + (hml + 1) * P],
                                    fm(BIG3, k, tg * 512, 512),
                                    start=(k == 0), stop=(k == KCH - 1)),
                                  inc=(k == KCH - 1 and tg == NTG - 1))
                        rmark(ks, TS)
                        for tg in range(NTG):
                            rmark(R_hf[tg], TS)
                        for tg in range(NTG):
                            bank = bset[tg]
                            wmark(R_pb[bank], TS)
                            sl = (hml * NTG + tg) % 2
                            em.wguard("scalar", R_sb512[sl])
                            em.rguard("scalar", R_pb[bank])
                            S(lambda e, bank=bank, sl=sl: e.activation(
                                out=SB512[:, sl * 512:(sl + 1) * 512], in_=pb[bank][:, :],
                                func=AF.Relu))
                            rmark(R_pb[bank], SS)
                            wmark(R_sb512[sl], SS)
                            em.rguard("vector", R_sb512[sl])
                            V(lambda e, hml=hml, tg=tg, sl=sl: e.tensor_tensor(
                                out=BIG4[:, hml * TCORE + tg * 512: hml * TCORE + (tg + 1) * 512],
                                in0=SB512[:, sl * 512:(sl + 1) * 512],
                                in1=SB512[:, sl * 512:(sl + 1) * 512], op=ALU.mult))
                            rmark(R_sb512[sl], VS)
                    wmark(R_hrh, VS)
                    for t in range(NT):
                        bpair = ((4, 5), (6, 7))[t % 2]
                        em.wguard("tensor", R_pb[bpair[0]])
                        em.wguard("tensor", R_pb[bpair[1]])
                        em.rguard("tensor", R_hrh, vs_)
                        for hm in range(NHMQ):
                            for ch in range(2):
                                T(lambda e, hm=hm, t=t, ch=ch, bank=bpair[ch], voff=voff: e.matmul(
                                    pb[bank][:, :],
                                    BIG4[:, hm * TCORE + t * P: hm * TCORE + (t + 1) * P],
                                    BIG1[:, voff + hm * C + ch * 512: voff + hm * C + (ch + 1) * 512],
                                    start=(hm == 0), stop=(hm == NHMQ - 1)),
                                  inc=(hm == NHMQ - 1 and ch == 1))
                        rmark(R_hrh, TS)
                        rmark(vs_, TS)
                        for ch in range(2):
                            bank = bpair[ch]
                            wmark(R_pb[bank], TS)
                            em.rguard("vector", R_pb[bank], R_wtok)
                            em.wguard("vector", R_xs[t])
                            V(lambda e, t=t, ch=ch, bank=bank, e_=e_: e.scalar_tensor_tensor(
                                out=xs_ap(t, ch), in0=pb[bank][:, :],
                                scalar=SM[:, WTK0 + 3 * t + e_:WTK0 + 3 * t + e_ + 1],
                                in1=xs_ap(t, ch), op0=ALU.mult, op1=ALU.add))
                            rmark(R_pb[bank], VS)
                            rmark(R_wtok, VS)
                            wmark(R_xs[t], VS)
                            if second and e_ == 1 and q4 == NQ - 1:
                                em.wait("sync", VS)
                                SD(lambda e, t=t, ch=ch: e.dma_start(
                                    out=y_out[t * P:(t + 1) * P, ch * 512:(ch + 1) * 512],
                                    in_=xs_ap(t, ch)))
                                rmark(R_xs[t], DSY)
            ck(pref + "f11")

        def emit_attention():
            R_y1f = [Res("y1f%d" % g) for g in range(NTG)]
            R_qf = Res("qf")
            R_ko = Res("ko")
            R_vo = Res("vo")
            R_ka = Res("ka")
            R_va = Res("va")
            R_attw = Res("attw")
            R_kmk = [Res("kmk%d" % i) for i in range(2)]
            R_osb = [Res("osb%d" % i) for i in range(2)]
            R_ptb = [Res("ptb%d" % i) for i in range(3)]
            R_lsb = [Res("lsb"), ]
            R_pbf = R_sb512

            AW0 = 3 * TCORE
            KMS0 = AW0 + 4096
            claim("gpsimd", "B3", R_attw)
            GD(lambda e: e.dma_start(
                out=BIG3[:, AW0:AW0 + 1024].rearrange("p (k n) -> p k n", n=DK),
                in_=wq_in.rearrange("(k p) n -> p k n", p=P)))
            GD(lambda e: e.dma_start(
                out=BIG3[:, AW0 + 1024:AW0 + 2048].rearrange("p (k n) -> p k n", n=DK),
                in_=wk_in.rearrange("(k p) n -> p k n", p=P)))
            GD(lambda e: e.dma_start(
                out=BIG3[:, AW0 + 2048:AW0 + 3072].rearrange("p (k n) -> p k n", n=DK),
                in_=wva_in.rearrange("(k p) n -> p k n", p=P)))
            GD(lambda e: e.dma_start(out=BIG3[:, AW0 + 3072:AW0 + 4096], in_=woa_in))
            wmark(R_attw, DG)

            def wq_k(k):
                return BIG3[:, AW0 + k * DK:AW0 + (k + 1) * DK]

            def wk_k(k):
                return BIG3[:, AW0 + 1024 + k * DK:AW0 + 1024 + (k + 1) * DK]

            def wv_k(k):
                return BIG3[:, AW0 + 2048 + k * DK:AW0 + 2048 + (k + 1) * DK]

            WOap = BIG3[:, AW0 + 3072:AW0 + 4096]
            QF0, KO0, VO0 = 0, TCORE, 2 * TCORE

            def emit_qk_col(which, tg2):
                dst0, wfn, scale, res = ((QF0, wq_k, INV_SQRT_DK, R_qf) if which == 0
                                         else (KO0, wk_k, 1.0, R_ko))
                bank = (2, 3)[(which + tg2) % 2]
                em.wguard("tensor", R_pb[bank])
                em.rguard("tensor", R_attw, R_y1f[tg2])
                for k in range(KCH):
                    T(lambda e, k=k, tg2=tg2, bank=bank, wfn=wfn: e.matmul(
                        pb[bank][:, :], wfn(k), fm(BIG1, k, tg2 * 512, 512),
                        start=(k == 0), stop=(k == KCH - 1)), inc=(k == KCH - 1))
                wmark(R_pb[bank], TS)
                rmark(R_attw, TS)
                rmark(R_y1f[tg2], TS)
                add_owner("B3", res)
                em.rguard("scalar", R_pb[bank])
                S(lambda e, dst0=dst0, tg2=tg2, bank=bank, scale=scale: e.activation(
                    out=BIG3[:, dst0 + tg2 * 512: dst0 + (tg2 + 1) * 512],
                    in_=pb[bank][:, :], func=AF.Copy, scale=scale))
                rmark(R_pb[bank], SS)
                if tg2 == NTG - 1:
                    wmark(res, SS)
                    norm_scan(BIG3, dst0, NRM0 if which == 0 else NRM0 + 1,
                              TCORE // 512, res)

            def emit_vo_tile(t2):
                bank = (4, 5)[t2 % 2]
                em.wguard("tensor", R_pb[bank])
                em.rguard("tensor", R_attw, R_y1f[t2 // 4])
                for k in range(KCH):
                    T(lambda e, k=k, t2=t2, bank=bank: e.matmul(
                        pb[bank][:, 0:P], fm(BIG1, k, t2 * P, P), wv_k(k),
                        start=(k == 0), stop=(k == KCH - 1)), inc=(k == KCH - 1))
                wmark(R_pb[bank], TS)
                rmark(R_attw, TS)
                rmark(R_y1f[t2 // 4], TS)
                add_owner("B3", R_vo)
                em.rguard("scalar", R_pb[bank])
                S(lambda e, t2=t2, bank=bank: e.copy(
                    out=BIG3[:, VO0 + t2 * P: VO0 + (t2 + 1) * P], in_=pb[bank][:, 0:P]))
                rmark(R_pb[bank], SS)

            nq, nk = NRM0, NRM0 + 1
            em.wait("vector", DSY, smalls_dma)
            V(lambda e: e.memset(SM[0:1, nq:nq + 2], 0.0))
            em.wait("tensor", DSY, smalls_dma)

            def norm_scan(big, src_off, nmax_col, nchunks, res_r):
                for ch_ in range(nchunks):
                    sl = ch_ % 2
                    em.rguard("vector", res_r)
                    em.wguard("vector", R_pbf[sl])
                    V(lambda e, big=big, ch_=ch_, src_off=src_off, sl=sl: e.tensor_tensor(
                        out=SB512[:, sl * 512:(sl + 1) * 512],
                        in0=big[:, src_off + ch_ * 512: src_off + (ch_ + 1) * 512],
                        in1=big[:, src_off + ch_ * 512: src_off + (ch_ + 1) * 512], op=ALU.mult))
                    wmark(R_pbf[sl], VS)
                    em.wguard("tensor", R_pb[7])
                    em.rguard("tensor", R_pbf[sl])
                    T(lambda e, sl=sl: e.matmul(pb[7][0:1, :], ONESP[:, :],
                                                SB512[:, sl * 512:(sl + 1) * 512],
                                                start=True, stop=True,
                                                skip_group_check=True), inc=True)
                    wmark(R_pb[7], TS)
                    rmark(R_pbf[sl], TS)
                    em.rguard("vector", R_pb[7])
                    Vd(lambda e: e.reduce_max(SM[0:1, NRM0 + 2:NRM0 + 3], pb[7][0:1, :], axis=AX))
                    Vd(lambda e, nmax_col=nmax_col: e.tensor_tensor(
                        SM[0:1, nmax_col:nmax_col + 1], SM[0:1, nmax_col:nmax_col + 1],
                        SM[0:1, NRM0 + 2:NRM0 + 3], ALU.max))
                    rmark(R_pb[7], VS)

            a2q = []
            for tg in range(NTG):
                for t in range(tg * 4, tg * 4 + 4):
                    em.rguard("vector", R_xs[t])
                    em.rguard("scalar", R_xs[t])
                    if t == 0:
                        claim("scalar", "B1", R_y1f[0])
                        for g2 in range(1, NTG):
                            add_owner("B1", R_y1f[g2])
                    ln_transpose_tile(t, xs_ap(t), BIG1, R_y1f[tg],
                                      lnw_off=0, double_ln=False, save_stats=False, plain=True)
                    rmark(R_xs[t], SS)
                    rmark(R_xs[t], VS)
                    for _ in range(2):
                        if a2q:
                            a2q.pop(0)()
                a2q += [lambda tg=tg, w=w: emit_qk_col(w, tg) for w in (0, 1)]
                a2q += [lambda t2=t2: emit_vo_tile(t2) for t2 in range(tg * 4, tg * 4 + 4)]
            while a2q:
                a2q.pop(0)()
            em.rguard("gpsimd", R_ko, R_vo)
            GD(lambda e: e.dma_start(out=send_d.ap()[:, 0:TCORE], in_=BIG3[:, KO0:KO0 + TCORE]))
            GD(lambda e: e.dma_start(out=send_d.ap()[:, TCORE:2 * TCORE],
                                     in_=BIG3[:, VO0:VO0 + TCORE]))
            rmark(R_ko, DG)
            rmark(R_vo, DG)
            wmark(R_send, DG)
            em.wait("gpsimd", DG)
            em.do("gpsimd", lambda e: e.collective_compute(
                "AllGather", ALU.bypass, replica_groups=[list(range(NCORES))],
                ins=[send_d.ap()], outs=[recv_d.ap()]), CCS, 1)
            wmark(R_recv, CCS)
            em.wait("gpsimd", CCS)
            claim("gpsimd", "B2", R_ka)
            rv = recv_d.ap().rearrange("(r p) m -> p r m", p=P)
            GD(lambda e: e.dma_start(
                out=BIG2[:, 0:GK].rearrange("p (r n) -> p r n", n=TCORE),
                in_=rv[:, :, 0:TCORE]))
            wmark(R_ka, DG)
            claim("gpsimd", "B4a", R_va)
            claim("gpsimd", "B4b", R_va)
            GD(lambda e: e.dma_start(
                out=BIG4[:, 0:GK].rearrange("p (r n) -> p r n", n=TCORE),
                in_=rv[:, :, TCORE:2 * TCORE]))
            wmark(R_va, DG)
            em.rguard("vector", R_ka)
            for i in range(GK // 2048):
                ksl = KMS0 + (i % 2) * 2048
                em.wguard("gpsimd", R_kmk[i % 2])
                GD(lambda e, i=i, ksl=ksl: e.dma_start(
                    out=BIG3[:, ksl:ksl + 2048], in_=kmask_in[:, i * 2048:(i + 1) * 2048]))
                wmark(R_kmk[i % 2], DG)
                em.rguard("vector", R_kmk[i % 2])
                V(lambda e, i=i, ksl=ksl: e.tensor_tensor(
                    out=BIG2[:, i * 2048:(i + 1) * 2048],
                    in0=BIG2[:, i * 2048:(i + 1) * 2048],
                    in1=BIG3[:, ksl:ksl + 2048], op=ALU.mult))
                rmark(R_kmk[i % 2], VS)
            wmark(R_ka, VS)
            em.rguard("vector", R_va)
            em.wait("vector", DSY, smalls_dma)
            for j in range(GK // P):
                V(lambda e, j=j: e.tensor_single_scalar(
                    BIG4[:, j * P:(j + 1) * P],
                    BIG4[:, j * P:(j + 1) * P],
                    VMK[:, j:j + 1], ALU.mult))
            wmark(R_va, VS)
            ck("a5")

            norm_scan(BIG2, 0, nk, GK // 512, R_ka)
            Vd(lambda e: e.tensor_tensor(SM[0:1, NRM0 + 2:NRM0 + 3], SM[0:1, nq:nq + 1],
                                        SM[0:1, nk:nk + 1], ALU.mult))
            em.wait("scalar", VS)
            Sd(lambda e: e.sqrt(out=SM[0:1, NRM0 + 3:NRM0 + 4], in_=SM[0:1, NRM0 + 2:NRM0 + 3]))
            em.wguard("scalar", R_smb)
            Sd(lambda e: e.copy(out=SMB[0:1, 63:64], in_=SM[0:1, NRM0 + 3:NRM0 + 4]))
            wmark(R_smb, SS)
            em.wait("tensor", SS)
            em.wguard("tensor", R_pb[7])
            T(lambda e: e.matmul(pb[7][:, 0:1], ONESR[0:1, :], SMB[0:1, 63:64],
                                 start=True, stop=True, skip_group_check=True), inc=True)
            wmark(R_pb[7], TS)
            em.rguard("vector", R_pb[7])
            Vd(lambda e: e.tensor_scalar_mul(SM[:, NRM0 + 5:NRM0 + 6], pb[7][:, 0:1], -1.0))
            rmark(R_pb[7], VS)
            negm_c = NRM0 + 5
            em.wait("scalar", VS)
            Sd(lambda e: e.activation(out=SM[:, NRM0 + 6:NRM0 + 7], in_=SM[:, negm_c:negm_c + 1],
                                     func=AF.Exp))
            em.wguard("scalar", R_smb)
            Sd(lambda e: e.copy(out=SMB[0:1, 62:63], in_=SM[0:1, NRM0 + 6:NRM0 + 7]))
            Sd(lambda e: e.copy(out=SM[0:1, NRM0 + 8:NRM0 + 9], in_=SMB[0:1, 62:63]))
            wmark(R_smb, SS)
            em.wait("tensor", SS)
            em.wguard("tensor", R_pb[1])
            T(lambda e: e.matmul(pb[1][:, 0:1], ONESR[0:1, :], SMB[0:1, 62:63],
                                 start=True, stop=True, skip_group_check=True), inc=True)
            wmark(R_pb[1], TS)
            em.rguard("vector", R_pb[1])
            Vd(lambda e: e.tensor_copy(out=SM[:, NRM0 + 6:NRM0 + 7], in_=pb[1][:, 0:1]))
            rmark(R_pb[1], VS)
            em.wait("vector", SS)
            em.wait("vector", DSY, smalls_dma)
            Vd(lambda e: e.tensor_tensor(SM[:, NRM0 + 7:NRM0 + 8], SM[:, NRM0 + 6:NRM0 + 7],
                                        NZERO[:, :], ALU.mult))
            wmark(R_norm, VS)
            ck("a3")

            NSC = GK // P
            for g in range(NTG):
                par = g % 2
                obank = 6 + par
                lbank = 0 + par
                qlo = g * 512
                em.wguard("tensor", R_pb[obank])
                em.wguard("tensor", R_pb[lbank])
                state = {"first": True, "idx": 0, "pend": []}

                def ptb_ap(sl, n=512):
                    if sl < 2:
                        return TMP2[:, sl * 512:sl * 512 + n]
                    return SB512[:, 0:n]

                def st_sub(src_big, koff, vtile, masked_ki, last):
                    idx = state["idx"]
                    state["idx"] += 1
                    sbank = (2, 3, 4)[idx % 3]
                    sl = idx % 3
                    em.wguard("tensor", R_pb[sbank])
                    em.rguard("tensor", R_qf, R_ka if src_big is BIG2 else R_ko)
                    T(lambda e, src_big=src_big, koff=koff, sbank=sbank, qlo=qlo: e.matmul(
                        pb[sbank][:, :], src_big[:, koff:koff + P],
                        BIG3[:, QF0 + qlo: QF0 + qlo + 512], start=True, stop=True,
                        skip_group_check=True), inc=True)
                    wmark(R_pb[sbank], TS)
                    rmark(R_qf, TS)
                    rmark(R_ka if src_big is BIG2 else R_ko, TS)
                    em.rguard("scalar", R_pb[sbank], R_norm)
                    em.wguard("scalar", R_ptb[sl])
                    S(lambda e, sbank=sbank, sl=sl: e.activation(
                        out=ptb_ap(sl), in_=pb[sbank][:, :],
                        func=AF.Exp, bias=SM[:, negm_c:negm_c + 1]))
                    rmark(R_pb[sbank], SS)
                    rmark(R_norm, SS)
                    wmark(R_ptb[sl], SS)
                    if masked_ki is not None:
                        em.rguard("vector", R_ptb[sl])
                        em.wait("vector", DSY, smalls_dma)
                        V(lambda e, sl=sl, ki=masked_ki: e.tensor_tensor(
                            out=ptb_ap(sl), in0=ptb_ap(sl),
                            in1=GMK[:, ki * 512:(ki + 1) * 512], op=ALU.mult))
                        wmark(R_ptb[sl], VS)
                    state["pend"].append((sl, vtile, last))
                    if len(state["pend"]) >= 3 or last:
                        flush()

                def flush():
                    while state["pend"]:
                        if len(state["pend"]) < 3 and not state["pend"][0][2]:
                            return
                        sl, vtile, last = state["pend"].pop(0)
                        first = state["first"]
                        state["first"] = False
                        em.rguard("tensor", R_ptb[sl])
                        em.wait("tensor", DSY, smalls_dma)
                        T(lambda e, sl=sl, lbank=lbank, first=first, last=last: e.matmul(
                            pb[lbank][0:1, :], ONESP[:, :], ptb_ap(sl),
                            start=first, stop=last, skip_group_check=True), inc=False)
                        em.rguard("tensor", R_va if vtile[0] is BIG4 else R_vo)
                        T(lambda e, sl=sl, vtile=vtile, obank=obank, first=first, last=last: e.matmul(
                            pb[obank][:, :], vtile[0][:, vtile[1]:vtile[1] + P],
                            ptb_ap(sl),
                            start=first, stop=last, skip_group_check=True), inc=last)
                        rmark(R_ptb[sl], TS)
                        rmark(R_va if vtile[0] is BIG4 else R_vo, TS)

                for sc in range(NSC):
                    st_sub(BIG2, sc * P, (BIG4, sc * P), None, False)
                for sc in range(4 * g):
                    st_sub(BIG3, KO0 + sc * P, (BIG3, VO0 + sc * P), None, False)
                for ki in range(4):
                    st_sub(BIG3, KO0 + qlo + ki * P, (BIG3, VO0 + qlo + ki * P), ki,
                           ki == 3)
                wmark(R_pb[obank], TS)
                wmark(R_pb[lbank], TS)
                em.rguard("scalar", R_pb[lbank])
                em.wguard("scalar", R_lsb[0])
                S(lambda e, lbank=lbank: e.copy(
                    out=LSF[0:1, :], in_=pb[lbank][0:1, :]))
                wmark(R_lsb[0], SS)
                rmark(R_pb[lbank], SS)
                for tq in range(4):
                    t = g * 4 + tq
                    tpar = t % 2
                    lcol = LT0 + tpar
                    em.wait("tensor", SS)
                    em.wguard("tensor", R_pb[4 + tq % 2])
                    em.rguard("tensor", R_lsb[0])
                    T(lambda e, tq=tq, bank=4 + tq % 2: e.matmul(
                        pb[bank][:, 0:1],
                        LSF[0:1, tq * P:(tq + 1) * P],
                        SM[0:1, ONEC:ONEC + 1],
                        start=True, stop=True, skip_group_check=True), inc=True)
                    wmark(R_pb[4 + tq % 2], TS)
                    rmark(R_lsb[0], TS)
                    em.rguard("vector", R_pb[4 + tq % 2], R_norm)
                    em.wguard("vector", R_lt[tpar])
                    Vd(lambda e, bank=4 + tq % 2, lcol=lcol: e.tensor_tensor(
                        SM[:, lcol:lcol + 1], pb[bank][:, 0:1],
                        SM[:, NRM0 + 7:NRM0 + 8], ALU.subtract))
                    rmark(R_pb[4 + tq % 2], VS)
                    rmark(R_norm, VS)
                    Vd(lambda e, lcol=lcol, tpar=tpar: e.reciprocal(
                        SM[:, LT0 + 4 + tpar:LT0 + 5 + tpar], SM[:, lcol:lcol + 1]))
                    wmark(R_lt[tpar], VS)
                em.rguard("scalar", R_pb[obank])
                em.wguard("scalar", R_osb[par])
                S(lambda e, obank=obank, par=par: e.copy(
                    out=TMP2[:, 1024 + par * 512:1024 + (par + 1) * 512],
                    in_=pb[obank][:, :]))
                wmark(R_osb[par], SS)
                rmark(R_pb[obank], SS)
                for tq in range(4):
                    t = g * 4 + tq
                    tpar = t % 2
                    linv_col = LT0 + 4 + tpar
                    for half in range(2):
                        wbank = (2, 3)[half]
                        em.wguard("tensor", R_pb[wbank])
                        em.rguard("tensor", R_osb[par], R_attw)
                        T(lambda e, half=half, par=par, tq=tq, wbank=wbank: e.matmul(
                            pb[wbank][:, :],
                            TMP2[:, 1024 + par * 512 + tq * P:1024 + par * 512 + (tq + 1) * P],
                            WOap[:, half * 512:(half + 1) * 512], start=True, stop=True,
                            skip_group_check=True), inc=True)
                        wmark(R_pb[wbank], TS)
                        rmark(R_osb[par], TS)
                        rmark(R_attw, TS)
                        em.rguard("vector", R_pb[wbank], R_lt[tpar])
                        em.wguard("vector", R_xs[t])
                        V(lambda e, t=t, half=half, wbank=wbank, linv_col=linv_col: e.scalar_tensor_tensor(
                            out=xs_ap(t, half), in0=pb[wbank][:, :],
                            scalar=SM[:, linv_col:linv_col + 1],
                            in1=xs_ap(t, half), op0=ALU.mult, op1=ALU.add))
                        rmark(R_pb[wbank], VS)
                        rmark(R_lt[tpar], VS)
                        wmark(R_xs[t], VS)
            for i in range(2):
                for (s_, v_) in list(R_ptb[i].rd.values()):
                    rmark(R_tmp2[i], s_)
                if R_ptb[i].w:
                    R_tmp2[i].w = R_ptb[i].w
                for (s_, v_) in list(R_osb[i].rd.values()):
                    rmark(R_tmp2[i], s_)

        try:
            emit_ffn(second=False)
            emit_attention()
            emit_ffn(second=True)
            ck("end")
        except _Stop:
            pass

        if debug_dump or stop_after is not None:
            em.wait("sync", VS)
            em.wait("sync", SS)
            em.wait("sync", TS)
            em.wait("sync", GS)
            em.wait("sync", DG)
            SD(lambda e: e.dma_start(
                out=dxs_out.rearrange("(t p) c -> p t c", p=P),
                in_=XS[:, :].rearrange("p (t c) -> p t c", c=C)))
            SD(lambda e: e.dma_start(out=dsm_out, in_=SM[:, :]))
            em.wait("gpsimd", DSY)
            for i, big in enumerate((BIG1, BIG2, BIG3, BIG4)):
                GD(lambda e, i=i, big=big: e.dma_start(out=db_out[i], in_=big[:, :]))

        em.wait("sync", DSY)
        em.wait("gpsimd", DG)

        for s in em.sems:
            s.h = st.enter_context(nc.semaphore(s.name))
        block = st.enter_context(nc.Block())

        def replay(engobj, eng):
            for op in em.ops[eng]:
                if op[0] == "wait":
                    engobj.wait_ge(op[1].h, op[2])
                else:
                    _, fn, sem, inc = op
                    ins = fn(engobj)
                    if sem is not None:
                        ins.then_inc(sem.h, inc)

        @block.sync
        def _(e):
            replay(e, "sync")

        @block.gpsimd
        def _(e):
            replay(e, "gpsimd")

        @block.scalar
        def _(e):
            replay(e, "scalar")

        @block.vector
        def _(e):
            replay(e, "vector")

        @block.tensor
        def _(e):
            replay(e, "tensor")

    em.simulate()
    return nc, em



def host_inputs(full, core, NT=16):
    TCORE = NT * P
    GK = 8 * TCORE
    b, g = core // 4, core % 4
    bf = ml_dtypes.bfloat16
    f32 = lambda k: np.ascontiguousarray(np.asarray(full[k], dtype=np.float32))
    d = {}
    x = f32("L0_attn_x")
    d["x"] = np.ascontiguousarray(x[b, g * TCORE:(g + 1) * TCORE, :])
    d["wq"] = f32("L0_attn_Wq")
    d["wk"] = f32("L0_attn_Wk")
    d["wva"] = f32("L0_attn_Wv")
    d["woa"] = f32("L0_attn_Wo")
    d["wr"] = f32("L0_ffn_Wr")
    d["wvf"] = f32("L0_ffn_Wv")
    d["wof"] = f32("L0_ffn_Wo")
    d["ws"] = f32("L0_ffn_Ws")
    d["w1"] = f32("L0_ffn_W1")
    d["w2"] = f32("L0_ffn_W2")
    d["w3"] = f32("L0_ffn_W3")
    d["krw"] = f32("L0_ffn_K_rwkv")
    d["vrw"] = f32("L0_ffn_V_rwkv")
    wsm = np.concatenate([
        f32("L0_ffn_conf_rwkv").T,
        f32("L0_ffn_conf_trans")[:, None],
        f32("L0_ffn_w_diff")[:, None],
        f32("L0_ffn_W_aff")], axis=1)
    d["wsmT"] = np.ascontiguousarray(wsm.T)
    lnc = np.zeros((P, 32), np.float32)
    for i, nm in enumerate(["ln1_w", "ln1_b", "ln2_w", "ln2_b"]):
        lnc[:, i * 8:(i + 1) * 8] = f32("L0_ffn_" + nm).reshape(8, P).T
    d["lnc"] = lnc
    d["sh3"] = np.ascontiguousarray(
        np.broadcast_to(f32("L0_ffn_capital_shares"), (P, 3)))
    d["idn"] = np.eye(P, dtype=np.float32).astype(bf)
    d["tri"] = np.tril(np.ones((P, P), np.float32)).astype(bf)
    gm = np.zeros((P, 4 * 512), np.float32)
    for ki in range(4):
        for k in range(P):
            gm[k, ki * 512 + ki * P + k:(ki + 1) * 512] = 1.0
    d["gmask"] = np.ascontiguousarray(gm).astype(bf)
    d["onesp"] = np.ones((P, 1), np.float32).astype(bf)
    d["onesr"] = np.ones((1, P), np.float32).astype(bf)
    km = np.zeros((GK,), np.float32)
    km[4 * b * TCORE: (4 * b + g) * TCORE] = 1.0
    d["kmask"] = np.ascontiguousarray(np.broadcast_to(km, (P, GK))).astype(bf)
    d["vmask"] = np.ascontiguousarray(km.reshape(GK // P, P).T)
    d["nzero"] = np.full((P, 1), float(GK - g * TCORE), np.float32)
    return d


def kernel(**inputs):
    NT = 16
    nc, _em = build_program(NT=NT)
    in_maps = [host_inputs(inputs, r, NT) for r in range(NCORES)]
    res = run_bass_kernel_spmd(nc, in_maps, core_ids=list(range(NCORES)))
    TCORE = NT * P
    out = np.zeros((2, 4 * TCORE, C), np.float32)
    for r in range(NCORES):
        b, g = r // 4, r % 4
        out[b, g * TCORE:(g + 1) * TCORE, :] = res.results[r]["y"]
    return out


# version 08
# speedup vs baseline: 1.0561x; 1.0457x over previous
import ast as _ast
import inspect
import re as _re
import numpy as np
import ml_dtypes
from contextlib import ExitStack

_PB_EXPR_CACHE = {}


def _pacc_of(fn, eng):
    code = fn.__code__
    if code in _PB_EXPR_CACHE:
        exprs, has_pb = _PB_EXPR_CACHE[code]
    else:
        try:
            src = inspect.getsource(fn)
        except Exception:
            src = ""
        exprs = _re.findall(r"pb\[([^\]]+)\]", src)
        has_pb = bool(exprs) or ("pb[" in src)
        _PB_EXPR_CACHE[code] = (exprs, has_pb)
    if not has_pb:
        return None
    names = code.co_varnames[:code.co_argcount]
    dflts = fn.__defaults__ or ()
    ctx = dict(zip(names[len(names) - len(dflts):], dflts))
    def _aeval(node):
        if isinstance(node, _ast.Expression):
            return _aeval(node.body)
        if isinstance(node, _ast.Constant):
            return node.value
        if isinstance(node, _ast.Name):
            return ctx[node.id]
        if isinstance(node, _ast.BinOp):
            a, b = _aeval(node.left), _aeval(node.right)
            if isinstance(node.op, _ast.Add):
                return a + b
            if isinstance(node.op, _ast.Sub):
                return a - b
            if isinstance(node.op, _ast.Mult):
                return a * b
            if isinstance(node.op, _ast.Mod):
                return a % b
            if isinstance(node.op, _ast.FloorDiv):
                return a // b
        raise ValueError("expr")

    banks = set()
    for ex in exprs:
        try:
            banks.add(int(_aeval(_ast.parse(ex, mode="eval"))))
        except Exception:
            raise RuntimeError("cannot resolve psum bank expr %r in %s op" % (ex, eng))
    if not banks:
        return None
    if eng == "tensor":
        kind = "w"
    elif eng == "scalar":
        srcline = inspect.getsource(fn)
        kind = "w" if ("square(" in srcline and "out=pb" in srcline) else "r"
    else:
        kind = "r"
    return [(b, kind) for b in sorted(banks)]

import concourse.bass as bass
import concourse.mybir as mybir
from concourse.bass_utils import run_bass_kernel_spmd

F32 = mybir.dt.float32
BF16 = mybir.dt.bfloat16
AF = mybir.ActivationFunctionType
ALU = mybir.AluOpType
AX = mybir.AxisListType.X

P = 128
C = 1024
KCH = 8
H = 4096
DK = 128
NCORES = 8
EPS = 1e-5
INV_SQRT_DK = float(1.0 / np.sqrt(128.0))

ENGS = ("sync", "gpsimd", "scalar", "vector", "tensor")


class Sem:
    __slots__ = ("name", "owner", "c", "h")

    def __init__(self, name, owner=None):
        self.name = name
        self.owner = owner
        self.c = 0
        self.h = None


class Res:
    __slots__ = ("name", "w", "rd")

    def __init__(self, name):
        self.name = name
        self.w = None
        self.rd = {}


class Em:
    def __init__(self):
        self.ops = {e: [] for e in ENGS}
        self.hw = {e: {} for e in ENGS}
        self.sems = []
        self.label = "?"  

    def sem(self, name, owner=None):
        s = Sem(name, owner)
        self.sems.append(s)
        return s

    def wait(self, eng, sem, val=None):
        val = sem.c if val is None else val
        if val <= 0:
            return
        if sem.owner == eng:
            return
        if self.hw[eng].get(sem.name, -1) >= val:
            return
        self.hw[eng][sem.name] = val
        self.ops[eng].append(("wait", sem, val))

    def do(self, eng, fn, sem=None, inc=1, pacc=None):
        if pacc is None and eng in ("tensor", "scalar", "vector"):
            pacc = _pacc_of(fn, eng)
        if sem is not None:
            sem.c += inc
            self.ops[eng].append(("do", fn, sem, inc, pacc, self.label))
            return sem.c
        self.ops[eng].append(("do", fn, None, 0, pacc, self.label))
        return None

    def rguard(self, eng, *ress):
        for r in ress:
            if r.w:
                self.wait(eng, r.w[0], r.w[1])

    def wguard(self, eng, *ress):
        for r in ress:
            if r.w:
                self.wait(eng, r.w[0], r.w[1])
            for (s, v) in list(r.rd.values()):
                self.wait(eng, s, v)

    def simulate(self):
        pos = {e: 0 for e in ENGS}
        cnt = {s.name: 0 for s in self.sems}
        progressed = True
        while progressed:
            progressed = False
            for e in ENGS:
                while pos[e] < len(self.ops[e]):
                    op = self.ops[e][pos[e]]
                    if op[0] == "wait":
                        if cnt[op[1].name] < op[2]:
                            break
                    else:
                        sem, inc = op[2], op[3]
                        if sem is not None:
                            cnt[sem.name] += inc
                    pos[e] += 1
                    progressed = True
        stuck = {e: (pos[e], len(self.ops[e])) for e in ENGS if pos[e] < len(self.ops[e])}
        if stuck:
            lines = []
            for e, (p_, n) in stuck.items():
                op = self.ops[e][p_]
                lines.append(f"  {e}: op {p_}/{n} waiting {op[1].name} >= {op[2]} (now {cnt[op[1].name]})")
            raise RuntimeError("DEADLOCK in simulated schedule:\n" + "\n".join(lines))

    def counts(self):
        return {e: len(self.ops[e]) for e in ENGS}

    def check_psum(self):
        inc_seq = {s.name: [] for s in self.sems}
        cum = {s.name: 0 for s in self.sems}
        pos = {e: 0 for e in ENGS}
        cnt = {s.name: 0 for s in self.sems}
        clocks = {e: [] for e in ENGS}
        cur = {e: {f: -1 for f in ENGS} for e in ENGS}
        sem_src = {s.name: [] for s in self.sems}
        order = []
        progressed = True
        while progressed:
            progressed = False
            for e in ENGS:
                while pos[e] < len(self.ops[e]):
                    op = self.ops[e][pos[e]]
                    if op[0] == "wait":
                        _, sem, val = op
                        if cnt[sem.name] < val:
                            break
                        for (cafter, f, j) in sem_src[sem.name]:
                            if cafter >= val:
                                src_clock = clocks[f][j]
                                for g in ENGS:
                                    if src_clock.get(g, -1) > cur[e][g]:
                                        cur[e][g] = src_clock[g]
                                break
                        clocks[e].append(dict(cur[e]))
                    else:
                        sem, inc = op[2], op[3]
                        cur[e][e] = pos[e]
                        clocks[e].append(dict(cur[e]))
                        if sem is not None:
                            cnt[sem.name] += inc
                            sem_src[sem.name].append((cnt[sem.name], e, pos[e]))
                        order.append((e, pos[e], op[4], op[5]))
                    pos[e] += 1
                    progressed = True
        def srcof(e, i):
            fn = self.ops[e][i][1]
            try:
                return inspect.getsource(fn).strip().split(chr(10))[0][:90] + " @%d" % fn.__code__.co_firstlineno
            except Exception:
                return "?"
        latest = {}
        bad = []
        for (e, i, pacc, label) in order:
            if not pacc:
                continue
            for (bank, kind) in (pacc if isinstance(pacc, list) else [pacc]):
                d = latest.setdefault(bank, {})
                myclock = clocks[e][i]
                for f, (j, k2, lb2) in d.items():
                    if f == e:
                        continue
                    if kind == "r" and k2 == "r":
                        continue
                    if myclock.get(f, -1) < j:
                        bad.append((bank, f, j, k2, e, i, kind))
                d[e] = (i, kind, label)
        if bad:
            for b in bad[:12]:
                print("PSUM RACE bank %d: %s[%d](%s): %s" % (b[0], b[1], b[2], b[3], srcof(b[1], b[2])))
                print("   unordered vs %s[%d](%s): %s" % (b[4], b[5], b[6], srcof(b[4], b[5])))
            raise RuntimeError("psum races: %d" % len(bad))


def rmark(res, sem):
    res.rd[id(sem)] = (sem, sem.c)


def wmark(res, sem):
    res.w = (sem, sem.c)
    res.rd = {}


class _Stop(Exception):
    pass


def build_program(NT=16, stop_after=None, debug_dump=False):
    TCORE = NT * P
    NTG = NT // 4
    GK = 8 * TCORE
    NKC = GK // 512
    HHALF = H // 2
    HQ = H // 4
    NHMQ = HQ // P

    nc = bass.Bass("TRN2", target_bir_lowering=False, debug=False)
    em = Em()

    def ck(name):
        if stop_after is not None and name == stop_after:
            raise _Stop()

    def din(name, shape, dt=F32):
        return nc.dram_tensor(name, shape, dt, kind="ExternalInput").ap()

    x_in = din("x", [TCORE, C])
    wq_in = din("wq", [C, DK])
    wk_in = din("wk", [C, DK])
    wva_in = din("wva", [C, DK])
    woa_in = din("woa", [DK, C])
    wr_in = din("wr", [C, C])
    wvf_in = din("wvf", [C, C])
    wof_in = din("wof", [C, C])
    ws_in = din("ws", [C, C])
    w1_in = din("w1", [C, C])
    w2_in = din("w2", [C, C])
    w3_in = din("w3", [C, C])
    krw_in = din("krw", [2, C, H])
    vrw_in = din("vrw", [2, H, C])
    wsmT_in = din("wsmT", [7, C])
    lnc_in = din("lnc", [P, 32])
    sh3_in = din("sh3", [P, 3])
    idn_in = din("idn", [P, P], BF16)
    gmask_in = din("gmask", [P, 4 * 512], BF16)
    onesp_in = din("onesp", [P, 1], BF16)
    onesr_in = din("onesr", [1, P], BF16)
    kmask_in = din("kmask", [P, GK], BF16)
    vmask_in = din("vmask", [P, GK // P])
    nzero_in = din("nzero", [P, 1])

    y_out = nc.dram_tensor("y", [TCORE, C], F32, kind="ExternalOutput").ap()

    if debug_dump or stop_after is not None:
        dxs_out = nc.dram_tensor("dxs", [TCORE, C], F32, kind="ExternalOutput").ap()
        db_out = [nc.dram_tensor("db%d" % i, [P, 16 * C], F32, kind="ExternalOutput").ap()
                  for i in range(4)]
        dsm_out = nc.dram_tensor("dsm", [P, 248], F32, kind="ExternalOutput").ap()
    send_d = nc.dram_tensor("kvsend", [P, 2 * TCORE], BF16, kind="Internal")
    recv_d = nc.dram_tensor("kvrecv", [NCORES * P, 2 * TCORE], BF16, kind="Internal")

    with ExitStack() as st:
        def sb(name, shape, dt):
            return st.enter_context(nc.sbuf_tensor(name, shape, dt))

        XS = sb("XS", [P, NT * C], F32)
        BIG1 = sb("BIG1", [P, 16 * C], BF16)
        BIG2 = sb("BIG2", [P, 16 * C], BF16)
        BIG3 = sb("BIG3", [P, 16 * C], BF16)
        BIG4 = sb("BIG4", [P, 16 * C], BF16)
        TMP2 = sb("TMP2", [P, 2 * C], BF16)
        SB512 = sb("SB512", [P, 2 * 512], BF16)
        SM = sb("SM", [P, 248], F32)
        SMB = sb("SMB", [P, 64], BF16)
        LNC = sb("LNC", [P, 32], F32)
        SH3 = sb("SH3", [P, 3], F32)
        IDN = sb("IDN", [P, P], BF16)
        GMK = sb("GMK", [P, 4 * 512], BF16)
        ONESP = sb("ONESP", [P, 1], BF16)
        ONESR = sb("ONESR", [1, P], BF16)
        VMK = sb("VMK", [P, GK // P], F32)
        NZERO = sb("NZERO", [P, 1], F32)
        LSF = sb("LSF", [1, 512], F32)

        pb = [st.enter_context(nc.psum_tensor("pb%d" % i, [P, 512], F32)) for i in range(8)]

        TS = em.sem("TS", "tensor")
        SS = em.sem("SS", "scalar")
        VS = em.sem("VS", "vector")
        GS = em.sem("GS", "gpsimd")
        DG = em.sem("DG", None)
        DSY = em.sem("DSY", None)
        CCS = em.sem("CCS", None)

        R_xs = [Res("xs%d" % t) for t in range(NT)]
        R_pb = [Res("pb%d" % i) for i in range(8)]
        R_tmp2 = [Res("tmp2_%d" % i) for i in range(2)]
        R_sb512 = [Res("sb512_%d" % i) for i in range(2)]
        R_smb = Res("smb")
        R_wt7 = Res("wt7")
        R_wtok = Res("wtok")
        R_sstat = Res("sstat")
        R_norm = Res("norm")
        R_send = Res("send")
        R_recv = Res("recv")
        R_lnst = [Res("lnst%d" % i) for i in range(2)]
        R_bid = [Res("bid%d" % i) for i in range(2)]
        R_lt = [Res("lt%d" % i) for i in range(2)]

        region_owner = {"B1": [], "B2": [], "B3": [], "B4a": [], "B4b": []}

        def claim(eng, region, new_res):
            for r in region_owner[region]:
                em.wguard(eng, r)
            region_owner[region] = [new_res]

        def add_owner(region, res):
            if res not in region_owner[region]:
                region_owner[region].append(res)

        SST0 = 0
        WTK0 = 32
        LNST0 = 80
        NRM0 = 176
        ONEC = 188
        BID0 = 190
        LT0 = 238

        def T(fn, inc=False):
            return em.do("tensor", fn, TS if inc else None)

        def S(fn):
            return em.do("scalar", fn, SS, 1)

        def V(fn):
            return em.do("vector", fn, VS, 1)

        def Vd(fn):
            em.do("vector", fn)
            return em.do("vector", lambda e: e.drain(), VS, 1)

        def Sd(fn):
            em.do("scalar", fn)
            return em.do("scalar", lambda e: e.drain(), SS, 1)

        def G(fn):
            return em.do("gpsimd", fn, GS, 1)

        def GD(fn):
            return em.do("gpsimd", fn, DG, 16)

        def SD(fn):
            return em.do("sync", fn, DSY, 16)

        def xs_ap(t, half=None):
            if half is None:
                return XS[:, t * C:(t + 1) * C]
            return XS[:, t * C + half * 512: t * C + (half + 1) * 512]

        def fm(big, k, lo, n):
            return big[:, k * TCORE + lo: k * TCORE + lo + n]

        SD(lambda e: e.dma_start(out=XS[:, :].rearrange("p (t c) -> p t c", c=C),
                                 in_=x_in.rearrange("(t p) c -> p t c", p=P)))
        for t in range(NT):
            wmark(R_xs[t], DSY)
        SD(lambda e: e.dma_start(out=LNC[:, :], in_=lnc_in))
        SD(lambda e: e.dma_start(out=SH3[:, :], in_=sh3_in))
        SD(lambda e: e.dma_start(out=IDN[:, :], in_=idn_in))
        SD(lambda e: e.dma_start(out=GMK[:, :], in_=gmask_in))
        SD(lambda e: e.dma_start(out=ONESP[:, :], in_=onesp_in))
        SD(lambda e: e.dma_start(out=ONESR[:, :], in_=onesr_in))
        SD(lambda e: e.dma_start(out=VMK[:, :], in_=vmask_in))
        SD(lambda e: e.dma_start(out=NZERO[:, :], in_=nzero_in))
        smalls_dma = DSY.c

        GD(lambda e: e.dma_start(out=TMP2[0:7, 0:C], in_=wsmT_in))
        wmark(R_wt7, DG)
        wmark(R_tmp2[0], DG)

        em.wait("vector", DSY, smalls_dma)
        V(lambda e: e.memset(SM[:, ONEC:ONEC + 1], 1.0))

        em.rguard("tensor", R_wt7)
        em.wait("tensor", DSY, smalls_dma)
        for c_ in range(KCH):
            T(lambda e, c_=c_: e.matmul(pb[6][:, c_ * 8:c_ * 8 + 7],
                                        TMP2[0:7, c_ * P:(c_ + 1) * P],
                                        IDN[0:7, 0:7], start=True, stop=True),
              inc=(c_ == KCH - 1))
        rmark(R_tmp2[0], TS)
        wmark(R_pb[6], TS)
        em.rguard("scalar", R_pb[6])
        for c_ in range(KCH):
            S(lambda e, c_=c_: e.copy(out=SMB[:, c_ * 8:c_ * 8 + 7],
                                      in_=pb[6][:, c_ * 8:c_ * 8 + 7]))
        rmark(R_pb[6], SS)
        wmark(R_smb, SS)

        R_ws = [Res("wsA"), Res("wsB")]

        def stage_w(src_ap, slot):
            off = slot * 8 * C
            r = R_ws[slot]
            claim("gpsimd", "B4a" if slot == 0 else "B4b", r)
            GD(lambda e, off=off, src_ap=src_ap: e.dma_start(
                out=BIG4[:, off:off + 8 * C].rearrange("p (k n) -> p k n", n=C),
                in_=src_ap.rearrange("(k p) n -> p k n", p=P)))
            wmark(r, DG)
            return (lambda k, off=off: BIG4[:, off + k * C: off + (k + 1) * C]), r

        def ln_group(tg, src_fn, dst_big, dst_res, lnw_off, double_ln, save_stats,
                     plain=False, icb=None):
            par = tg % 2
            b = LNST0 + par * 48
            SUMc, SQAc, SQBc, MMc = b, b + 4, b + 8, b + 12
            EX2c, INVc, INV2c = b + 16, b + 20, b + 24
            SCc, S2Cc, CBc, NBc = b + 28, b + 32, b + 36, b + 40
            tiles = list(range(tg * 4, tg * 4 + 4))
            em.wguard("vector", R_lnst[par])
            for j, t in enumerate(tiles):
                src = src_fn(t)
                V(lambda e, src=src, j=j: e.reduce_sum(SM[:, SUMc + j:SUMc + j + 1], src, axis=AX))
                em.wguard("scalar", R_pb[6])
                em.wguard("scalar", R_pb[7])
                for hb in range(2):
                    em.do("scalar", lambda e, src=src, hb=hb: e.square(
                        out=pb[6 + hb][:, :], in_=src[:, hb * 512:(hb + 1) * 512]),
                          pacc=[(6 + hb, "w")])
                em.do("scalar", lambda e: e.drain(), SS, 1)
                wmark(R_pb[6], SS)
                wmark(R_pb[7], SS)
                em.rguard("vector", R_pb[6])
                em.rguard("vector", R_pb[7])
                V(lambda e, j=j: e.reduce_sum(SM[:, SQAc + j:SQAc + j + 1], pb[6][:, :], axis=AX))
                V(lambda e, j=j: e.reduce_sum(SM[:, SQBc + j:SQBc + j + 1], pb[7][:, :], axis=AX))
                rmark(R_pb[6], VS)
                rmark(R_pb[7], VS)
                if icb:
                    icb()
            em.do("vector", lambda e: e.drain())
            Vd(lambda e: e.tensor_tensor(SM[:, SQAc:SQAc + 4], SM[:, SQAc:SQAc + 4],
                                         SM[:, SQBc:SQBc + 4], ALU.add))
            Vd(lambda e: e.tensor_scalar_mul(SM[:, MMc:MMc + 4], SM[:, SUMc:SUMc + 4], 1.0 / C))
            Vd(lambda e: e.tensor_scalar_mul(SM[:, EX2c:EX2c + 4], SM[:, SQAc:SQAc + 4], 1.0 / C))
            Vd(lambda e: e.tensor_tensor(SM[:, SQBc:SQBc + 4], SM[:, MMc:MMc + 4],
                                         SM[:, MMc:MMc + 4], ALU.mult))
            Vd(lambda e: e.tensor_tensor(SM[:, EX2c:EX2c + 4], SM[:, EX2c:EX2c + 4],
                                         SM[:, SQBc:SQBc + 4], ALU.subtract))
            Vd(lambda e: e.tensor_scalar_add(SM[:, SQBc:SQBc + 4], SM[:, EX2c:EX2c + 4], EPS))
            Vd(lambda e: e.reciprocal(SM[:, INVc:INVc + 4], SM[:, SQBc:SQBc + 4]))
            if double_ln:
                Vd(lambda e: e.tensor_tensor(SM[:, SQAc:SQAc + 4], SM[:, EX2c:EX2c + 4],
                                             SM[:, INVc:INVc + 4], ALU.mult))
                Vd(lambda e: e.tensor_scalar_add(SM[:, SQAc:SQAc + 4], SM[:, SQAc:SQAc + 4], EPS))
                Vd(lambda e: e.reciprocal(SM[:, INV2c:INV2c + 4], SM[:, SQAc:SQAc + 4]))
            em.wait("scalar", VS)
            if double_ln:
                Sd(lambda e: e.sqrt(out=SM[:, SCc:SCc + 8], in_=SM[:, INVc:INVc + 8]))
            else:
                Sd(lambda e: e.sqrt(out=SM[:, SCc:SCc + 4], in_=SM[:, INVc:INVc + 4]))
            em.wait("vector", SS)
            if double_ln:
                Vd(lambda e: e.tensor_tensor(SM[:, CBc:CBc + 4], SM[:, SCc:SCc + 4],
                                             SM[:, S2Cc:S2Cc + 4], ALU.mult))
            else:
                Vd(lambda e: e.tensor_copy(out=SM[:, CBc:CBc + 4], in_=SM[:, SCc:SCc + 4]))
            Vd(lambda e: e.scalar_tensor_tensor(
                out=SM[:, NBc:NBc + 4], in0=SM[:, MMc:MMc + 4], scalar=-1.0,
                in1=SM[:, CBc:CBc + 4], op0=ALU.mult, op1=ALU.mult))
            if save_stats:
                em.wguard("vector", R_sstat) if tg == 0 else None
                Vd(lambda e, tg=tg: e.tensor_copy(out=SM[:, SST0 + tg * 4:SST0 + tg * 4 + 4],
                                                  in_=SM[:, SCc:SCc + 4]))
                Vd(lambda e, tg=tg: e.scalar_tensor_tensor(
                    out=SM[:, SST0 + 16 + tg * 4:SST0 + 16 + tg * 4 + 4],
                    in0=SM[:, MMc:MMc + 4], scalar=-1.0,
                    in1=SM[:, SCc:SCc + 4], op0=ALU.mult, op1=ALU.mult))
                wmark(R_sstat, VS)
            wmark(R_lnst[par], VS)
            vstat = VS.c
            for j, t in enumerate(tiles):
                sl = t % 2
                src = src_fn(t)
                em.wguard("scalar", R_tmp2[sl])
                em.wait("scalar", VS, vstat)
                S(lambda e, src=src, sl=sl, j=j: e.activation(
                    out=TMP2[:, sl * C:(sl + 1) * C], in_=src, func=AF.Identity,
                    bias=SM[:, NBc + j:NBc + j + 1], scale=SM[:, CBc + j:CBc + j + 1]))
                wmark(R_tmp2[sl], SS)
                rmark(R_lnst[par], SS)
                for hb in range(2):
                    bank = 6 + hb
                    em.wguard("tensor", R_pb[bank])
                    em.rguard("tensor", R_tmp2[sl])
                    em.wait("tensor", DSY, smalls_dma)
                    for jj in range(4):
                        cc = hb * 4 + jj
                        T(lambda e, cc=cc, jj=jj, bank=bank, sl=sl: e.matmul(
                            pb[bank][:, jj * P:(jj + 1) * P],
                            TMP2[:, sl * C + cc * P: sl * C + (cc + 1) * P],
                            IDN[:, :], start=True, stop=True), inc=(jj == 3))
                    wmark(R_pb[bank], TS)
                    rmark(R_tmp2[sl], TS)
                    em.rguard("scalar", R_pb[bank])
                    for jj in range(4):
                        cc = hb * 4 + jj
                        if plain:
                            S(lambda e, cc=cc, jj=jj, bank=bank, t=t: e.copy(
                                out=fm(dst_big, cc, t * P, P), in_=pb[bank][:, jj * P:(jj + 1) * P]))
                        else:
                            S(lambda e, cc=cc, jj=jj, bank=bank, t=t: e.activation(
                                out=fm(dst_big, cc, t * P, P), in_=pb[bank][:, jj * P:(jj + 1) * P],
                                func=AF.Identity, bias=LNC[:, lnw_off + 8 + cc:lnw_off + 9 + cc],
                                scale=LNC[:, lnw_off + cc:lnw_off + 1 + cc]))
                    rmark(R_pb[bank], SS)
                if icb:
                    icb()
            wmark(dst_res, SS)

        def ccmm_fm(wslot, in_big, in_res_list, out_big, out_res, kind, other_big=None,
                    banks=None, mcb=None):
            wfn, wres = wslot
            for m in range(KCH):
                if mcb is not None:
                    mcb(m)
                if NTG == 4:
                    bset = [(m % 2) * 4 + tg for tg in range(4)] if banks is None else \
                           [banks[tg % len(banks)] for tg in range(4)]
                else:
                    bset = [(2, 3)[(m * NTG + tg) % 2] for tg in range(NTG)]
                for tg in range(NTG):
                    em.wguard("tensor", R_pb[bset[tg]])
                em.rguard("tensor", wres, *in_res_list)
                for k in range(KCH):
                    for tg in range(NTG):
                        bank = bset[tg]
                        T(lambda e, k=k, m=m, tg=tg, bank=bank: e.matmul(
                            pb[bank][:, :], wfn(k)[:, m * P:(m + 1) * P],
                            fm(in_big, k, tg * 512, 512),
                            start=(k == 0), stop=(k == KCH - 1)),
                          inc=(k == KCH - 1 and tg == NTG - 1))
                rmark(wres, TS)
                for rr in in_res_list:
                    rmark(rr, TS)
                for tg in range(NTG):
                    bank = bset[tg]
                    wmark(R_pb[bank], TS)
                    if kind == "mult":
                        em.rguard("vector", R_pb[bank])
                        V(lambda e, m=m, tg=tg, bank=bank: e.tensor_tensor(
                            out=fm(out_big, m, tg * 512, 512),
                            in0=fm(other_big, m, tg * 512, 512),
                            in1=pb[bank][:, :], op=ALU.mult))
                        rmark(R_pb[bank], VS)
                    else:
                        em.rguard("scalar", R_pb[bank])
                        if kind == "copy":
                            S(lambda e, m=m, tg=tg, bank=bank: e.copy(
                                out=fm(out_big, m, tg * 512, 512), in_=pb[bank][:, :]))
                        else:
                            S(lambda e, m=m, tg=tg, bank=bank: e.activation(
                                out=fm(out_big, m, tg * 512, 512), in_=pb[bank][:, :],
                                func=AF.Sigmoid))
                        rmark(R_pb[bank], SS)
            wmark(out_res, VS if kind == "mult" else SS)

        def ccmm_tok(wslot, in_big, in_res_list, evac, banks=((4, 5), (6, 7))):
            wfn, wres = wslot
            for t in range(NT):
                bpair = banks[t % 2]
                em.wguard("tensor", R_pb[bpair[0]])
                em.wguard("tensor", R_pb[bpair[1]])
                em.rguard("tensor", wres, *in_res_list)
                for k in range(KCH):
                    for half in range(2):
                        T(lambda e, k=k, t=t, half=half, bank=bpair[half]: e.matmul(
                            pb[bank][:, :], fm(in_big, k, t * P, P),
                            wfn(k)[:, half * 512:(half + 1) * 512],
                            start=(k == 0), stop=(k == KCH - 1)),
                          inc=(k == KCH - 1 and half == 1))
                rmark(wres, TS)
                for rr in in_res_list:
                    rmark(rr, TS)
                for half in range(2):
                    wmark(R_pb[bpair[half]], TS)
                    evac(t, half, bpair[half])

        def emit_ffn(second):
            pref = "2" if second else "1"
            R_xlnf = [Res("xlnf%d_%d" % (second, g)) for g in range(NTG)]
            R_stf = Res("stf%d" % second)
            R_sgf = Res("sgf%d" % second)
            R_xf = [Res("xf%d_%d" % (second, t)) for t in range(NT)]
            R_hf = [Res("hf%d_%d" % (second, g)) for g in range(NTG)]
            R_sg2f = Res("sg2f%d" % second)
            R_hrh = Res("hrh%d" % second)
            R_ks = [Res("ks%d_%d" % (second, i)) for i in range(2)]
            R_vst = [Res("vst%d_%d" % (second, i)) for i in range(2)]

            ws_slot = stage_w(ws_in, 0)

            def emit_f2_col(m, tg2):
                bank = (2, 3, 4, 5)[(m + tg2 * KCH) % 4]
                em.wguard("tensor", R_pb[bank])
                em.rguard("tensor", ws_slot[1], R_xlnf[tg2])
                for k in range(KCH):
                    T(lambda e, k=k, m=m, tg2=tg2, bank=bank: e.matmul(
                        pb[bank][:, :], ws_slot[0](k)[:, m * P:(m + 1) * P],
                        fm(BIG1, k, tg2 * 512, 512),
                        start=(k == 0), stop=(k == KCH - 1)), inc=(k == KCH - 1))
                wmark(R_pb[bank], TS)
                rmark(ws_slot[1], TS)
                rmark(R_xlnf[tg2], TS)
                em.rguard("scalar", R_pb[bank])
                S(lambda e, m=m, tg2=tg2, bank=bank: e.copy(
                    out=fm(BIG2, m, tg2 * 512, 512), in_=pb[bank][:, :]))
                rmark(R_pb[bank], SS)

            colq = []

            def f12_icb():
                if colq:
                    emit_f2_col(*colq.pop(0))

            for tg in range(NTG):
                for t in range(tg * 4, tg * 4 + 4):
                    em.rguard("vector", R_xs[t])
                    em.rguard("scalar", R_xs[t])
                if tg == 0:
                    claim("scalar", "B1", R_xlnf[0])
                    for g2 in range(1, NTG):
                        add_owner("B1", R_xlnf[g2])
                    claim("scalar", "B2", R_stf)
                ln_group(tg, lambda tt: xs_ap(tt), BIG1, R_xlnf[tg],
                         lnw_off=0, double_ln=True, save_stats=True, icb=f12_icb)
                for t in range(tg * 4, tg * 4 + 4):
                    rmark(R_xs[t], SS)
                    rmark(R_xs[t], VS)
                colq += [(m, tg) for m in range(KCH)]
            while colq:
                emit_f2_col(*colq.pop(0))
            wmark(R_stf, SS)
            ck(pref + "f2")

            wr_slot = stage_w(wr_in, 1)
            claim("scalar", "B3", R_sgf)
            ccmm_fm(wr_slot, BIG1, R_xlnf, BIG3, R_sgf, "sigmoid")
            ck(pref + "f3")
            wv_slot = stage_w(wvf_in, 0)
            ccmm_fm(wv_slot, BIG1, R_xlnf, BIG3, R_sgf, "mult", other_big=BIG3)
            ck(pref + "f4")
            wo_slot = stage_w(wof_in, 1)

            def evac_f5(t, half, bank):
                if t == 0 and half == 0:
                    claim("scalar", "B1", R_xf[0])
                    for t2 in range(1, NT):
                        add_owner("B1", R_xf[t2])
                em.rguard("scalar", R_sstat)
                em.rguard("scalar", R_xs[t])
                S(lambda e, t=t, half=half: e.activation(
                    out=BIG1[:, t * C + half * 512: t * C + (half + 1) * 512],
                    in_=xs_ap(t, half), func=AF.Identity,
                    bias=SM[:, SST0 + 16 + t:SST0 + 17 + t],
                    scale=SM[:, SST0 + t:SST0 + 1 + t]))
                rmark(R_sstat, SS)
                rmark(R_xs[t], SS)
                em.wait("vector", SS)
                em.rguard("vector", R_pb[bank])
                V(lambda e, t=t, half=half, bank=bank: e.tensor_tensor(
                    out=BIG1[:, t * C + half * 512: t * C + (half + 1) * 512],
                    in0=BIG1[:, t * C + half * 512: t * C + (half + 1) * 512],
                    in1=pb[bank][:, :], op=ALU.add))
                rmark(R_pb[bank], VS)
                wmark(R_xf[t], VS)
                em.wguard("vector", R_xs[t])
                V(lambda e, t=t, half=half: e.scalar_tensor_tensor(
                    out=xs_ap(t, half),
                    in0=BIG1[:, t * C + half * 512: t * C + (half + 1) * 512],
                    scalar=0.5, in1=xs_ap(t, half), op0=ALU.mult, op1=ALU.add))
                rmark(R_xf[t], VS)
                wmark(R_xs[t], VS)

            ccmm_tok(wo_slot, BIG3, [R_sgf], evac_f5)
            ck(pref + "f5")
            w2_slot = stage_w(w2_in, 0)

            claim("scalar", "B1", R_sg2f)
            f7_cols = []

            def emit_f7_col(m):
                bset = [(2, 3, 4, 5)[(m * 2 + i) % 4] for i in range(NTG)] if NTG == 4 \
                    else [(2, 3)[(m * NTG + tg_) % 2] for tg_ in range(NTG)]
                for tg_ in range(NTG):
                    em.wguard("tensor", R_pb[bset[tg_]])
                em.rguard("tensor", w2_slot[1], R_stf)
                for k in range(KCH):
                    for tg_ in range(NTG):
                        T(lambda e, k=k, m=m, tg_=tg_, bank=bset[tg_]: e.matmul(
                            pb[bank][:, :], w2_slot[0](k)[:, m * P:(m + 1) * P],
                            fm(BIG2, k, tg_ * 512, 512),
                            start=(k == 0), stop=(k == KCH - 1)),
                          inc=(k == KCH - 1 and tg_ == NTG - 1))
                rmark(w2_slot[1], TS)
                rmark(R_stf, TS)
                for tx in range((m * TCORE) // C, ((m + 1) * TCORE + C - 1) // C):
                    em.wguard("scalar", R_xf[tx])
                for tg_ in range(NTG):
                    bank = bset[tg_]
                    wmark(R_pb[bank], TS)
                    em.rguard("scalar", R_pb[bank])
                    S(lambda e, m=m, tg_=tg_, bank=bank: e.activation(
                        out=fm(BIG1, m, tg_ * 512, 512), in_=pb[bank][:, :],
                        func=AF.Sigmoid))
                    rmark(R_pb[bank], SS)

            mque = list(range(KCH))
            tiles_done = [0]

            def f67_need(m):
                return ((m + 1) * TCORE - 1) // C

            def f67_icb():
                if mque and f67_need(mque[0]) < tiles_done[0]:
                    emit_f7_col(mque.pop(0))

            for tg in range(NTG):
                for t in range(tg * 4, tg * 4 + 4):
                    em.rguard("vector", R_xf[t])
                    em.rguard("scalar", R_xf[t])
                if tg == 0:
                    claim("scalar", "B3", R_hf[0])
                    for g2 in range(1, NTG):
                        add_owner("B3", R_hf[g2])

                def f67_src(tt):
                    return BIG1[:, tt * C:(tt + 1) * C]

                ln_group(tg, f67_src, BIG3, R_hf[tg],
                         lnw_off=16, double_ln=False, save_stats=False,
                         icb=lambda: None)
                for t in range(tg * 4, tg * 4 + 4):
                    rmark(R_xf[t], SS)
                    rmark(R_xf[t], VS)
                tiles_done[0] = tg * 4 + 4
                for _ in range(8):
                    f67_icb()
            while mque:
                emit_f7_col(mque.pop(0))
            wmark(R_sg2f, SS)
            ck(pref + "f6")
            ck(pref + "f7")
            w1_slot = stage_w(w1_in, 1)
            ccmm_fm(w1_slot, BIG3, R_hf, BIG1, R_sg2f, "mult", other_big=BIG1)
            ck(pref + "f8")

            for t in range(NT):
                bank = 6 + (t % 2)
                par = t % 2
                em.wguard("tensor", R_pb[bank])
                em.rguard("tensor", R_smb, R_hf[t // 4])
                for k in range(KCH):
                    T(lambda e, k=k, t=t, bank=bank: e.matmul(
                        pb[bank][:, 0:7], fm(BIG3, k, t * P, P),
                        SMB[:, k * 8:k * 8 + 7],
                        start=(k == 0), stop=(k == KCH - 1)), inc=(k == KCH - 1))
                wmark(R_pb[bank], TS)
                rmark(R_smb, TS)
                rmark(R_hf[t // 4], TS)
                b0 = BID0 + par * 24
                em.rguard("scalar", R_pb[bank])
                em.wguard("scalar", R_bid[par])
                S(lambda e, bank=bank, b0=b0: e.activation(out=SM[:, b0:b0 + 4],
                                                           in_=pb[bank][:, 0:4], func=AF.Sigmoid))
                wmark(R_bid[par], SS)
                em.wait("vector", SS)
                em.wait("vector", DSY, smalls_dma)
                Vd(lambda e, b0=b0: e.tensor_single_scalar(SM[:, b0 + 4:b0 + 7], SH3[:, :],
                                                          SM[:, b0 + 3:b0 + 4], ALU.mult))
                Vd(lambda e, b0=b0: e.tensor_tensor(SM[:, b0 + 7:b0 + 10], SM[:, b0:b0 + 3],
                                                   SM[:, b0 + 4:b0 + 7], ALU.mult))
                em.rguard("vector", R_pb[bank])
                Vd(lambda e, bank=bank, b0=b0: e.tensor_tensor(
                    SM[:, b0 + 7:b0 + 10], SM[:, b0 + 7:b0 + 10], pb[bank][:, 4:7], ALU.add))
                rmark(R_pb[bank], VS)
                Vd(lambda e, b0=b0: e.reduce_max(SM[:, b0 + 10:b0 + 11], SM[:, b0 + 7:b0 + 10], axis=AX))
                Vd(lambda e, b0=b0: e.tensor_single_scalar(SM[:, b0 + 11:b0 + 14],
                                                          SM[:, b0 + 7:b0 + 10],
                                                          SM[:, b0 + 10:b0 + 11], ALU.is_ge))
                Vd(lambda e, b0=b0: e.tensor_copy(out=SM[:, b0 + 14:b0 + 15], in_=SM[:, b0 + 11:b0 + 12]))
                Vd(lambda e, b0=b0: e.tensor_tensor(SM[:, b0 + 17:b0 + 18], SM[:, b0 + 12:b0 + 13],
                                                   SM[:, b0 + 11:b0 + 12], ALU.mult))
                Vd(lambda e, b0=b0: e.tensor_tensor(SM[:, b0 + 15:b0 + 16], SM[:, b0 + 12:b0 + 13],
                                                   SM[:, b0 + 17:b0 + 18], ALU.subtract))
                Vd(lambda e, b0=b0: e.tensor_tensor(SM[:, b0 + 17:b0 + 18], SM[:, b0 + 14:b0 + 15],
                                                   SM[:, b0 + 15:b0 + 16], ALU.add))
                Vd(lambda e, b0=b0: e.scalar_tensor_tensor(
                    out=SM[:, b0 + 16:b0 + 17], in0=SM[:, b0 + 17:b0 + 18], scalar=-1.0,
                    in1=SM[:, ONEC:ONEC + 1], op0=ALU.mult, op1=ALU.add))
                Vd(lambda e, b0=b0: e.tensor_tensor(SM[:, b0 + 17:b0 + 20], SM[:, b0 + 14:b0 + 17],
                                                   SM[:, b0:b0 + 3], ALU.mult))
                Vd(lambda e, b0=b0: e.reduce_sum(SM[:, b0 + 20:b0 + 21], SM[:, b0 + 17:b0 + 20], axis=AX))
                Vd(lambda e, b0=b0: e.tensor_scalar_add(SM[:, b0 + 21:b0 + 22], SM[:, b0 + 20:b0 + 21], 1e-6))
                Vd(lambda e, b0=b0: e.reciprocal(SM[:, b0 + 22:b0 + 23], SM[:, b0 + 21:b0 + 22]))
                Vd(lambda e, b0=b0: e.tensor_tensor(SM[:, b0 + 23:b0 + 24], SM[:, b0 + 20:b0 + 21],
                                                   SM[:, b0 + 22:b0 + 23], ALU.mult))
                Vd(lambda e, b0=b0: e.tensor_scalar_mul(SM[:, b0 + 23:b0 + 24], SM[:, b0 + 23:b0 + 24], 0.5))
                Vd(lambda e, t=t, b0=b0: e.tensor_single_scalar(
                    SM[:, WTK0 + 3 * t:WTK0 + 3 * t + 3], SM[:, b0 + 14:b0 + 17],
                    SM[:, b0 + 23:b0 + 24], ALU.mult))
                rmark(R_bid[par], VS)
            wmark(R_wtok, VS)
            ck(pref + "f9")

            w3_slot = stage_w(w3_in, 0)

            def evac_f10(t, half, bank):
                em.rguard("vector", R_pb[bank], R_wtok)
                em.wguard("vector", R_xs[t])
                V(lambda e, t=t, half=half, bank=bank: e.scalar_tensor_tensor(
                    out=xs_ap(t, half), in0=pb[bank][:, :],
                    scalar=SM[:, WTK0 + 3 * t + 2:WTK0 + 3 * t + 3],
                    in1=xs_ap(t, half), op0=ALU.mult, op1=ALU.add))
                rmark(R_pb[bank], VS)
                rmark(R_wtok, VS)
                wmark(R_xs[t], VS)

            ccmm_tok(w3_slot, BIG1, [R_sg2f], evac_f10)
            ck(pref + "f10")

            NQ = 4
            for e_ in range(2):
                for q4 in range(NQ):
                    qi = e_ * NQ + q4
                    ks = R_ks[qi % 2]
                    koff = (qi % 2) * 8 * C
                    if qi == 0 and not second:
                        claim("gpsimd", "B2", R_ks[0])
                        add_owner("B2", R_ks[1])
                    elif qi == 0:
                        claim("gpsimd", "B2", R_ks[0])
                        add_owner("B2", R_ks[1])
                    else:
                        em.wguard("gpsimd", ks)
                    GD(lambda e, e_=e_, q4=q4, koff=koff: e.dma_start(
                        out=BIG2[:, koff:koff + 8 * HQ].rearrange("p (k n) -> p k n", n=HQ),
                        in_=krw_in[e_][:, q4 * HQ:(q4 + 1) * HQ]
                        .rearrange("(k p) n -> p k n", p=P)))
                    wmark(ks, DG)
                    vs_ = R_vst[qi % 2]
                    voff = (qi % 2) * 8 * C
                    if qi == 0:
                        claim("gpsimd", "B1", R_vst[0])
                        add_owner("B1", R_vst[1])
                    else:
                        em.wguard("gpsimd", vs_)
                    GD(lambda e, e_=e_, q4=q4, voff=voff: e.dma_start(
                        out=BIG1[:, voff:voff + 8 * C].rearrange("p (m c) -> p m c", c=C),
                        in_=vrw_in[e_][q4 * HQ:(q4 + 1) * HQ, :].rearrange(
                            "(m p) c -> p m c", p=P)))
                    wmark(vs_, DG)
                    if qi == 0:
                        claim("vector", "B4a", R_hrh)
                        claim("vector", "B4b", R_hrh)
                    else:
                        em.wguard("vector", R_hrh)
                    for hml in range(NHMQ):
                        if NTG == 4:
                            bset = [(hml % 2) * 4 + tg for tg in range(4)]
                        else:
                            bset = [(0, 1, 2, 3)[(hml * NTG + tg) % 4] for tg in range(NTG)]
                        for tg in range(NTG):
                            em.wguard("tensor", R_pb[bset[tg]])
                        em.rguard("tensor", ks)
                        for tg in range(NTG):
                            em.rguard("tensor", R_hf[tg])
                        for k in range(KCH):
                            for tg in range(NTG):
                                T(lambda e, k=k, hml=hml, tg=tg, bank=bset[tg], koff=koff: e.matmul(
                                    pb[bank][:, :],
                                    BIG2[:, koff + k * HQ + hml * P: koff + k * HQ + (hml + 1) * P],
                                    fm(BIG3, k, tg * 512, 512),
                                    start=(k == 0), stop=(k == KCH - 1)),
                                  inc=(k == KCH - 1 and tg == NTG - 1))
                        rmark(ks, TS)
                        for tg in range(NTG):
                            rmark(R_hf[tg], TS)
                        for tg in range(NTG):
                            bank = bset[tg]
                            wmark(R_pb[bank], TS)
                            sl = (hml * NTG + tg) % 2
                            em.wguard("scalar", R_sb512[sl])
                            em.rguard("scalar", R_pb[bank])
                            S(lambda e, bank=bank, sl=sl: e.activation(
                                out=SB512[:, sl * 512:(sl + 1) * 512], in_=pb[bank][:, :],
                                func=AF.Relu))
                            rmark(R_pb[bank], SS)
                            wmark(R_sb512[sl], SS)
                            em.rguard("vector", R_sb512[sl])
                            V(lambda e, hml=hml, tg=tg, sl=sl: e.tensor_tensor(
                                out=BIG4[:, hml * TCORE + tg * 512: hml * TCORE + (tg + 1) * 512],
                                in0=SB512[:, sl * 512:(sl + 1) * 512],
                                in1=SB512[:, sl * 512:(sl + 1) * 512], op=ALU.mult))
                            rmark(R_sb512[sl], VS)
                    wmark(R_hrh, VS)
                    for t in range(NT):
                        bpair = ((4, 5), (6, 7))[t % 2]
                        em.wguard("tensor", R_pb[bpair[0]])
                        em.wguard("tensor", R_pb[bpair[1]])
                        em.rguard("tensor", R_hrh, vs_)
                        for hm in range(NHMQ):
                            for ch in range(2):
                                T(lambda e, hm=hm, t=t, ch=ch, bank=bpair[ch], voff=voff: e.matmul(
                                    pb[bank][:, :],
                                    BIG4[:, hm * TCORE + t * P: hm * TCORE + (t + 1) * P],
                                    BIG1[:, voff + hm * C + ch * 512: voff + hm * C + (ch + 1) * 512],
                                    start=(hm == 0), stop=(hm == NHMQ - 1)),
                                  inc=(hm == NHMQ - 1 and ch == 1))
                        rmark(R_hrh, TS)
                        rmark(vs_, TS)
                        for ch in range(2):
                            bank = bpair[ch]
                            wmark(R_pb[bank], TS)
                            em.rguard("vector", R_pb[bank], R_wtok)
                            em.wguard("vector", R_xs[t])
                            V(lambda e, t=t, ch=ch, bank=bank, e_=e_: e.scalar_tensor_tensor(
                                out=xs_ap(t, ch), in0=pb[bank][:, :],
                                scalar=SM[:, WTK0 + 3 * t + e_:WTK0 + 3 * t + e_ + 1],
                                in1=xs_ap(t, ch), op0=ALU.mult, op1=ALU.add))
                            rmark(R_pb[bank], VS)
                            rmark(R_wtok, VS)
                            wmark(R_xs[t], VS)
                            if second and e_ == 1 and q4 == NQ - 1:
                                em.wait("sync", VS)
                                SD(lambda e, t=t, ch=ch: e.dma_start(
                                    out=y_out[t * P:(t + 1) * P, ch * 512:(ch + 1) * 512],
                                    in_=xs_ap(t, ch)))
                                rmark(R_xs[t], DSY)
            ck(pref + "f11")

        def emit_attention():
            R_y1f = [Res("y1f%d" % g) for g in range(NTG)]
            R_qf = Res("qf")
            R_ko = Res("ko")
            R_vo = Res("vo")
            R_ka = Res("ka")
            R_va = Res("va")
            R_attw = Res("attw")
            R_kmk = [Res("kmk%d" % i) for i in range(2)]
            R_osb = [Res("osb%d" % i) for i in range(2)]
            R_ptb = [Res("ptb%d" % i) for i in range(3)]
            R_lsb = [Res("lsb"), ]
            R_pbf = R_sb512

            AW0 = 3 * TCORE
            KMS0 = AW0 + 4096
            claim("gpsimd", "B3", R_attw)
            GD(lambda e: e.dma_start(
                out=BIG3[:, AW0:AW0 + 1024].rearrange("p (k n) -> p k n", n=DK),
                in_=wq_in.rearrange("(k p) n -> p k n", p=P)))
            GD(lambda e: e.dma_start(
                out=BIG3[:, AW0 + 1024:AW0 + 2048].rearrange("p (k n) -> p k n", n=DK),
                in_=wk_in.rearrange("(k p) n -> p k n", p=P)))
            GD(lambda e: e.dma_start(
                out=BIG3[:, AW0 + 2048:AW0 + 3072].rearrange("p (k n) -> p k n", n=DK),
                in_=wva_in.rearrange("(k p) n -> p k n", p=P)))
            GD(lambda e: e.dma_start(out=BIG3[:, AW0 + 3072:AW0 + 4096], in_=woa_in))
            wmark(R_attw, DG)

            def wq_k(k):
                return BIG3[:, AW0 + k * DK:AW0 + (k + 1) * DK]

            def wk_k(k):
                return BIG3[:, AW0 + 1024 + k * DK:AW0 + 1024 + (k + 1) * DK]

            def wv_k(k):
                return BIG3[:, AW0 + 2048 + k * DK:AW0 + 2048 + (k + 1) * DK]

            WOap = BIG3[:, AW0 + 3072:AW0 + 4096]
            QF0, KO0, VO0 = 0, TCORE, 2 * TCORE

            def emit_qk_col(which, tg2):
                dst0, wfn, scale, res = ((QF0, wq_k, INV_SQRT_DK, R_qf) if which == 0
                                         else (KO0, wk_k, 1.0, R_ko))
                bank = (2, 3)[(which + tg2) % 2]
                em.wguard("tensor", R_pb[bank])
                em.rguard("tensor", R_attw, R_y1f[tg2])
                for k in range(KCH):
                    T(lambda e, k=k, tg2=tg2, bank=bank, wfn=wfn: e.matmul(
                        pb[bank][:, :], wfn(k), fm(BIG1, k, tg2 * 512, 512),
                        start=(k == 0), stop=(k == KCH - 1)), inc=(k == KCH - 1))
                wmark(R_pb[bank], TS)
                rmark(R_attw, TS)
                rmark(R_y1f[tg2], TS)
                add_owner("B3", res)
                em.rguard("scalar", R_pb[bank])
                S(lambda e, dst0=dst0, tg2=tg2, bank=bank, scale=scale: e.activation(
                    out=BIG3[:, dst0 + tg2 * 512: dst0 + (tg2 + 1) * 512],
                    in_=pb[bank][:, :], func=AF.Copy, scale=scale))
                rmark(R_pb[bank], SS)
                if tg2 == NTG - 1:
                    wmark(res, SS)
                    norm_scan(BIG3, dst0, NRM0 if which == 0 else NRM0 + 1,
                              TCORE // 512, res)

            def emit_vo_tile(t2):
                bank = (4, 5)[t2 % 2]
                em.wguard("tensor", R_pb[bank])
                em.rguard("tensor", R_attw, R_y1f[t2 // 4])
                for k in range(KCH):
                    T(lambda e, k=k, t2=t2, bank=bank: e.matmul(
                        pb[bank][:, 0:P], fm(BIG1, k, t2 * P, P), wv_k(k),
                        start=(k == 0), stop=(k == KCH - 1)), inc=(k == KCH - 1))
                wmark(R_pb[bank], TS)
                rmark(R_attw, TS)
                rmark(R_y1f[t2 // 4], TS)
                add_owner("B3", R_vo)
                em.rguard("scalar", R_pb[bank])
                S(lambda e, t2=t2, bank=bank: e.copy(
                    out=BIG3[:, VO0 + t2 * P: VO0 + (t2 + 1) * P], in_=pb[bank][:, 0:P]))
                rmark(R_pb[bank], SS)

            nq, nk = NRM0, NRM0 + 1
            em.wait("vector", DSY, smalls_dma)
            V(lambda e: e.memset(SM[0:1, nq:nq + 2], 0.0))
            em.wait("tensor", DSY, smalls_dma)

            def norm_scan(big, src_off, nmax_col, nchunks, res_r):
                for ch_ in range(nchunks):
                    sl = ch_ % 2
                    em.rguard("vector", res_r)
                    em.wguard("vector", R_pbf[sl])
                    V(lambda e, big=big, ch_=ch_, src_off=src_off, sl=sl: e.tensor_tensor(
                        out=SB512[:, sl * 512:(sl + 1) * 512],
                        in0=big[:, src_off + ch_ * 512: src_off + (ch_ + 1) * 512],
                        in1=big[:, src_off + ch_ * 512: src_off + (ch_ + 1) * 512], op=ALU.mult))
                    wmark(R_pbf[sl], VS)
                    em.wguard("tensor", R_pb[7])
                    em.rguard("tensor", R_pbf[sl])
                    T(lambda e, sl=sl: e.matmul(pb[7][0:1, :], ONESP[:, :],
                                                SB512[:, sl * 512:(sl + 1) * 512],
                                                start=True, stop=True,
                                                skip_group_check=True), inc=True)
                    wmark(R_pb[7], TS)
                    rmark(R_pbf[sl], TS)
                    em.rguard("vector", R_pb[7])
                    Vd(lambda e: e.reduce_max(SM[0:1, NRM0 + 2:NRM0 + 3], pb[7][0:1, :], axis=AX))
                    Vd(lambda e, nmax_col=nmax_col: e.tensor_tensor(
                        SM[0:1, nmax_col:nmax_col + 1], SM[0:1, nmax_col:nmax_col + 1],
                        SM[0:1, NRM0 + 2:NRM0 + 3], ALU.max))
                    rmark(R_pb[7], VS)

            a2q = []

            def a12_icb():
                if a2q:
                    a2q.pop(0)()

            for tg in range(NTG):
                for t in range(tg * 4, tg * 4 + 4):
                    em.rguard("vector", R_xs[t])
                    em.rguard("scalar", R_xs[t])
                if tg == 0:
                    claim("scalar", "B1", R_y1f[0])
                    for g2 in range(1, NTG):
                        add_owner("B1", R_y1f[g2])
                ln_group(tg, lambda tt: xs_ap(tt), BIG1, R_y1f[tg],
                         lnw_off=0, double_ln=False, save_stats=False, plain=True,
                         icb=a12_icb)
                for t in range(tg * 4, tg * 4 + 4):
                    rmark(R_xs[t], SS)
                    rmark(R_xs[t], VS)
                a2q += [lambda tg=tg, w=w: emit_qk_col(w, tg) for w in (0, 1)]
                a2q += [lambda t2=t2: emit_vo_tile(t2) for t2 in range(tg * 4, tg * 4 + 4)]
            while a2q:
                a2q.pop(0)()
            em.rguard("gpsimd", R_ko, R_vo)
            GD(lambda e: e.dma_start(out=send_d.ap()[:, 0:TCORE], in_=BIG3[:, KO0:KO0 + TCORE]))
            GD(lambda e: e.dma_start(out=send_d.ap()[:, TCORE:2 * TCORE],
                                     in_=BIG3[:, VO0:VO0 + TCORE]))
            rmark(R_ko, DG)
            rmark(R_vo, DG)
            wmark(R_send, DG)
            em.wait("gpsimd", DG)
            em.do("gpsimd", lambda e: e.collective_compute(
                "AllGather", ALU.bypass, replica_groups=[list(range(NCORES))],
                ins=[send_d.ap()], outs=[recv_d.ap()]), CCS, 1)
            wmark(R_recv, CCS)
            em.wait("gpsimd", CCS)
            claim("gpsimd", "B2", R_ka)
            rv = recv_d.ap().rearrange("(r p) m -> p r m", p=P)
            GD(lambda e: e.dma_start(
                out=BIG2[:, 0:GK].rearrange("p (r n) -> p r n", n=TCORE),
                in_=rv[:, :, 0:TCORE]))
            wmark(R_ka, DG)
            claim("gpsimd", "B4a", R_va)
            claim("gpsimd", "B4b", R_va)
            GD(lambda e: e.dma_start(
                out=BIG4[:, 0:GK].rearrange("p (r n) -> p r n", n=TCORE),
                in_=rv[:, :, TCORE:2 * TCORE]))
            wmark(R_va, DG)
            em.rguard("vector", R_ka)
            for i in range(GK // 2048):
                ksl = KMS0 + (i % 2) * 2048
                em.wguard("gpsimd", R_kmk[i % 2])
                GD(lambda e, i=i, ksl=ksl: e.dma_start(
                    out=BIG3[:, ksl:ksl + 2048], in_=kmask_in[:, i * 2048:(i + 1) * 2048]))
                wmark(R_kmk[i % 2], DG)
                em.rguard("vector", R_kmk[i % 2])
                V(lambda e, i=i, ksl=ksl: e.tensor_tensor(
                    out=BIG2[:, i * 2048:(i + 1) * 2048],
                    in0=BIG2[:, i * 2048:(i + 1) * 2048],
                    in1=BIG3[:, ksl:ksl + 2048], op=ALU.mult))
                rmark(R_kmk[i % 2], VS)
            wmark(R_ka, VS)
            em.rguard("vector", R_va)
            em.wait("vector", DSY, smalls_dma)
            for j in range(GK // P):
                V(lambda e, j=j: e.tensor_single_scalar(
                    BIG4[:, j * P:(j + 1) * P],
                    BIG4[:, j * P:(j + 1) * P],
                    VMK[:, j:j + 1], ALU.mult))
            wmark(R_va, VS)
            ck("a5")

            norm_scan(BIG2, 0, nk, GK // 512, R_ka)
            Vd(lambda e: e.tensor_tensor(SM[0:1, NRM0 + 2:NRM0 + 3], SM[0:1, nq:nq + 1],
                                        SM[0:1, nk:nk + 1], ALU.mult))
            em.wait("scalar", VS)
            Sd(lambda e: e.sqrt(out=SM[0:1, NRM0 + 3:NRM0 + 4], in_=SM[0:1, NRM0 + 2:NRM0 + 3]))
            em.wguard("scalar", R_smb)
            Sd(lambda e: e.copy(out=SMB[0:1, 63:64], in_=SM[0:1, NRM0 + 3:NRM0 + 4]))
            wmark(R_smb, SS)
            em.wait("tensor", SS)
            em.wguard("tensor", R_pb[7])
            T(lambda e: e.matmul(pb[7][:, 0:1], ONESR[0:1, :], SMB[0:1, 63:64],
                                 start=True, stop=True, skip_group_check=True), inc=True)
            wmark(R_pb[7], TS)
            em.rguard("vector", R_pb[7])
            Vd(lambda e: e.tensor_scalar_mul(SM[:, NRM0 + 5:NRM0 + 6], pb[7][:, 0:1], -1.0))
            rmark(R_pb[7], VS)
            negm_c = NRM0 + 5
            em.wait("scalar", VS)
            Sd(lambda e: e.activation(out=SM[:, NRM0 + 6:NRM0 + 7], in_=SM[:, negm_c:negm_c + 1],
                                     func=AF.Exp))
            em.wguard("scalar", R_smb)
            Sd(lambda e: e.copy(out=SMB[0:1, 62:63], in_=SM[0:1, NRM0 + 6:NRM0 + 7]))
            Sd(lambda e: e.copy(out=SM[0:1, NRM0 + 8:NRM0 + 9], in_=SMB[0:1, 62:63]))
            wmark(R_smb, SS)
            em.wait("tensor", SS)
            em.wguard("tensor", R_pb[1])
            T(lambda e: e.matmul(pb[1][:, 0:1], ONESR[0:1, :], SMB[0:1, 62:63],
                                 start=True, stop=True, skip_group_check=True), inc=True)
            wmark(R_pb[1], TS)
            em.rguard("vector", R_pb[1])
            Vd(lambda e: e.tensor_copy(out=SM[:, NRM0 + 6:NRM0 + 7], in_=pb[1][:, 0:1]))
            rmark(R_pb[1], VS)
            em.wait("vector", SS)
            em.wait("vector", DSY, smalls_dma)
            Vd(lambda e: e.tensor_tensor(SM[:, NRM0 + 7:NRM0 + 8], SM[:, NRM0 + 6:NRM0 + 7],
                                        NZERO[:, :], ALU.mult))
            wmark(R_norm, VS)
            ck("a3")

            NSC = GK // P
            for g in range(NTG):
                par = g % 2
                obank = 6 + par
                lbank = 0 + par
                qlo = g * 512
                em.wguard("tensor", R_pb[obank])
                em.wguard("tensor", R_pb[lbank])
                state = {"first": True, "idx": 0, "pend": [], "drain": False}

                def ptb_ap(sl, n=512):
                    if sl < 2:
                        return TMP2[:, sl * 512:sl * 512 + n]
                    return SB512[:, 0:n]

                def st_sub(src_big, koff, vtile, masked_ki, last):
                    idx = state["idx"]
                    state["idx"] += 1
                    sbank = (2, 3)[idx % 2]
                    sl = idx % 2
                    em.wguard("tensor", R_pb[sbank])
                    em.rguard("tensor", R_qf, R_ka if src_big is BIG2 else R_ko)
                    T(lambda e, src_big=src_big, koff=koff, sbank=sbank, qlo=qlo: e.matmul(
                        pb[sbank][:, :], src_big[:, koff:koff + P],
                        BIG3[:, QF0 + qlo: QF0 + qlo + 512], start=True, stop=True,
                        skip_group_check=True), inc=True)
                    wmark(R_pb[sbank], TS)
                    rmark(R_qf, TS)
                    rmark(R_ka if src_big is BIG2 else R_ko, TS)
                    em.rguard("scalar", R_pb[sbank], R_norm)
                    em.wguard("scalar", R_ptb[sl])
                    S(lambda e, sbank=sbank, sl=sl: e.activation(
                        out=ptb_ap(sl), in_=pb[sbank][:, :],
                        func=AF.Exp, bias=SM[:, negm_c:negm_c + 1]))
                    rmark(R_pb[sbank], SS)
                    rmark(R_norm, SS)
                    wmark(R_ptb[sl], SS)
                    if masked_ki is not None:
                        em.rguard("vector", R_ptb[sl])
                        em.wait("vector", DSY, smalls_dma)
                        V(lambda e, sl=sl, ki=masked_ki: e.tensor_tensor(
                            out=ptb_ap(sl), in0=ptb_ap(sl),
                            in1=GMK[:, ki * 512:(ki + 1) * 512], op=ALU.mult))
                        wmark(R_ptb[sl], VS)
                    state["pend"].append((sl, vtile, last))
                    if last:
                        state["drain"] = True
                    if len(state["pend"]) >= 2 or state["drain"]:
                        flush()

                def flush():
                    while state["pend"]:
                        if len(state["pend"]) < 2 and not state["drain"]:
                            return
                        sl, vtile, last = state["pend"].pop(0)
                        first = state["first"]
                        state["first"] = False
                        em.rguard("tensor", R_ptb[sl])
                        em.wait("tensor", DSY, smalls_dma)
                        T(lambda e, sl=sl, lbank=lbank, first=first, last=last: e.matmul(
                            pb[lbank][0:1, :], ONESP[:, :], ptb_ap(sl),
                            start=first, stop=last, skip_group_check=True), inc=False)
                        em.rguard("tensor", R_va if vtile[0] is BIG4 else R_vo)
                        T(lambda e, sl=sl, vtile=vtile, obank=obank, first=first, last=last: e.matmul(
                            pb[obank][:, :], vtile[0][:, vtile[1]:vtile[1] + P],
                            ptb_ap(sl),
                            start=first, stop=last, skip_group_check=True), inc=last)
                        rmark(R_ptb[sl], TS)
                        rmark(R_va if vtile[0] is BIG4 else R_vo, TS)

                for sc in range(NSC):
                    st_sub(BIG2, sc * P, (BIG4, sc * P), None, False)
                for sc in range(4 * g):
                    st_sub(BIG3, KO0 + sc * P, (BIG3, VO0 + sc * P), None, False)
                for ki in range(4):
                    st_sub(BIG3, KO0 + qlo + ki * P, (BIG3, VO0 + qlo + ki * P), ki,
                           ki == 3)
                wmark(R_pb[obank], TS)
                wmark(R_pb[lbank], TS)
                em.rguard("scalar", R_pb[lbank])
                em.wguard("scalar", R_lsb[0])
                S(lambda e, lbank=lbank: e.copy(
                    out=LSF[0:1, :], in_=pb[lbank][0:1, :]))
                wmark(R_lsb[0], SS)
                rmark(R_pb[lbank], SS)
                for tq in range(4):
                    t = g * 4 + tq
                    tpar = t % 2
                    lcol = LT0 + tpar
                    em.wait("tensor", SS)
                    em.wguard("tensor", R_pb[4 + tq % 2])
                    em.rguard("tensor", R_lsb[0])
                    T(lambda e, tq=tq, bank=4 + tq % 2: e.matmul(
                        pb[bank][:, 0:1],
                        LSF[0:1, tq * P:(tq + 1) * P],
                        SM[0:1, ONEC:ONEC + 1],
                        start=True, stop=True, skip_group_check=True), inc=True)
                    wmark(R_pb[4 + tq % 2], TS)
                    rmark(R_lsb[0], TS)
                    em.rguard("vector", R_pb[4 + tq % 2], R_norm)
                    em.wguard("vector", R_lt[tpar])
                    Vd(lambda e, bank=4 + tq % 2, lcol=lcol: e.tensor_tensor(
                        SM[:, lcol:lcol + 1], pb[bank][:, 0:1],
                        SM[:, NRM0 + 7:NRM0 + 8], ALU.subtract))
                    rmark(R_pb[4 + tq % 2], VS)
                    rmark(R_norm, VS)
                    Vd(lambda e, lcol=lcol, tpar=tpar: e.reciprocal(
                        SM[:, LT0 + 4 + tpar:LT0 + 5 + tpar], SM[:, lcol:lcol + 1]))
                    wmark(R_lt[tpar], VS)
                em.rguard("scalar", R_pb[obank])
                em.wguard("scalar", R_osb[par])
                S(lambda e, obank=obank, par=par: e.copy(
                    out=TMP2[:, 1024 + par * 512:1024 + (par + 1) * 512],
                    in_=pb[obank][:, :]))
                wmark(R_osb[par], SS)
                rmark(R_pb[obank], SS)
                for tq in range(4):
                    t = g * 4 + tq
                    tpar = t % 2
                    linv_col = LT0 + 4 + tpar
                    for half in range(2):
                        wbank = (2, 3)[half]
                        em.wguard("tensor", R_pb[wbank])
                        em.rguard("tensor", R_osb[par], R_attw)
                        T(lambda e, half=half, par=par, tq=tq, wbank=wbank: e.matmul(
                            pb[wbank][:, :],
                            TMP2[:, 1024 + par * 512 + tq * P:1024 + par * 512 + (tq + 1) * P],
                            WOap[:, half * 512:(half + 1) * 512], start=True, stop=True,
                            skip_group_check=True), inc=True)
                        wmark(R_pb[wbank], TS)
                        rmark(R_osb[par], TS)
                        rmark(R_attw, TS)
                        em.rguard("vector", R_pb[wbank], R_lt[tpar])
                        em.wguard("vector", R_xs[t])
                        V(lambda e, t=t, half=half, wbank=wbank, linv_col=linv_col: e.scalar_tensor_tensor(
                            out=xs_ap(t, half), in0=pb[wbank][:, :],
                            scalar=SM[:, linv_col:linv_col + 1],
                            in1=xs_ap(t, half), op0=ALU.mult, op1=ALU.add))
                        rmark(R_pb[wbank], VS)
                        rmark(R_lt[tpar], VS)
                        wmark(R_xs[t], VS)
            ck("a6")
            for i in range(2):
                for (s_, v_) in list(R_ptb[i].rd.values()):
                    rmark(R_tmp2[i], s_)
                if R_ptb[i].w:
                    R_tmp2[i].w = R_ptb[i].w
                for (s_, v_) in list(R_osb[i].rd.values()):
                    rmark(R_tmp2[i], s_)

        try:
            emit_ffn(second=False)
            emit_attention()
            emit_ffn(second=True)
            ck("end")
        except _Stop:
            pass

        if debug_dump or stop_after is not None:
            em.wait("sync", VS)
            em.wait("sync", SS)
            em.wait("sync", TS)
            em.wait("sync", GS)
            em.wait("sync", DG)
            SD(lambda e: e.dma_start(
                out=dxs_out.rearrange("(t p) c -> p t c", p=P),
                in_=XS[:, :].rearrange("p (t c) -> p t c", c=C)))
            SD(lambda e: e.dma_start(out=dsm_out, in_=SM[:, :]))
            em.wait("gpsimd", DSY)
            for i, big in enumerate((BIG1, BIG2, BIG3, BIG4)):
                GD(lambda e, i=i, big=big: e.dma_start(out=db_out[i], in_=big[:, :]))

        em.wait("sync", DSY)
        em.wait("gpsimd", DG)

        for s in em.sems:
            s.h = st.enter_context(nc.semaphore(s.name))
        block = st.enter_context(nc.Block())

        def replay(engobj, eng):
            for op in em.ops[eng]:
                if op[0] == "wait":
                    engobj.wait_ge(op[1].h, op[2])
                else:
                    fn, sem, inc = op[1], op[2], op[3]
                    ins = fn(engobj)
                    if sem is not None:
                        ins.then_inc(sem.h, inc)

        @block.sync
        def _(e):
            replay(e, "sync")

        @block.gpsimd
        def _(e):
            replay(e, "gpsimd")

        @block.scalar
        def _(e):
            replay(e, "scalar")

        @block.vector
        def _(e):
            replay(e, "vector")

        @block.tensor
        def _(e):
            replay(e, "tensor")

    em.simulate()
    return nc, em



def host_inputs(full, core, NT=16):
    TCORE = NT * P
    GK = 8 * TCORE
    b, g = core // 4, core % 4
    bf = ml_dtypes.bfloat16
    f32 = lambda k: np.ascontiguousarray(np.asarray(full[k], dtype=np.float32))
    d = {}
    x = f32("L0_attn_x")
    d["x"] = np.ascontiguousarray(x[b, g * TCORE:(g + 1) * TCORE, :])
    d["wq"] = f32("L0_attn_Wq")
    d["wk"] = f32("L0_attn_Wk")
    d["wva"] = f32("L0_attn_Wv")
    d["woa"] = f32("L0_attn_Wo")
    d["wr"] = f32("L0_ffn_Wr")
    d["wvf"] = f32("L0_ffn_Wv")
    d["wof"] = f32("L0_ffn_Wo")
    d["ws"] = f32("L0_ffn_Ws")
    d["w1"] = f32("L0_ffn_W1")
    d["w2"] = f32("L0_ffn_W2")
    d["w3"] = f32("L0_ffn_W3")
    d["krw"] = f32("L0_ffn_K_rwkv")
    d["vrw"] = f32("L0_ffn_V_rwkv")
    wsm = np.concatenate([
        f32("L0_ffn_conf_rwkv").T,
        f32("L0_ffn_conf_trans")[:, None],
        f32("L0_ffn_w_diff")[:, None],
        f32("L0_ffn_W_aff")], axis=1)
    d["wsmT"] = np.ascontiguousarray(wsm.T)
    lnc = np.zeros((P, 32), np.float32)
    for i, nm in enumerate(["ln1_w", "ln1_b", "ln2_w", "ln2_b"]):
        lnc[:, i * 8:(i + 1) * 8] = f32("L0_ffn_" + nm).reshape(8, P).T
    d["lnc"] = lnc
    d["sh3"] = np.ascontiguousarray(
        np.broadcast_to(f32("L0_ffn_capital_shares"), (P, 3)))
    d["idn"] = np.eye(P, dtype=np.float32).astype(bf)
    d["tri"] = np.tril(np.ones((P, P), np.float32)).astype(bf)
    gm = np.zeros((P, 4 * 512), np.float32)
    for ki in range(4):
        for k in range(P):
            gm[k, ki * 512 + ki * P + k:(ki + 1) * 512] = 1.0
    d["gmask"] = np.ascontiguousarray(gm).astype(bf)
    d["onesp"] = np.ones((P, 1), np.float32).astype(bf)
    d["onesr"] = np.ones((1, P), np.float32).astype(bf)
    km = np.zeros((GK,), np.float32)
    km[4 * b * TCORE: (4 * b + g) * TCORE] = 1.0
    d["kmask"] = np.ascontiguousarray(np.broadcast_to(km, (P, GK))).astype(bf)
    d["vmask"] = np.ascontiguousarray(km.reshape(GK // P, P).T)
    d["nzero"] = np.full((P, 1), float(GK - g * TCORE), np.float32)
    return d


def kernel(**inputs):
    NT = 16
    nc, _em = build_program(NT=NT)
    in_maps = [host_inputs(inputs, r, NT) for r in range(NCORES)]
    res = run_bass_kernel_spmd(nc, in_maps, core_ids=list(range(NCORES)))
    TCORE = NT * P
    out = np.zeros((2, 4 * TCORE, C), np.float32)
    for r in range(NCORES):
        b, g = r // 4, r % 4
        out[b, g * TCORE:(g + 1) * TCORE, :] = res.results[r]["y"]
    return out


# version 09 (final)
# speedup vs baseline: 1.0633x; 1.0068x over previous
import ast as _ast
import inspect
import re as _re
import numpy as np
import ml_dtypes
from contextlib import ExitStack

_PB_EXPR_CACHE = {}


def _pacc_of(fn, eng):
    code = fn.__code__
    if code in _PB_EXPR_CACHE:
        exprs, has_pb = _PB_EXPR_CACHE[code]
    else:
        try:
            src = inspect.getsource(fn)
        except Exception:
            src = ""
        exprs = _re.findall(r"pb\[([^\]]+)\]", src)
        has_pb = bool(exprs) or ("pb[" in src)
        _PB_EXPR_CACHE[code] = (exprs, has_pb)
    if not has_pb:
        return None
    names = code.co_varnames[:code.co_argcount]
    dflts = fn.__defaults__ or ()
    ctx = dict(zip(names[len(names) - len(dflts):], dflts))
    def _aeval(node):
        if isinstance(node, _ast.Expression):
            return _aeval(node.body)
        if isinstance(node, _ast.Constant):
            return node.value
        if isinstance(node, _ast.Name):
            return ctx[node.id]
        if isinstance(node, _ast.BinOp):
            a, b = _aeval(node.left), _aeval(node.right)
            if isinstance(node.op, _ast.Add):
                return a + b
            if isinstance(node.op, _ast.Sub):
                return a - b
            if isinstance(node.op, _ast.Mult):
                return a * b
            if isinstance(node.op, _ast.Mod):
                return a % b
            if isinstance(node.op, _ast.FloorDiv):
                return a // b
        raise ValueError("expr")

    banks = set()
    for ex in exprs:
        try:
            banks.add(int(_aeval(_ast.parse(ex, mode="eval"))))
        except Exception:
            raise RuntimeError("cannot resolve psum bank expr %r in %s op" % (ex, eng))
    if not banks:
        return None
    if eng == "tensor":
        kind = "w"
    elif eng == "scalar":
        srcline = inspect.getsource(fn)
        kind = "w" if ("square(" in srcline and "out=pb" in srcline) else "r"
    else:
        kind = "r"
    return [(b, kind) for b in sorted(banks)]

import concourse.bass as bass
import concourse.mybir as mybir
from concourse.bass_utils import run_bass_kernel_spmd

F32 = mybir.dt.float32
BF16 = mybir.dt.bfloat16
AF = mybir.ActivationFunctionType
ALU = mybir.AluOpType
AX = mybir.AxisListType.X

P = 128
C = 1024
KCH = 8
H = 4096
DK = 128
NCORES = 8
EPS = 1e-5
INV_SQRT_DK = float(1.0 / np.sqrt(128.0))

ENGS = ("sync", "gpsimd", "scalar", "vector", "tensor")


class Sem:
    __slots__ = ("name", "owner", "c", "h")

    def __init__(self, name, owner=None):
        self.name = name
        self.owner = owner
        self.c = 0
        self.h = None


class Res:
    __slots__ = ("name", "w", "rd")

    def __init__(self, name):
        self.name = name
        self.w = None
        self.rd = {}


class Em:
    def __init__(self):
        self.ops = {e: [] for e in ENGS}
        self.hw = {e: {} for e in ENGS}
        self.sems = []
        self.label = "?"  

    def sem(self, name, owner=None):
        s = Sem(name, owner)
        self.sems.append(s)
        return s

    def wait(self, eng, sem, val=None):
        val = sem.c if val is None else val
        if val <= 0:
            return
        if sem.owner == eng:
            return
        if self.hw[eng].get(sem.name, -1) >= val:
            return
        self.hw[eng][sem.name] = val
        self.ops[eng].append(("wait", sem, val))

    def do(self, eng, fn, sem=None, inc=1, pacc=None):
        if pacc is None and eng in ("tensor", "scalar", "vector"):
            pacc = _pacc_of(fn, eng)
        if sem is not None:
            sem.c += inc
            self.ops[eng].append(("do", fn, sem, inc, pacc, self.label))
            return sem.c
        self.ops[eng].append(("do", fn, None, 0, pacc, self.label))
        return None

    def rguard(self, eng, *ress):
        for r in ress:
            if r.w:
                self.wait(eng, r.w[0], r.w[1])

    def wguard(self, eng, *ress):
        for r in ress:
            if r.w:
                self.wait(eng, r.w[0], r.w[1])
            for (s, v) in list(r.rd.values()):
                self.wait(eng, s, v)

    def simulate(self):
        pos = {e: 0 for e in ENGS}
        cnt = {s.name: 0 for s in self.sems}
        progressed = True
        while progressed:
            progressed = False
            for e in ENGS:
                while pos[e] < len(self.ops[e]):
                    op = self.ops[e][pos[e]]
                    if op[0] == "wait":
                        if cnt[op[1].name] < op[2]:
                            break
                    else:
                        sem, inc = op[2], op[3]
                        if sem is not None:
                            cnt[sem.name] += inc
                    pos[e] += 1
                    progressed = True
        stuck = {e: (pos[e], len(self.ops[e])) for e in ENGS if pos[e] < len(self.ops[e])}
        if stuck:
            lines = []
            for e, (p_, n) in stuck.items():
                op = self.ops[e][p_]
                lines.append(f"  {e}: op {p_}/{n} waiting {op[1].name} >= {op[2]} (now {cnt[op[1].name]})")
            raise RuntimeError("DEADLOCK in simulated schedule:\n" + "\n".join(lines))

    def counts(self):
        return {e: len(self.ops[e]) for e in ENGS}

    def check_psum(self):
        inc_seq = {s.name: [] for s in self.sems}
        cum = {s.name: 0 for s in self.sems}
        pos = {e: 0 for e in ENGS}
        cnt = {s.name: 0 for s in self.sems}
        clocks = {e: [] for e in ENGS}
        cur = {e: {f: -1 for f in ENGS} for e in ENGS}
        sem_src = {s.name: [] for s in self.sems}
        order = []
        progressed = True
        while progressed:
            progressed = False
            for e in ENGS:
                while pos[e] < len(self.ops[e]):
                    op = self.ops[e][pos[e]]
                    if op[0] == "wait":
                        _, sem, val = op
                        if cnt[sem.name] < val:
                            break
                        for (cafter, f, j) in sem_src[sem.name]:
                            if cafter >= val:
                                src_clock = clocks[f][j]
                                for g in ENGS:
                                    if src_clock.get(g, -1) > cur[e][g]:
                                        cur[e][g] = src_clock[g]
                                break
                        clocks[e].append(dict(cur[e]))
                    else:
                        sem, inc = op[2], op[3]
                        cur[e][e] = pos[e]
                        clocks[e].append(dict(cur[e]))
                        if sem is not None:
                            cnt[sem.name] += inc
                            sem_src[sem.name].append((cnt[sem.name], e, pos[e]))
                        order.append((e, pos[e], op[4], op[5]))
                    pos[e] += 1
                    progressed = True
        def srcof(e, i):
            fn = self.ops[e][i][1]
            try:
                return inspect.getsource(fn).strip().split(chr(10))[0][:90] + " @%d" % fn.__code__.co_firstlineno
            except Exception:
                return "?"
        latest = {}
        bad = []
        for (e, i, pacc, label) in order:
            if not pacc:
                continue
            for (bank, kind) in (pacc if isinstance(pacc, list) else [pacc]):
                d = latest.setdefault(bank, {})
                myclock = clocks[e][i]
                for f, (j, k2, lb2) in d.items():
                    if f == e:
                        continue
                    if kind == "r" and k2 == "r":
                        continue
                    if myclock.get(f, -1) < j:
                        bad.append((bank, f, j, k2, e, i, kind))
                d[e] = (i, kind, label)
        if bad:
            for b in bad[:12]:
                print("PSUM RACE bank %d: %s[%d](%s): %s" % (b[0], b[1], b[2], b[3], srcof(b[1], b[2])))
                print("   unordered vs %s[%d](%s): %s" % (b[4], b[5], b[6], srcof(b[4], b[5])))
            raise RuntimeError("psum races: %d" % len(bad))


def rmark(res, sem):
    res.rd[id(sem)] = (sem, sem.c)


def wmark(res, sem):
    res.w = (sem, sem.c)
    res.rd = {}


class _Stop(Exception):
    pass


def build_program(NT=16, stop_after=None, debug_dump=False):
    TCORE = NT * P
    NTG = NT // 4
    GK = 8 * TCORE
    NKC = GK // 512
    HHALF = H // 2
    HQ = H // 4
    NHMQ = HQ // P

    nc = bass.Bass("TRN2", target_bir_lowering=False, debug=False)
    em = Em()

    def ck(name):
        if stop_after is not None and name == stop_after:
            raise _Stop()

    def din(name, shape, dt=F32):
        return nc.dram_tensor(name, shape, dt, kind="ExternalInput").ap()

    x_in = din("x", [TCORE, C])
    wq_in = din("wq", [C, DK])
    wk_in = din("wk", [C, DK])
    wva_in = din("wva", [C, DK])
    woa_in = din("woa", [DK, C])
    wr_in = din("wr", [C, C])
    wvf_in = din("wvf", [C, C])
    wof_in = din("wof", [C, C])
    ws_in = din("ws", [C, C])
    w1_in = din("w1", [C, C])
    w2_in = din("w2", [C, C])
    w3_in = din("w3", [C, C])
    krw_in = din("krw", [2, C, H])
    vrw_in = din("vrw", [2, H, C])
    wsmT_in = din("wsmT", [7, C])
    lnc_in = din("lnc", [P, 32])
    sh3_in = din("sh3", [P, 3])
    idn_in = din("idn", [P, P], BF16)
    gmask_in = din("gmask", [P, 4 * 512], BF16)
    onesp_in = din("onesp", [P, 1], BF16)
    onesr_in = din("onesr", [1, P], BF16)
    kmask_in = din("kmask", [P, GK], BF16)
    vmask_in = din("vmask", [P, GK // P])
    nzero_in = din("nzero", [P, 1])

    y_out = nc.dram_tensor("y", [TCORE, C], F32, kind="ExternalOutput").ap()

    if debug_dump or stop_after is not None:
        dxs_out = nc.dram_tensor("dxs", [TCORE, C], F32, kind="ExternalOutput").ap()
        db_out = [nc.dram_tensor("db%d" % i, [P, 16 * C], F32, kind="ExternalOutput").ap()
                  for i in range(4)]
        dsm_out = nc.dram_tensor("dsm", [P, 248], F32, kind="ExternalOutput").ap()
    send_d = nc.dram_tensor("kvsend", [P, 2 * TCORE + 16], BF16, kind="Internal")
    recv_d = nc.dram_tensor("kvrecv", [NCORES * P, 2 * TCORE + 16], BF16, kind="Internal")

    with ExitStack() as st:
        def sb(name, shape, dt):
            return st.enter_context(nc.sbuf_tensor(name, shape, dt))

        XS = sb("XS", [P, NT * C], F32)
        BIG1 = sb("BIG1", [P, 16 * C], BF16)
        BIG2 = sb("BIG2", [P, 16 * C], BF16)
        BIG3 = sb("BIG3", [P, 16 * C], BF16)
        BIG4 = sb("BIG4", [P, 16 * C], BF16)
        TMP2 = sb("TMP2", [P, 2 * C], BF16)
        SB512 = sb("SB512", [P, 2 * 512], BF16)
        SM = sb("SM", [P, 248], F32)
        SMB = sb("SMB", [P, 64], BF16)
        STG = sb("STG", [1, 16], BF16)
        LNC = sb("LNC", [P, 32], F32)
        SH3 = sb("SH3", [P, 3], F32)
        IDN = sb("IDN", [P, P], BF16)
        GMK = sb("GMK", [P, 4 * 512], BF16)
        ONESP = sb("ONESP", [P, 1], BF16)
        ONESR = sb("ONESR", [1, P], BF16)
        VMK = sb("VMK", [P, GK // P], F32)
        NZERO = sb("NZERO", [P, 1], F32)
        LSF = sb("LSF", [1, 512], F32)

        pb = [st.enter_context(nc.psum_tensor("pb%d" % i, [P, 512], F32)) for i in range(8)]

        TS = em.sem("TS", "tensor")
        SS = em.sem("SS", "scalar")
        VS = em.sem("VS", "vector")
        GS = em.sem("GS", "gpsimd")
        DG = em.sem("DG", None)
        DSY = em.sem("DSY", None)
        CCS = em.sem("CCS", None)

        R_xs = [Res("xs%d" % t) for t in range(NT)]
        R_pb = [Res("pb%d" % i) for i in range(8)]
        R_tmp2 = [Res("tmp2_%d" % i) for i in range(2)]
        R_sb512 = [Res("sb512_%d" % i) for i in range(2)]
        R_smb = Res("smb")
        R_wt7 = Res("wt7")
        R_wtok = Res("wtok")
        R_sstat = Res("sstat")
        R_norm = Res("norm")
        R_send = Res("send")
        R_recv = Res("recv")
        R_lnst = [Res("lnst%d" % i) for i in range(2)]
        R_bid = [Res("bid%d" % i) for i in range(2)]
        R_lt = [Res("lt%d" % i) for i in range(2)]

        region_owner = {"B1": [], "B2": [], "B3": [], "B4a": [], "B4b": []}

        def claim(eng, region, new_res):
            for r in region_owner[region]:
                em.wguard(eng, r)
            region_owner[region] = [new_res]

        def add_owner(region, res):
            if res not in region_owner[region]:
                region_owner[region].append(res)

        SST0 = 0
        WTK0 = 32
        LNST0 = 80
        NRM0 = 176
        ONEC = 188
        BID0 = 190
        LT0 = 238

        def T(fn, inc=False):
            return em.do("tensor", fn, TS if inc else None)

        def S(fn):
            return em.do("scalar", fn, SS, 1)

        def V(fn):
            return em.do("vector", fn, VS, 1)

        def Vd(fn):
            em.do("vector", fn)
            return em.do("vector", lambda e: e.drain(), VS, 1)

        def Sd(fn):
            em.do("scalar", fn)
            return em.do("scalar", lambda e: e.drain(), SS, 1)

        def G(fn):
            return em.do("gpsimd", fn, GS, 1)

        def GD(fn):
            return em.do("gpsimd", fn, DG, 16)

        def SD(fn):
            return em.do("sync", fn, DSY, 16)

        def xs_ap(t, half=None):
            if half is None:
                return XS[:, t * C:(t + 1) * C]
            return XS[:, t * C + half * 512: t * C + (half + 1) * 512]

        def fm(big, k, lo, n):
            return big[:, k * TCORE + lo: k * TCORE + lo + n]

        SD(lambda e: e.dma_start(out=XS[:, :].rearrange("p (t c) -> p t c", c=C),
                                 in_=x_in.rearrange("(t p) c -> p t c", p=P)))
        for t in range(NT):
            wmark(R_xs[t], DSY)
        SD(lambda e: e.dma_start(out=LNC[:, :], in_=lnc_in))
        SD(lambda e: e.dma_start(out=SH3[:, :], in_=sh3_in))
        SD(lambda e: e.dma_start(out=IDN[:, :], in_=idn_in))
        SD(lambda e: e.dma_start(out=GMK[:, :], in_=gmask_in))
        SD(lambda e: e.dma_start(out=ONESP[:, :], in_=onesp_in))
        SD(lambda e: e.dma_start(out=ONESR[:, :], in_=onesr_in))
        SD(lambda e: e.dma_start(out=VMK[:, :], in_=vmask_in))
        SD(lambda e: e.dma_start(out=NZERO[:, :], in_=nzero_in))
        smalls_dma = DSY.c

        GD(lambda e: e.dma_start(out=TMP2[0:7, 0:C], in_=wsmT_in))
        wmark(R_wt7, DG)
        wmark(R_tmp2[0], DG)

        em.wait("vector", DSY, smalls_dma)
        V(lambda e: e.memset(SM[:, ONEC:ONEC + 1], 1.0))

        em.rguard("tensor", R_wt7)
        em.wait("tensor", DSY, smalls_dma)
        for c_ in range(KCH):
            T(lambda e, c_=c_: e.matmul(pb[6][:, c_ * 8:c_ * 8 + 7],
                                        TMP2[0:7, c_ * P:(c_ + 1) * P],
                                        IDN[0:7, 0:7], start=True, stop=True),
              inc=(c_ == KCH - 1))
        rmark(R_tmp2[0], TS)
        wmark(R_pb[6], TS)
        em.rguard("scalar", R_pb[6])
        for c_ in range(KCH):
            S(lambda e, c_=c_: e.copy(out=SMB[:, c_ * 8:c_ * 8 + 7],
                                      in_=pb[6][:, c_ * 8:c_ * 8 + 7]))
        rmark(R_pb[6], SS)
        wmark(R_smb, SS)

        R_ws = [Res("wsA"), Res("wsB")]

        def stage_w(src_ap, slot):
            off = slot * 8 * C
            r = R_ws[slot]
            claim("gpsimd", "B4a" if slot == 0 else "B4b", r)
            GD(lambda e, off=off, src_ap=src_ap: e.dma_start(
                out=BIG4[:, off:off + 8 * C].rearrange("p (k n) -> p k n", n=C),
                in_=src_ap.rearrange("(k p) n -> p k n", p=P)))
            wmark(r, DG)
            return (lambda k, off=off: BIG4[:, off + k * C: off + (k + 1) * C]), r

        def ln_group(tg, src_fn, dst_big, dst_res, lnw_off, double_ln, save_stats,
                     plain=False, icb=None, tile_cb=None):
            par = tg % 2
            b = LNST0 + par * 48
            SUMc, SQAc, SQBc, MMc = b, b + 4, b + 8, b + 12
            EX2c, INVc, INV2c = b + 16, b + 20, b + 24
            SCc, S2Cc, CBc, NBc = b + 28, b + 32, b + 36, b + 40
            tiles = list(range(tg * 4, tg * 4 + 4))
            em.wguard("vector", R_lnst[par])
            for j, t in enumerate(tiles):
                src = src_fn(t)
                V(lambda e, src=src, j=j: e.reduce_sum(SM[:, SUMc + j:SUMc + j + 1], src, axis=AX))
                em.wguard("scalar", R_pb[6])
                em.wguard("scalar", R_pb[7])
                for hb in range(2):
                    em.do("scalar", lambda e, src=src, hb=hb: e.square(
                        out=pb[6 + hb][:, :], in_=src[:, hb * 512:(hb + 1) * 512]),
                          pacc=[(6 + hb, "w")])
                em.do("scalar", lambda e: e.drain(), SS, 1)
                wmark(R_pb[6], SS)
                wmark(R_pb[7], SS)
                em.rguard("vector", R_pb[6])
                em.rguard("vector", R_pb[7])
                V(lambda e, j=j: e.reduce_sum(SM[:, SQAc + j:SQAc + j + 1], pb[6][:, :], axis=AX))
                V(lambda e, j=j: e.reduce_sum(SM[:, SQBc + j:SQBc + j + 1], pb[7][:, :], axis=AX))
                rmark(R_pb[6], VS)
                rmark(R_pb[7], VS)
                if icb:
                    icb()
            em.do("vector", lambda e: e.drain())
            Vd(lambda e: e.tensor_tensor(SM[:, SQAc:SQAc + 4], SM[:, SQAc:SQAc + 4],
                                         SM[:, SQBc:SQBc + 4], ALU.add))
            Vd(lambda e: e.tensor_scalar_mul(SM[:, MMc:MMc + 4], SM[:, SUMc:SUMc + 4], 1.0 / C))
            Vd(lambda e: e.tensor_scalar_mul(SM[:, EX2c:EX2c + 4], SM[:, SQAc:SQAc + 4], 1.0 / C))
            Vd(lambda e: e.tensor_tensor(SM[:, SQBc:SQBc + 4], SM[:, MMc:MMc + 4],
                                         SM[:, MMc:MMc + 4], ALU.mult))
            Vd(lambda e: e.tensor_tensor(SM[:, EX2c:EX2c + 4], SM[:, EX2c:EX2c + 4],
                                         SM[:, SQBc:SQBc + 4], ALU.subtract))
            Vd(lambda e: e.tensor_scalar_add(SM[:, SQBc:SQBc + 4], SM[:, EX2c:EX2c + 4], EPS))
            Vd(lambda e: e.reciprocal(SM[:, INVc:INVc + 4], SM[:, SQBc:SQBc + 4]))
            if double_ln:
                Vd(lambda e: e.tensor_tensor(SM[:, SQAc:SQAc + 4], SM[:, EX2c:EX2c + 4],
                                             SM[:, INVc:INVc + 4], ALU.mult))
                Vd(lambda e: e.tensor_scalar_add(SM[:, SQAc:SQAc + 4], SM[:, SQAc:SQAc + 4], EPS))
                Vd(lambda e: e.reciprocal(SM[:, INV2c:INV2c + 4], SM[:, SQAc:SQAc + 4]))
            em.wait("scalar", VS)
            if double_ln:
                Sd(lambda e: e.sqrt(out=SM[:, SCc:SCc + 8], in_=SM[:, INVc:INVc + 8]))
            else:
                Sd(lambda e: e.sqrt(out=SM[:, SCc:SCc + 4], in_=SM[:, INVc:INVc + 4]))
            em.wait("vector", SS)
            if double_ln:
                Vd(lambda e: e.tensor_tensor(SM[:, CBc:CBc + 4], SM[:, SCc:SCc + 4],
                                             SM[:, S2Cc:S2Cc + 4], ALU.mult))
            else:
                Vd(lambda e: e.tensor_copy(out=SM[:, CBc:CBc + 4], in_=SM[:, SCc:SCc + 4]))
            Vd(lambda e: e.scalar_tensor_tensor(
                out=SM[:, NBc:NBc + 4], in0=SM[:, MMc:MMc + 4], scalar=-1.0,
                in1=SM[:, CBc:CBc + 4], op0=ALU.mult, op1=ALU.mult))
            if save_stats:
                em.wguard("vector", R_sstat) if tg == 0 else None
                Vd(lambda e, tg=tg: e.tensor_copy(out=SM[:, SST0 + tg * 4:SST0 + tg * 4 + 4],
                                                  in_=SM[:, SCc:SCc + 4]))
                Vd(lambda e, tg=tg: e.scalar_tensor_tensor(
                    out=SM[:, SST0 + 16 + tg * 4:SST0 + 16 + tg * 4 + 4],
                    in0=SM[:, MMc:MMc + 4], scalar=-1.0,
                    in1=SM[:, SCc:SCc + 4], op0=ALU.mult, op1=ALU.mult))
                wmark(R_sstat, VS)
            wmark(R_lnst[par], VS)
            vstat = VS.c
            for j, t in enumerate(tiles):
                sl = t % 2
                src = src_fn(t)
                em.wguard("scalar", R_tmp2[sl])
                em.wait("scalar", VS, vstat)
                S(lambda e, src=src, sl=sl, j=j: e.activation(
                    out=TMP2[:, sl * C:(sl + 1) * C], in_=src, func=AF.Identity,
                    bias=SM[:, NBc + j:NBc + j + 1], scale=SM[:, CBc + j:CBc + j + 1]))
                wmark(R_tmp2[sl], SS)
                rmark(R_lnst[par], SS)
                for hb in range(2):
                    bank = 6 + hb
                    em.wguard("tensor", R_pb[bank])
                    em.rguard("tensor", R_tmp2[sl])
                    em.wait("tensor", DSY, smalls_dma)
                    for jj in range(4):
                        cc = hb * 4 + jj
                        T(lambda e, cc=cc, jj=jj, bank=bank, sl=sl: e.matmul(
                            pb[bank][:, jj * P:(jj + 1) * P],
                            TMP2[:, sl * C + cc * P: sl * C + (cc + 1) * P],
                            IDN[:, :], start=True, stop=True), inc=(jj == 3))
                    wmark(R_pb[bank], TS)
                    rmark(R_tmp2[sl], TS)
                    em.rguard("scalar", R_pb[bank])
                    for jj in range(4):
                        cc = hb * 4 + jj
                        if plain:
                            S(lambda e, cc=cc, jj=jj, bank=bank, t=t: e.copy(
                                out=fm(dst_big, cc, t * P, P), in_=pb[bank][:, jj * P:(jj + 1) * P]))
                        else:
                            S(lambda e, cc=cc, jj=jj, bank=bank, t=t: e.activation(
                                out=fm(dst_big, cc, t * P, P), in_=pb[bank][:, jj * P:(jj + 1) * P],
                                func=AF.Identity, bias=LNC[:, lnw_off + 8 + cc:lnw_off + 9 + cc],
                                scale=LNC[:, lnw_off + cc:lnw_off + 1 + cc]))
                    rmark(R_pb[bank], SS)
                if tile_cb:
                    tile_cb(t)
                if icb:
                    icb()
            wmark(dst_res, SS)

        def ccmm_fm(wslot, in_big, in_res_list, out_big, out_res, kind, other_big=None,
                    banks=None, mcb=None):
            wfn, wres = wslot
            for m in range(KCH):
                if mcb is not None:
                    mcb(m)
                if NTG == 4:
                    bset = [(m % 2) * 4 + tg for tg in range(4)] if banks is None else \
                           [banks[tg % len(banks)] for tg in range(4)]
                else:
                    bset = [(2, 3)[(m * NTG + tg) % 2] for tg in range(NTG)]
                for tg in range(NTG):
                    em.wguard("tensor", R_pb[bset[tg]])
                em.rguard("tensor", wres, *in_res_list)
                for k in range(KCH):
                    for tg in range(NTG):
                        bank = bset[tg]
                        T(lambda e, k=k, m=m, tg=tg, bank=bank: e.matmul(
                            pb[bank][:, :], wfn(k)[:, m * P:(m + 1) * P],
                            fm(in_big, k, tg * 512, 512),
                            start=(k == 0), stop=(k == KCH - 1)),
                          inc=(k == KCH - 1 and tg == NTG - 1))
                rmark(wres, TS)
                for rr in in_res_list:
                    rmark(rr, TS)
                for tg in range(NTG):
                    bank = bset[tg]
                    wmark(R_pb[bank], TS)
                    if kind == "mult":
                        em.rguard("vector", R_pb[bank])
                        V(lambda e, m=m, tg=tg, bank=bank: e.tensor_tensor(
                            out=fm(out_big, m, tg * 512, 512),
                            in0=fm(other_big, m, tg * 512, 512),
                            in1=pb[bank][:, :], op=ALU.mult))
                        rmark(R_pb[bank], VS)
                    else:
                        em.rguard("scalar", R_pb[bank])
                        if kind == "copy":
                            S(lambda e, m=m, tg=tg, bank=bank: e.copy(
                                out=fm(out_big, m, tg * 512, 512), in_=pb[bank][:, :]))
                        else:
                            S(lambda e, m=m, tg=tg, bank=bank: e.activation(
                                out=fm(out_big, m, tg * 512, 512), in_=pb[bank][:, :],
                                func=AF.Sigmoid))
                        rmark(R_pb[bank], SS)
            wmark(out_res, VS if kind == "mult" else SS)

        def ccmm_tok(wslot, in_big, in_res_list, evac, banks=((4, 5), (6, 7))):
            wfn, wres = wslot
            for t in range(NT):
                bpair = banks[t % 2]
                em.wguard("tensor", R_pb[bpair[0]])
                em.wguard("tensor", R_pb[bpair[1]])
                em.rguard("tensor", wres, *in_res_list)
                for k in range(KCH):
                    for half in range(2):
                        T(lambda e, k=k, t=t, half=half, bank=bpair[half]: e.matmul(
                            pb[bank][:, :], fm(in_big, k, t * P, P),
                            wfn(k)[:, half * 512:(half + 1) * 512],
                            start=(k == 0), stop=(k == KCH - 1)),
                          inc=(k == KCH - 1 and half == 1))
                rmark(wres, TS)
                for rr in in_res_list:
                    rmark(rr, TS)
                for half in range(2):
                    wmark(R_pb[bpair[half]], TS)
                    evac(t, half, bpair[half])

        def emit_ffn(second):
            pref = "2" if second else "1"
            R_xlnf = [Res("xlnf%d_%d" % (second, g)) for g in range(NTG)]
            R_stf = Res("stf%d" % second)
            R_sgf = Res("sgf%d" % second)
            R_xf = [Res("xf%d_%d" % (second, t)) for t in range(NT)]
            R_hf = [Res("hf%d_%d" % (second, g)) for g in range(NTG)]
            R_sg2f = Res("sg2f%d" % second)
            R_hrh = Res("hrh%d" % second)
            R_ks = [Res("ks%d_%d" % (second, i)) for i in range(2)]
            R_vst = [Res("vst%d_%d" % (second, i)) for i in range(2)]

            ws_slot = stage_w(ws_in, 0)

            def emit_f2_col(m, tg2):
                bank = (2, 3, 4, 5)[(m + tg2 * KCH) % 4]
                em.wguard("tensor", R_pb[bank])
                em.rguard("tensor", ws_slot[1], R_xlnf[tg2])
                for k in range(KCH):
                    T(lambda e, k=k, m=m, tg2=tg2, bank=bank: e.matmul(
                        pb[bank][:, :], ws_slot[0](k)[:, m * P:(m + 1) * P],
                        fm(BIG1, k, tg2 * 512, 512),
                        start=(k == 0), stop=(k == KCH - 1)), inc=(k == KCH - 1))
                wmark(R_pb[bank], TS)
                rmark(ws_slot[1], TS)
                rmark(R_xlnf[tg2], TS)
                em.rguard("scalar", R_pb[bank])
                S(lambda e, m=m, tg2=tg2, bank=bank: e.copy(
                    out=fm(BIG2, m, tg2 * 512, 512), in_=pb[bank][:, :]))
                rmark(R_pb[bank], SS)

            colq = []

            def f12_icb():
                n = 2 if len(colq) > 4 else 1
                for _ in range(n):
                    if colq:
                        emit_f2_col(*colq.pop(0))

            for tg in range(NTG):
                for t in range(tg * 4, tg * 4 + 4):
                    em.rguard("vector", R_xs[t])
                    em.rguard("scalar", R_xs[t])
                if tg == 0:
                    claim("scalar", "B1", R_xlnf[0])
                    for g2 in range(1, NTG):
                        add_owner("B1", R_xlnf[g2])
                    claim("scalar", "B2", R_stf)
                ln_group(tg, lambda tt: xs_ap(tt), BIG1, R_xlnf[tg],
                         lnw_off=0, double_ln=True, save_stats=True, icb=f12_icb)
                for t in range(tg * 4, tg * 4 + 4):
                    rmark(R_xs[t], SS)
                    rmark(R_xs[t], VS)
                colq += [(m, tg) for m in range(KCH)]
            while colq:
                emit_f2_col(*colq.pop(0))
            wmark(R_stf, SS)
            ck(pref + "f2")

            wr_slot = stage_w(wr_in, 1)
            claim("scalar", "B3", R_sgf)
            ccmm_fm(wr_slot, BIG1, R_xlnf, BIG3, R_sgf, "sigmoid")
            ck(pref + "f3")
            wv_slot = stage_w(wvf_in, 0)
            ccmm_fm(wv_slot, BIG1, R_xlnf, BIG3, R_sgf, "mult", other_big=BIG3)
            ck(pref + "f4")
            wo_slot = stage_w(wof_in, 1)

            def evac_f5(t, half, bank):
                if t == 0 and half == 0:
                    claim("scalar", "B1", R_xf[0])
                    for t2 in range(1, NT):
                        add_owner("B1", R_xf[t2])
                em.rguard("scalar", R_sstat)
                em.rguard("scalar", R_xs[t])
                S(lambda e, t=t, half=half: e.activation(
                    out=BIG1[:, t * C + half * 512: t * C + (half + 1) * 512],
                    in_=xs_ap(t, half), func=AF.Identity,
                    bias=SM[:, SST0 + 16 + t:SST0 + 17 + t],
                    scale=SM[:, SST0 + t:SST0 + 1 + t]))
                rmark(R_sstat, SS)
                rmark(R_xs[t], SS)
                em.wait("vector", SS)
                em.rguard("vector", R_pb[bank])
                V(lambda e, t=t, half=half, bank=bank: e.tensor_tensor(
                    out=BIG1[:, t * C + half * 512: t * C + (half + 1) * 512],
                    in0=BIG1[:, t * C + half * 512: t * C + (half + 1) * 512],
                    in1=pb[bank][:, :], op=ALU.add))
                rmark(R_pb[bank], VS)
                wmark(R_xf[t], VS)
                em.wguard("vector", R_xs[t])
                V(lambda e, t=t, half=half: e.scalar_tensor_tensor(
                    out=xs_ap(t, half),
                    in0=BIG1[:, t * C + half * 512: t * C + (half + 1) * 512],
                    scalar=0.5, in1=xs_ap(t, half), op0=ALU.mult, op1=ALU.add))
                rmark(R_xf[t], VS)
                wmark(R_xs[t], VS)

            ccmm_tok(wo_slot, BIG3, [R_sgf], evac_f5)
            ck(pref + "f5")
            w2_slot = stage_w(w2_in, 0)

            claim("scalar", "B1", R_sg2f)
            f7_cols = []

            def emit_f7_col(m):
                bset = [(2, 3, 4, 5)[(m * 2 + i) % 4] for i in range(NTG)] if NTG == 4 \
                    else [(2, 3)[(m * NTG + tg_) % 2] for tg_ in range(NTG)]
                for tg_ in range(NTG):
                    em.wguard("tensor", R_pb[bset[tg_]])
                em.rguard("tensor", w2_slot[1], R_stf)
                for k in range(KCH):
                    for tg_ in range(NTG):
                        T(lambda e, k=k, m=m, tg_=tg_, bank=bset[tg_]: e.matmul(
                            pb[bank][:, :], w2_slot[0](k)[:, m * P:(m + 1) * P],
                            fm(BIG2, k, tg_ * 512, 512),
                            start=(k == 0), stop=(k == KCH - 1)),
                          inc=(k == KCH - 1 and tg_ == NTG - 1))
                rmark(w2_slot[1], TS)
                rmark(R_stf, TS)
                for tx in range((m * TCORE) // C, ((m + 1) * TCORE + C - 1) // C):
                    em.wguard("scalar", R_xf[tx])
                for tg_ in range(NTG):
                    bank = bset[tg_]
                    wmark(R_pb[bank], TS)
                    em.rguard("scalar", R_pb[bank])
                    S(lambda e, m=m, tg_=tg_, bank=bank: e.activation(
                        out=fm(BIG1, m, tg_ * 512, 512), in_=pb[bank][:, :],
                        func=AF.Sigmoid))
                    rmark(R_pb[bank], SS)

            mque = list(range(KCH))
            tiles_done = [0]

            def f67_need(m):
                return ((m + 1) * TCORE - 1) // C

            def f67_icb():
                if mque and f67_need(mque[0]) < tiles_done[0]:
                    emit_f7_col(mque.pop(0))

            def f67_tile_cb(t):
                rmark(R_xf[t], SS)
                rmark(R_xf[t], VS)
                tiles_done[0] = t + 1
                f67_icb()

            for tg in range(NTG):
                for t in range(tg * 4, tg * 4 + 4):
                    em.rguard("vector", R_xf[t])
                    em.rguard("scalar", R_xf[t])
                if tg == 0:
                    claim("scalar", "B3", R_hf[0])
                    for g2 in range(1, NTG):
                        add_owner("B3", R_hf[g2])

                def f67_src(tt):
                    return BIG1[:, tt * C:(tt + 1) * C]

                ln_group(tg, f67_src, BIG3, R_hf[tg],
                         lnw_off=16, double_ln=False, save_stats=False,
                         icb=f67_icb, tile_cb=f67_tile_cb)
            while mque:
                emit_f7_col(mque.pop(0))
            wmark(R_sg2f, SS)
            ck(pref + "f6")
            ck(pref + "f7")
            w1_slot = stage_w(w1_in, 1)
            ccmm_fm(w1_slot, BIG3, R_hf, BIG1, R_sg2f, "mult", other_big=BIG1)
            ck(pref + "f8")

            for t in range(NT):
                bank = 6 + (t % 2)
                par = t % 2
                em.wguard("tensor", R_pb[bank])
                em.rguard("tensor", R_smb, R_hf[t // 4])
                for k in range(KCH):
                    T(lambda e, k=k, t=t, bank=bank: e.matmul(
                        pb[bank][:, 0:7], fm(BIG3, k, t * P, P),
                        SMB[:, k * 8:k * 8 + 7],
                        start=(k == 0), stop=(k == KCH - 1)), inc=(k == KCH - 1))
                wmark(R_pb[bank], TS)
                rmark(R_smb, TS)
                rmark(R_hf[t // 4], TS)
                b0 = BID0 + par * 24
                em.rguard("vector", R_pb[bank])
                em.wguard("vector", R_bid[par])
                Vd(lambda e, bank=bank, b0=b0: e.tensor_copy(out=SM[:, b0 + 4:b0 + 11],
                                                             in_=pb[bank][:, 0:7]))
                rmark(R_pb[bank], VS)
                wmark(R_bid[par], VS)
                em.wait("scalar", VS)
                S(lambda e, b0=b0: e.activation(out=SM[:, b0:b0 + 4],
                                                in_=SM[:, b0 + 4:b0 + 8], func=AF.Sigmoid))
                em.wait("vector", SS)
                Vd(lambda e, b0=b0: e.tensor_single_scalar(SM[:, b0 + 11:b0 + 14], SH3[:, :],
                                                           SM[:, b0 + 3:b0 + 4], ALU.mult))
                Vd(lambda e, b0=b0: e.tensor_tensor(SM[:, b0 + 11:b0 + 14], SM[:, b0:b0 + 3],
                                                    SM[:, b0 + 11:b0 + 14], ALU.mult))
                Vd(lambda e, b0=b0: e.tensor_tensor(SM[:, b0 + 11:b0 + 14],
                                                    SM[:, b0 + 11:b0 + 14],
                                                    SM[:, b0 + 8:b0 + 11], ALU.add))
                Vd(lambda e, b0=b0: e.reduce_max(SM[:, b0 + 14:b0 + 15], SM[:, b0 + 11:b0 + 14], axis=AX))
                Vd(lambda e, b0=b0: e.tensor_single_scalar(SM[:, b0 + 15:b0 + 18],
                                                           SM[:, b0 + 11:b0 + 14],
                                                           SM[:, b0 + 14:b0 + 15], ALU.is_ge))
                Vd(lambda e, b0=b0: e.tensor_tensor(SM[:, b0 + 18:b0 + 19], SM[:, b0 + 16:b0 + 17],
                                                    SM[:, b0 + 15:b0 + 16], ALU.mult))
                Vd(lambda e, b0=b0: e.tensor_tensor(SM[:, b0 + 19:b0 + 20], SM[:, b0 + 16:b0 + 17],
                                                    SM[:, b0 + 18:b0 + 19], ALU.subtract))
                Vd(lambda e, b0=b0: e.tensor_tensor(SM[:, b0 + 20:b0 + 21], SM[:, b0 + 15:b0 + 16],
                                                    SM[:, b0 + 19:b0 + 20], ALU.add))
                Vd(lambda e, b0=b0: e.scalar_tensor_tensor(
                    out=SM[:, b0 + 20:b0 + 21], in0=SM[:, b0 + 20:b0 + 21], scalar=-1.0,
                    in1=SM[:, ONEC:ONEC + 1], op0=ALU.mult, op1=ALU.add))
                Vd(lambda e, b0=b0: e.tensor_copy(out=SM[:, b0 + 16:b0 + 17], in_=SM[:, b0 + 19:b0 + 20]))
                Vd(lambda e, b0=b0: e.tensor_copy(out=SM[:, b0 + 17:b0 + 18], in_=SM[:, b0 + 20:b0 + 21]))
                Vd(lambda e, b0=b0: e.tensor_tensor(SM[:, b0 + 18:b0 + 21], SM[:, b0 + 15:b0 + 18],
                                                    SM[:, b0:b0 + 3], ALU.mult))
                Vd(lambda e, b0=b0: e.reduce_sum(SM[:, b0 + 21:b0 + 22], SM[:, b0 + 18:b0 + 21], axis=AX))
                Vd(lambda e, b0=b0: e.tensor_scalar_add(SM[:, b0 + 22:b0 + 23], SM[:, b0 + 21:b0 + 22], 1e-6))
                Vd(lambda e, b0=b0: e.reciprocal(SM[:, b0 + 23:b0 + 24], SM[:, b0 + 22:b0 + 23]))
                Vd(lambda e, b0=b0: e.tensor_tensor(SM[:, b0 + 22:b0 + 23], SM[:, b0 + 21:b0 + 22],
                                                    SM[:, b0 + 23:b0 + 24], ALU.mult))
                Vd(lambda e, b0=b0: e.tensor_scalar_mul(SM[:, b0 + 22:b0 + 23], SM[:, b0 + 22:b0 + 23], 0.5))
                Vd(lambda e, t=t, b0=b0: e.tensor_single_scalar(
                    SM[:, WTK0 + 3 * t:WTK0 + 3 * t + 3], SM[:, b0 + 15:b0 + 18],
                    SM[:, b0 + 22:b0 + 23], ALU.mult))
                rmark(R_bid[par], VS)
            wmark(R_wtok, VS)
            ck(pref + "f9")

            w3_slot = stage_w(w3_in, 0)

            def evac_f10(t, half, bank):
                em.rguard("vector", R_pb[bank], R_wtok)
                em.wguard("vector", R_xs[t])
                V(lambda e, t=t, half=half, bank=bank: e.scalar_tensor_tensor(
                    out=xs_ap(t, half), in0=pb[bank][:, :],
                    scalar=SM[:, WTK0 + 3 * t + 2:WTK0 + 3 * t + 3],
                    in1=xs_ap(t, half), op0=ALU.mult, op1=ALU.add))
                rmark(R_pb[bank], VS)
                rmark(R_wtok, VS)
                wmark(R_xs[t], VS)

            ccmm_tok(w3_slot, BIG1, [R_sg2f], evac_f10)
            ck(pref + "f10")

            NQ = 4
            for e_ in range(2):
                for q4 in range(NQ):
                    qi = e_ * NQ + q4
                    ks = R_ks[qi % 2]
                    koff = (qi % 2) * 8 * C
                    if qi == 0 and not second:
                        claim("gpsimd", "B2", R_ks[0])
                        add_owner("B2", R_ks[1])
                    elif qi == 0:
                        claim("gpsimd", "B2", R_ks[0])
                        add_owner("B2", R_ks[1])
                    else:
                        em.wguard("gpsimd", ks)
                    GD(lambda e, e_=e_, q4=q4, koff=koff: e.dma_start(
                        out=BIG2[:, koff:koff + 8 * HQ].rearrange("p (k n) -> p k n", n=HQ),
                        in_=krw_in[e_][:, q4 * HQ:(q4 + 1) * HQ]
                        .rearrange("(k p) n -> p k n", p=P)))
                    wmark(ks, DG)
                    vs_ = R_vst[qi % 2]
                    voff = (qi % 2) * 8 * C
                    if qi == 0:
                        claim("gpsimd", "B1", R_vst[0])
                        add_owner("B1", R_vst[1])
                    else:
                        em.wguard("gpsimd", vs_)
                    GD(lambda e, e_=e_, q4=q4, voff=voff: e.dma_start(
                        out=BIG1[:, voff:voff + 8 * C].rearrange("p (m c) -> p m c", c=C),
                        in_=vrw_in[e_][q4 * HQ:(q4 + 1) * HQ, :].rearrange(
                            "(m p) c -> p m c", p=P)))
                    wmark(vs_, DG)
                    if qi == 0:
                        claim("vector", "B4a", R_hrh)
                        claim("vector", "B4b", R_hrh)
                    else:
                        em.wguard("vector", R_hrh)
                    for hml in range(NHMQ):
                        if NTG == 4:
                            bset = [(hml % 2) * 4 + tg for tg in range(4)]
                        else:
                            bset = [(0, 1, 2, 3)[(hml * NTG + tg) % 4] for tg in range(NTG)]
                        for tg in range(NTG):
                            em.wguard("tensor", R_pb[bset[tg]])
                        em.rguard("tensor", ks)
                        for tg in range(NTG):
                            em.rguard("tensor", R_hf[tg])
                        for k in range(KCH):
                            for tg in range(NTG):
                                T(lambda e, k=k, hml=hml, tg=tg, bank=bset[tg], koff=koff: e.matmul(
                                    pb[bank][:, :],
                                    BIG2[:, koff + k * HQ + hml * P: koff + k * HQ + (hml + 1) * P],
                                    fm(BIG3, k, tg * 512, 512),
                                    start=(k == 0), stop=(k == KCH - 1)),
                                  inc=(k == KCH - 1 and tg == NTG - 1))
                        rmark(ks, TS)
                        for tg in range(NTG):
                            rmark(R_hf[tg], TS)
                        for tg in range(NTG):
                            bank = bset[tg]
                            wmark(R_pb[bank], TS)
                            sl = (hml * NTG + tg) % 2
                            em.wguard("scalar", R_sb512[sl])
                            em.rguard("scalar", R_pb[bank])
                            S(lambda e, bank=bank, sl=sl: e.activation(
                                out=SB512[:, sl * 512:(sl + 1) * 512], in_=pb[bank][:, :],
                                func=AF.Relu))
                            rmark(R_pb[bank], SS)
                            wmark(R_sb512[sl], SS)
                            em.rguard("vector", R_sb512[sl])
                            V(lambda e, hml=hml, tg=tg, sl=sl: e.tensor_tensor(
                                out=BIG4[:, hml * TCORE + tg * 512: hml * TCORE + (tg + 1) * 512],
                                in0=SB512[:, sl * 512:(sl + 1) * 512],
                                in1=SB512[:, sl * 512:(sl + 1) * 512], op=ALU.mult))
                            rmark(R_sb512[sl], VS)
                    wmark(R_hrh, VS)
                    for t in range(NT):
                        bpair = ((4, 5), (6, 7))[t % 2]
                        em.wguard("tensor", R_pb[bpair[0]])
                        em.wguard("tensor", R_pb[bpair[1]])
                        em.rguard("tensor", R_hrh, vs_)
                        for hm in range(NHMQ):
                            for ch in range(2):
                                T(lambda e, hm=hm, t=t, ch=ch, bank=bpair[ch], voff=voff: e.matmul(
                                    pb[bank][:, :],
                                    BIG4[:, hm * TCORE + t * P: hm * TCORE + (t + 1) * P],
                                    BIG1[:, voff + hm * C + ch * 512: voff + hm * C + (ch + 1) * 512],
                                    start=(hm == 0), stop=(hm == NHMQ - 1)),
                                  inc=(hm == NHMQ - 1 and ch == 1))
                        rmark(R_hrh, TS)
                        rmark(vs_, TS)
                        for ch in range(2):
                            bank = bpair[ch]
                            wmark(R_pb[bank], TS)
                            em.rguard("vector", R_pb[bank], R_wtok)
                            em.wguard("vector", R_xs[t])
                            V(lambda e, t=t, ch=ch, bank=bank, e_=e_: e.scalar_tensor_tensor(
                                out=xs_ap(t, ch), in0=pb[bank][:, :],
                                scalar=SM[:, WTK0 + 3 * t + e_:WTK0 + 3 * t + e_ + 1],
                                in1=xs_ap(t, ch), op0=ALU.mult, op1=ALU.add))
                            rmark(R_pb[bank], VS)
                            rmark(R_wtok, VS)
                            wmark(R_xs[t], VS)
                            if second and e_ == 1 and q4 == NQ - 1:
                                em.wait("sync", VS)
                                SD(lambda e, t=t, ch=ch: e.dma_start(
                                    out=y_out[t * P:(t + 1) * P, ch * 512:(ch + 1) * 512],
                                    in_=xs_ap(t, ch)))
                                rmark(R_xs[t], DSY)
            ck(pref + "f11")

        def emit_attention():
            R_y1f = [Res("y1f%d" % g) for g in range(NTG)]
            R_qf = Res("qf")
            R_ko = Res("ko")
            R_vo = Res("vo")
            R_ka = Res("ka")
            R_va = Res("va")
            R_attw = Res("attw")
            R_kmk = [Res("kmk%d" % i) for i in range(2)]
            R_osb = [Res("osb%d" % i) for i in range(2)]
            R_ptb = [Res("ptb%d" % i) for i in range(3)]
            R_lsb = [Res("lsb"), ]
            R_stg = Res("stg")
            R_pbf = R_sb512

            AW0 = 3 * TCORE
            KMS0 = AW0 + 4096
            claim("gpsimd", "B3", R_attw)
            GD(lambda e: e.dma_start(
                out=BIG3[:, AW0:AW0 + 1024].rearrange("p (k n) -> p k n", n=DK),
                in_=wq_in.rearrange("(k p) n -> p k n", p=P)))
            GD(lambda e: e.dma_start(
                out=BIG3[:, AW0 + 1024:AW0 + 2048].rearrange("p (k n) -> p k n", n=DK),
                in_=wk_in.rearrange("(k p) n -> p k n", p=P)))
            GD(lambda e: e.dma_start(
                out=BIG3[:, AW0 + 2048:AW0 + 3072].rearrange("p (k n) -> p k n", n=DK),
                in_=wva_in.rearrange("(k p) n -> p k n", p=P)))
            GD(lambda e: e.dma_start(out=BIG3[:, AW0 + 3072:AW0 + 4096], in_=woa_in))
            wmark(R_attw, DG)

            def wq_k(k):
                return BIG3[:, AW0 + k * DK:AW0 + (k + 1) * DK]

            def wk_k(k):
                return BIG3[:, AW0 + 1024 + k * DK:AW0 + 1024 + (k + 1) * DK]

            def wv_k(k):
                return BIG3[:, AW0 + 2048 + k * DK:AW0 + 2048 + (k + 1) * DK]

            WOap = BIG3[:, AW0 + 3072:AW0 + 4096]
            QF0, KO0, VO0 = 0, TCORE, 2 * TCORE

            def emit_qk_col(which, tg2):
                dst0, wfn, scale, res = ((QF0, wq_k, INV_SQRT_DK, R_qf) if which == 0
                                         else (KO0, wk_k, 1.0, R_ko))
                bank = (2, 3)[(which + tg2) % 2]
                em.wguard("tensor", R_pb[bank])
                em.rguard("tensor", R_attw, R_y1f[tg2])
                for k in range(KCH):
                    T(lambda e, k=k, tg2=tg2, bank=bank, wfn=wfn: e.matmul(
                        pb[bank][:, :], wfn(k), fm(BIG1, k, tg2 * 512, 512),
                        start=(k == 0), stop=(k == KCH - 1)), inc=(k == KCH - 1))
                wmark(R_pb[bank], TS)
                rmark(R_attw, TS)
                rmark(R_y1f[tg2], TS)
                add_owner("B3", res)
                em.rguard("scalar", R_pb[bank])
                S(lambda e, dst0=dst0, tg2=tg2, bank=bank, scale=scale: e.activation(
                    out=BIG3[:, dst0 + tg2 * 512: dst0 + (tg2 + 1) * 512],
                    in_=pb[bank][:, :], func=AF.Copy, scale=scale))
                rmark(R_pb[bank], SS)
                if tg2 == NTG - 1:
                    wmark(res, SS)
                    norm_scan(BIG3, dst0, NRM0 if which == 0 else NRM0 + 1,
                              TCORE // 512, res)
                    if which == 1:
                        em.wait("scalar", VS)
                        em.wguard("scalar", R_stg)
                        Sd(lambda e: e.copy(out=STG[0:1, 0:1], in_=SM[0:1, NRM0 + 1:NRM0 + 2]))
                        wmark(R_stg, SS)
                        em.wait("gpsimd", SS)
                        GD(lambda e: e.dma_start(out=send_d.ap()[0:1, 2 * TCORE:2 * TCORE + 1],
                                                 in_=STG[0:1, 0:1]))
                        rmark(R_stg, DG)

            def emit_vo_tile(t2):
                bank = (4, 5)[t2 % 2]
                em.wguard("tensor", R_pb[bank])
                em.rguard("tensor", R_attw, R_y1f[t2 // 4])
                for k in range(KCH):
                    T(lambda e, k=k, t2=t2, bank=bank: e.matmul(
                        pb[bank][:, 0:P], fm(BIG1, k, t2 * P, P), wv_k(k),
                        start=(k == 0), stop=(k == KCH - 1)), inc=(k == KCH - 1))
                wmark(R_pb[bank], TS)
                rmark(R_attw, TS)
                rmark(R_y1f[t2 // 4], TS)
                add_owner("B3", R_vo)
                em.rguard("scalar", R_pb[bank])
                S(lambda e, t2=t2, bank=bank: e.copy(
                    out=BIG3[:, VO0 + t2 * P: VO0 + (t2 + 1) * P], in_=pb[bank][:, 0:P]))
                rmark(R_pb[bank], SS)

            nq, nk = NRM0, NRM0 + 1
            em.wait("vector", DSY, smalls_dma)
            V(lambda e: e.memset(SM[0:1, nq:nq + 2], 0.0))
            em.wait("tensor", DSY, smalls_dma)

            def norm_scan(big, src_off, nmax_col, nchunks, res_r):
                for ch_ in range(nchunks):
                    sl = ch_ % 2
                    em.rguard("vector", res_r)
                    em.wguard("vector", R_pbf[sl])
                    V(lambda e, big=big, ch_=ch_, src_off=src_off, sl=sl: e.tensor_tensor(
                        out=SB512[:, sl * 512:(sl + 1) * 512],
                        in0=big[:, src_off + ch_ * 512: src_off + (ch_ + 1) * 512],
                        in1=big[:, src_off + ch_ * 512: src_off + (ch_ + 1) * 512], op=ALU.mult))
                    wmark(R_pbf[sl], VS)
                    em.wguard("tensor", R_pb[7])
                    em.rguard("tensor", R_pbf[sl])
                    T(lambda e, sl=sl: e.matmul(pb[7][0:1, :], ONESP[:, :],
                                                SB512[:, sl * 512:(sl + 1) * 512],
                                                start=True, stop=True,
                                                skip_group_check=True), inc=True)
                    wmark(R_pb[7], TS)
                    rmark(R_pbf[sl], TS)
                    em.rguard("vector", R_pb[7])
                    Vd(lambda e: e.reduce_max(SM[0:1, NRM0 + 2:NRM0 + 3], pb[7][0:1, :], axis=AX))
                    Vd(lambda e, nmax_col=nmax_col: e.tensor_tensor(
                        SM[0:1, nmax_col:nmax_col + 1], SM[0:1, nmax_col:nmax_col + 1],
                        SM[0:1, NRM0 + 2:NRM0 + 3], ALU.max))
                    rmark(R_pb[7], VS)

            a2q = []

            def a12_icb():
                if a2q:
                    a2q.pop(0)()

            for tg in range(NTG):
                for t in range(tg * 4, tg * 4 + 4):
                    em.rguard("vector", R_xs[t])
                    em.rguard("scalar", R_xs[t])
                if tg == 0:
                    claim("scalar", "B1", R_y1f[0])
                    for g2 in range(1, NTG):
                        add_owner("B1", R_y1f[g2])
                ln_group(tg, lambda tt: xs_ap(tt), BIG1, R_y1f[tg],
                         lnw_off=0, double_ln=False, save_stats=False, plain=True,
                         icb=a12_icb)
                for t in range(tg * 4, tg * 4 + 4):
                    rmark(R_xs[t], SS)
                    rmark(R_xs[t], VS)
                a2q += [lambda tg=tg, w=w: emit_qk_col(w, tg) for w in (0, 1)]
                a2q += [lambda t2=t2: emit_vo_tile(t2) for t2 in range(tg * 4, tg * 4 + 4)]
            while a2q:
                a2q.pop(0)()
            em.rguard("gpsimd", R_ko, R_vo)
            GD(lambda e: e.dma_start(out=send_d.ap()[:, 0:TCORE], in_=BIG3[:, KO0:KO0 + TCORE]))
            GD(lambda e: e.dma_start(out=send_d.ap()[:, TCORE:2 * TCORE],
                                     in_=BIG3[:, VO0:VO0 + TCORE]))
            rmark(R_ko, DG)
            rmark(R_vo, DG)
            wmark(R_send, DG)
            em.wait("gpsimd", DG)
            em.do("gpsimd", lambda e: e.collective_compute(
                "AllGather", ALU.bypass, replica_groups=[list(range(NCORES))],
                ins=[send_d.ap()], outs=[recv_d.ap()]), CCS, 1)
            wmark(R_recv, CCS)
            em.wait("gpsimd", CCS)
            claim("gpsimd", "B2", R_ka)
            rv = recv_d.ap().rearrange("(r p) m -> p r m", p=P)
            GD(lambda e: e.dma_start(
                out=BIG2[:, 0:GK].rearrange("p (r n) -> p r n", n=TCORE),
                in_=rv[:, :, 0:TCORE]))
            wmark(R_ka, DG)
            claim("gpsimd", "B4a", R_va)
            claim("gpsimd", "B4b", R_va)
            GD(lambda e: e.dma_start(
                out=BIG4[:, 0:GK].rearrange("p (r n) -> p r n", n=TCORE),
                in_=rv[:, :, TCORE:2 * TCORE]))
            wmark(R_va, DG)
            em.rguard("vector", R_ka)
            for i in range(GK // 2048):
                ksl = KMS0 + (i % 2) * 2048
                em.wguard("gpsimd", R_kmk[i % 2])
                GD(lambda e, i=i, ksl=ksl: e.dma_start(
                    out=BIG3[:, ksl:ksl + 2048], in_=kmask_in[:, i * 2048:(i + 1) * 2048]))
                wmark(R_kmk[i % 2], DG)
                em.rguard("vector", R_kmk[i % 2])
                V(lambda e, i=i, ksl=ksl: e.tensor_tensor(
                    out=BIG2[:, i * 2048:(i + 1) * 2048],
                    in0=BIG2[:, i * 2048:(i + 1) * 2048],
                    in1=BIG3[:, ksl:ksl + 2048], op=ALU.mult))
                rmark(R_kmk[i % 2], VS)
            wmark(R_ka, VS)
            em.rguard("vector", R_va)
            em.wait("vector", DSY, smalls_dma)
            for j in range(GK // P):
                V(lambda e, j=j: e.tensor_single_scalar(
                    BIG4[:, j * P:(j + 1) * P],
                    BIG4[:, j * P:(j + 1) * P],
                    VMK[:, j:j + 1], ALU.mult))
            wmark(R_va, VS)
            ck("a5")

            em.wguard("gpsimd", R_lsb[0])
            GD(lambda e: e.dma_start(
                out=LSF[0:1, 0:8],
                in_=recv_d.ap().rearrange("(r p) m -> p r m", p=P)[0:1, :, 2 * TCORE],
                allow_slow_non_contiguous=True))
            wmark(R_lsb[0], DG)
            em.rguard("vector", R_lsb[0])
            Vd(lambda e: e.reduce_max(SM[0:1, NRM0 + 3:NRM0 + 4], LSF[0:1, 0:8], axis=AX))
            Vd(lambda e: e.tensor_tensor(SM[0:1, NRM0 + 1:NRM0 + 2], SM[0:1, NRM0 + 1:NRM0 + 2],
                                         SM[0:1, NRM0 + 3:NRM0 + 4], ALU.max))
            rmark(R_lsb[0], VS)
            Vd(lambda e: e.tensor_tensor(SM[0:1, NRM0 + 2:NRM0 + 3], SM[0:1, nq:nq + 1],
                                        SM[0:1, nk:nk + 1], ALU.mult))
            em.wait("scalar", VS)
            Sd(lambda e: e.sqrt(out=SM[0:1, NRM0 + 3:NRM0 + 4], in_=SM[0:1, NRM0 + 2:NRM0 + 3]))
            em.wguard("scalar", R_stg)
            Sd(lambda e: e.copy(out=STG[0:1, 2:3], in_=SM[0:1, NRM0 + 3:NRM0 + 4]))
            wmark(R_stg, SS)
            em.wait("tensor", SS)
            em.wguard("tensor", R_pb[7])
            T(lambda e: e.matmul(pb[7][:, 0:1], ONESR[0:1, :], STG[0:1, 2:3],
                                 start=True, stop=True, skip_group_check=True), inc=True)
            rmark(R_stg, TS)
            wmark(R_pb[7], TS)
            em.rguard("vector", R_pb[7])
            Vd(lambda e: e.tensor_scalar_mul(SM[:, NRM0 + 5:NRM0 + 6], pb[7][:, 0:1], -1.0))
            rmark(R_pb[7], VS)
            negm_c = NRM0 + 5
            em.wait("scalar", VS)
            Sd(lambda e: e.activation(out=SM[:, NRM0 + 6:NRM0 + 7], in_=SM[:, negm_c:negm_c + 1],
                                     func=AF.Exp))
            em.wguard("scalar", R_stg)
            Sd(lambda e: e.copy(out=STG[0:1, 1:2], in_=SM[0:1, NRM0 + 6:NRM0 + 7]))
            Sd(lambda e: e.copy(out=SM[0:1, NRM0 + 8:NRM0 + 9], in_=STG[0:1, 1:2]))
            wmark(R_stg, SS)
            em.wait("tensor", SS)
            em.wguard("tensor", R_pb[1])
            T(lambda e: e.matmul(pb[1][:, 0:1], ONESR[0:1, :], STG[0:1, 1:2],
                                 start=True, stop=True, skip_group_check=True), inc=True)
            rmark(R_stg, TS)
            wmark(R_pb[1], TS)
            em.rguard("vector", R_pb[1])
            Vd(lambda e: e.tensor_copy(out=SM[:, NRM0 + 6:NRM0 + 7], in_=pb[1][:, 0:1]))
            rmark(R_pb[1], VS)
            em.wait("vector", SS)
            em.wait("vector", DSY, smalls_dma)
            Vd(lambda e: e.tensor_tensor(SM[:, NRM0 + 7:NRM0 + 8], SM[:, NRM0 + 6:NRM0 + 7],
                                        NZERO[:, :], ALU.mult))
            wmark(R_norm, VS)
            ck("a3")

            NSC = GK // P
            for g in range(NTG):
                par = g % 2
                obank = 6 + par
                lbank = 0 + par
                qlo = g * 512
                em.wguard("tensor", R_pb[obank])
                em.wguard("tensor", R_pb[lbank])
                state = {"first": True, "idx": 0, "pend": [], "drain": False}

                def ptb_ap(sl, n=512):
                    if sl < 2:
                        return TMP2[:, sl * 512:sl * 512 + n]
                    return SB512[:, 0:n]

                def st_sub(src_big, koff, vtile, masked_ki, last):
                    idx = state["idx"]
                    state["idx"] += 1
                    sbank = (2, 3)[idx % 2]
                    sl = idx % 2
                    em.wguard("tensor", R_pb[sbank])
                    em.rguard("tensor", R_qf, R_ka if src_big is BIG2 else R_ko)
                    T(lambda e, src_big=src_big, koff=koff, sbank=sbank, qlo=qlo: e.matmul(
                        pb[sbank][:, :], src_big[:, koff:koff + P],
                        BIG3[:, QF0 + qlo: QF0 + qlo + 512], start=True, stop=True,
                        skip_group_check=True), inc=True)
                    wmark(R_pb[sbank], TS)
                    rmark(R_qf, TS)
                    rmark(R_ka if src_big is BIG2 else R_ko, TS)
                    em.rguard("scalar", R_pb[sbank], R_norm)
                    em.wguard("scalar", R_ptb[sl])
                    S(lambda e, sbank=sbank, sl=sl: e.activation(
                        out=ptb_ap(sl), in_=pb[sbank][:, :],
                        func=AF.Exp, bias=SM[:, negm_c:negm_c + 1]))
                    rmark(R_pb[sbank], SS)
                    rmark(R_norm, SS)
                    wmark(R_ptb[sl], SS)
                    if masked_ki is not None:
                        em.rguard("vector", R_ptb[sl])
                        em.wait("vector", DSY, smalls_dma)
                        V(lambda e, sl=sl, ki=masked_ki: e.tensor_tensor(
                            out=ptb_ap(sl), in0=ptb_ap(sl),
                            in1=GMK[:, ki * 512:(ki + 1) * 512], op=ALU.mult))
                        wmark(R_ptb[sl], VS)
                    state["pend"].append((sl, vtile, last))
                    if last:
                        state["drain"] = True
                    if len(state["pend"]) >= 2 or state["drain"]:
                        flush()

                def flush():
                    while state["pend"]:
                        if len(state["pend"]) < 2 and not state["drain"]:
                            return
                        sl, vtile, last = state["pend"].pop(0)
                        first = state["first"]
                        state["first"] = False
                        em.rguard("tensor", R_ptb[sl])
                        em.wait("tensor", DSY, smalls_dma)
                        T(lambda e, sl=sl, lbank=lbank, first=first, last=last: e.matmul(
                            pb[lbank][0:1, :], ONESP[:, :], ptb_ap(sl),
                            start=first, stop=last, skip_group_check=True), inc=False)
                        em.rguard("tensor", R_va if vtile[0] is BIG4 else R_vo)
                        T(lambda e, sl=sl, vtile=vtile, obank=obank, first=first, last=last: e.matmul(
                            pb[obank][:, :], vtile[0][:, vtile[1]:vtile[1] + P],
                            ptb_ap(sl),
                            start=first, stop=last, skip_group_check=True), inc=last)
                        rmark(R_ptb[sl], TS)
                        rmark(R_va if vtile[0] is BIG4 else R_vo, TS)

                for sc in range(4 * g):
                    st_sub(BIG3, KO0 + sc * P, (BIG3, VO0 + sc * P), None, False)
                for ki in range(4):
                    st_sub(BIG3, KO0 + qlo + ki * P, (BIG3, VO0 + qlo + ki * P), ki, False)
                for sc in range(NSC):
                    st_sub(BIG2, sc * P, (BIG4, sc * P), None, sc == NSC - 1)
                wmark(R_pb[obank], TS)
                wmark(R_pb[lbank], TS)
                em.rguard("scalar", R_pb[lbank])
                em.wguard("scalar", R_lsb[0])
                S(lambda e, lbank=lbank: e.copy(
                    out=LSF[0:1, :], in_=pb[lbank][0:1, :]))
                wmark(R_lsb[0], SS)
                rmark(R_pb[lbank], SS)
                for tq in range(4):
                    t = g * 4 + tq
                    tpar = t % 2
                    lcol = LT0 + tq
                    em.wait("tensor", SS)
                    em.wguard("tensor", R_pb[4 + tq % 2])
                    em.rguard("tensor", R_lsb[0])
                    T(lambda e, tq=tq, bank=4 + tq % 2: e.matmul(
                        pb[bank][:, 0:1],
                        LSF[0:1, tq * P:(tq + 1) * P],
                        SM[0:1, ONEC:ONEC + 1],
                        start=True, stop=True, skip_group_check=True), inc=True)
                    wmark(R_pb[4 + tq % 2], TS)
                    rmark(R_lsb[0], TS)
                    em.rguard("vector", R_pb[4 + tq % 2], R_norm)
                    em.wguard("vector", R_lt[tpar])
                    Vd(lambda e, bank=4 + tq % 2, lcol=lcol: e.tensor_tensor(
                        SM[:, lcol:lcol + 1], pb[bank][:, 0:1],
                        SM[:, NRM0 + 7:NRM0 + 8], ALU.subtract))
                    rmark(R_pb[4 + tq % 2], VS)
                    rmark(R_norm, VS)
                    Vd(lambda e, lcol=lcol, tq=tq: e.reciprocal(
                        SM[:, LT0 + 4 + tq:LT0 + 5 + tq], SM[:, lcol:lcol + 1]))
                    wmark(R_lt[tpar], VS)
                em.rguard("scalar", R_pb[obank])
                em.wguard("scalar", R_osb[par])
                S(lambda e, obank=obank, par=par: e.copy(
                    out=TMP2[:, 1024 + par * 512:1024 + (par + 1) * 512],
                    in_=pb[obank][:, :]))
                wmark(R_osb[par], SS)
                rmark(R_pb[obank], SS)
                for tq in range(4):
                    t = g * 4 + tq
                    tpar = t % 2
                    linv_col = LT0 + 4 + tq
                    for half in range(2):
                        wbank = (2, 3)[half]
                        em.wguard("tensor", R_pb[wbank])
                        em.rguard("tensor", R_osb[par], R_attw)
                        T(lambda e, half=half, par=par, tq=tq, wbank=wbank: e.matmul(
                            pb[wbank][:, :],
                            TMP2[:, 1024 + par * 512 + tq * P:1024 + par * 512 + (tq + 1) * P],
                            WOap[:, half * 512:(half + 1) * 512], start=True, stop=True,
                            skip_group_check=True), inc=True)
                        wmark(R_pb[wbank], TS)
                        rmark(R_osb[par], TS)
                        rmark(R_attw, TS)
                        em.rguard("vector", R_pb[wbank], R_lt[tpar])
                        em.wguard("vector", R_xs[t])
                        V(lambda e, t=t, half=half, wbank=wbank, linv_col=linv_col: e.scalar_tensor_tensor(
                            out=xs_ap(t, half), in0=pb[wbank][:, :],
                            scalar=SM[:, linv_col:linv_col + 1],
                            in1=xs_ap(t, half), op0=ALU.mult, op1=ALU.add))
                        rmark(R_pb[wbank], VS)
                        rmark(R_lt[tpar], VS)
                        wmark(R_xs[t], VS)
            ck("a6")
            for i in range(2):
                for (s_, v_) in list(R_ptb[i].rd.values()):
                    rmark(R_tmp2[i], s_)
                if R_ptb[i].w:
                    R_tmp2[i].w = R_ptb[i].w
                for (s_, v_) in list(R_osb[i].rd.values()):
                    rmark(R_tmp2[i], s_)

        try:
            emit_ffn(second=False)
            emit_attention()
            emit_ffn(second=True)
            ck("end")
        except _Stop:
            pass

        if debug_dump or stop_after is not None:
            em.wait("sync", VS)
            em.wait("sync", SS)
            em.wait("sync", TS)
            em.wait("sync", GS)
            em.wait("sync", DG)
            SD(lambda e: e.dma_start(
                out=dxs_out.rearrange("(t p) c -> p t c", p=P),
                in_=XS[:, :].rearrange("p (t c) -> p t c", c=C)))
            SD(lambda e: e.dma_start(out=dsm_out, in_=SM[:, :]))
            em.wait("gpsimd", DSY)
            for i, big in enumerate((BIG1, BIG2, BIG3, BIG4)):
                GD(lambda e, i=i, big=big: e.dma_start(out=db_out[i], in_=big[:, :]))

        em.wait("sync", DSY)
        em.wait("gpsimd", DG)

        for s in em.sems:
            s.h = st.enter_context(nc.semaphore(s.name))
        block = st.enter_context(nc.Block())

        def replay(engobj, eng):
            for op in em.ops[eng]:
                if op[0] == "wait":
                    engobj.wait_ge(op[1].h, op[2])
                else:
                    fn, sem, inc = op[1], op[2], op[3]
                    ins = fn(engobj)
                    if sem is not None:
                        ins.then_inc(sem.h, inc)

        @block.sync
        def _(e):
            replay(e, "sync")

        @block.gpsimd
        def _(e):
            replay(e, "gpsimd")

        @block.scalar
        def _(e):
            replay(e, "scalar")

        @block.vector
        def _(e):
            replay(e, "vector")

        @block.tensor
        def _(e):
            replay(e, "tensor")

    em.simulate()
    return nc, em



def host_inputs(full, core, NT=16):
    TCORE = NT * P
    GK = 8 * TCORE
    b, g = core // 4, core % 4
    bf = ml_dtypes.bfloat16
    f32 = lambda k: np.ascontiguousarray(np.asarray(full[k], dtype=np.float32))
    d = {}
    x = f32("L0_attn_x")
    d["x"] = np.ascontiguousarray(x[b, g * TCORE:(g + 1) * TCORE, :])
    d["wq"] = f32("L0_attn_Wq")
    d["wk"] = f32("L0_attn_Wk")
    d["wva"] = f32("L0_attn_Wv")
    d["woa"] = f32("L0_attn_Wo")
    d["wr"] = f32("L0_ffn_Wr")
    d["wvf"] = f32("L0_ffn_Wv")
    d["wof"] = f32("L0_ffn_Wo")
    d["ws"] = f32("L0_ffn_Ws")
    d["w1"] = f32("L0_ffn_W1")
    d["w2"] = f32("L0_ffn_W2")
    d["w3"] = f32("L0_ffn_W3")
    d["krw"] = f32("L0_ffn_K_rwkv")
    d["vrw"] = f32("L0_ffn_V_rwkv")
    wsm = np.concatenate([
        f32("L0_ffn_conf_rwkv").T,
        f32("L0_ffn_conf_trans")[:, None],
        f32("L0_ffn_w_diff")[:, None],
        f32("L0_ffn_W_aff")], axis=1)
    d["wsmT"] = np.ascontiguousarray(wsm.T)
    lnc = np.zeros((P, 32), np.float32)
    for i, nm in enumerate(["ln1_w", "ln1_b", "ln2_w", "ln2_b"]):
        lnc[:, i * 8:(i + 1) * 8] = f32("L0_ffn_" + nm).reshape(8, P).T
    d["lnc"] = lnc
    d["sh3"] = np.ascontiguousarray(
        np.broadcast_to(f32("L0_ffn_capital_shares"), (P, 3)))
    d["idn"] = np.eye(P, dtype=np.float32).astype(bf)
    d["tri"] = np.tril(np.ones((P, P), np.float32)).astype(bf)
    gm = np.zeros((P, 4 * 512), np.float32)
    for ki in range(4):
        for k in range(P):
            gm[k, ki * 512 + ki * P + k:(ki + 1) * 512] = 1.0
    d["gmask"] = np.ascontiguousarray(gm).astype(bf)
    d["onesp"] = np.ones((P, 1), np.float32).astype(bf)
    d["onesr"] = np.ones((1, P), np.float32).astype(bf)
    km = np.zeros((GK,), np.float32)
    km[4 * b * TCORE: (4 * b + g) * TCORE] = 1.0
    d["kmask"] = np.ascontiguousarray(np.broadcast_to(km, (P, GK))).astype(bf)
    d["vmask"] = np.ascontiguousarray(km.reshape(GK // P, P).T)
    d["nzero"] = np.full((P, 1), float(GK - g * TCORE), np.float32)
    return d


def kernel(**inputs):
    NT = 16
    nc, _em = build_program(NT=NT)
    in_maps = [host_inputs(inputs, r, NT) for r in range(NCORES)]
    res = run_bass_kernel_spmd(nc, in_maps, core_ids=list(range(NCORES)))
    TCORE = NT * P
    out = np.zeros((2, 4 * TCORE, C), np.float32)
    for r in range(NCORES):
        b, g = r // 4, r % 4
        out[b, g * TCORE:(g + 1) * TCORE, :] = res.results[r]["y"]
    return out
